# version 02
# baseline (speedup 1.0000x reference)
import numpy as np
import ml_dtypes
import concourse.bass as bass
import concourse.mybir as mybir
from concourse.bass_utils import run_bass_kernel_spmd

F32 = mybir.dt.float32
BF16 = mybir.dt.bfloat16
AF = mybir.ActivationFunctionType
ALU = mybir.AluOpType
X = mybir.AxisListType.X
BF = ml_dtypes.bfloat16

NCORES = 8
T = 4096
D = 1024
CH = 64
NCHUNK = T // CH
H0 = 32
P0 = 64
DI = 2048
DS = 128
CONVD = DI + DS
H1 = 16
DD = 64
MAGIC = 12582912.0
EPS = 1e-5


_NAME_SEQ = [0]


class Sem:

    _reg = []

    def __init__(self, nc, name):
        self.h = nc.alloc_semaphore(name)
        self.v = 0
        Sem._reg.append(self)

    def inc(self, instr, k=1):
        self.v += k
        instr.then_inc(self.h, k)
        return self.v


def _b(ap, shape):
    return ap.broadcast_to(shape)


def _hq(ap, q):
    return ap.rearrange("p (h q) -> p h q", q=q)



def build_l0a():
    nc = bass.Bass("TRN2", target_bir_lowering=False, detect_race_conditions=False)
    tn = {}
    def inp(name, shape, dt=F32):
        tn[name] = nc.dram_tensor(name, shape, dt, kind="ExternalInput").ap()
    def outp(name, shape, dt=F32):
        tn[name] = nc.dram_tensor(name, shape, dt, kind="ExternalOutput").ap()
    inp("u", [T, D]); inp("uh", [128, D])
    inp("wtxb", [D, CONVD]); inp("wtdt", [D, H0])
    inp("cw", [CONVD, 4]); inp("cb", [128, 17])
    inp("dtb", [H0, 1]); inp("alog", [H0, 1])
    outp("xn_fm", [D, 128 + T], BF16)
    outp("xcb_fm", [CONVD, T], BF16)
    outp("xdt_tm", [T, DI], BF16)
    outp("xw_tm", [T, DI], BF16)
    outp("cumA_fm", [H0, T])
    outp("cumA_tm", [T, H0])
    outp("S_loc", [DS, DI])
    outp("D_loc", [H0, 1])
    tn["id128"] = nc.inline_tensor(np.eye(128, dtype=BF), name="id128").ap()
    d0 = np.ones((H0, 512), np.float32)
    d0[:, 0::CH] = 0.0
    tn["d0pat"] = nc.inline_tensor(d0, name="d0pat").ap()
    tn["ones1"] = nc.inline_tensor(np.ones((1, 128), dtype=BF), name="ones1c").ap()
    tn["cvals"] = nc.inline_tensor(np.tile(np.array([EPS, 1.0 / D], np.float32), (128, 1)),
                                   name="cvals").ap()
    tn["ecl_d"] = nc.dram_tensor("ecl_d", [H0, NCHUNK], F32, kind="Internal").ap()
    emit_l0a(nc, tn)
    return nc


def emit_l0a(nc, t):
    NJ = 33
    s_uld = Sem(nc, "a_uld")
    s_wld = Sem(nc, "a_wld")
    s_ecl = Sem(nc, "a_ecl")
    s_pe = Sem(nc, "a_pe")
    s_ve = Sem(nc, "a_ve")
    s_ac = Sem(nc, "a_ac")
    s_st = Sem(nc, "a_st")
    s_tb = Sem(nc, "a_tb")

    import contextlib
    with contextlib.ExitStack() as _stk:
        _pfx = f"s{_NAME_SEQ[0]}_"
        _NAME_SEQ[0] += 1
        def _sb(nm, *a):
            return _stk.enter_context(nc.sbuf_tensor(_pfx + nm, *a))
        def _ps(nm, *a):
            return _stk.enter_context(nc.psum_tensor(_pfx + nm, *a))
        wxb_sb = _sb("wxb_sb", [128, 8, CONVD], BF16)
        wdt_sb = _sb("wdt_sb", [128, 8, H0], BF16)
        idn = _sb("idn", [128, 128], BF16)
        d0p = _sb("d0p", [H0, 512], F32)
        ones1 = _sb("ones1", [1, 128], BF16)
        cw_sb = _sb("cw_sb", [128, 17, 4], F32)
        cb_sb = _sb("cb_sb", [128, 17], F32)
        dtb_sb = _sb("dtb_sb", [H0, 1], F32)
        nega_sb = _sb("nega_sb", [H0, 1], F32)
        cst = _sb("cst", [128, 2], F32)
        ut = _sb("ut", [128, 2, D], F32)
        sq = _sb("sq", [128, 2, D], F32)
        xnb = _sb("xnb", [128, 2, D], BF16)
        stg = _sb("stg", [128, 2, 8, 512], BF16)
        stgh = _sb("stgh", [128, 8, 128], BF16)
        cols = _sb("cols", [128, 16], F32)
        pc = _sb("pc", [128, 515], F32)
        carry = _sb("carry", [128, 17, 3], F32)
        cvo = _sb("cvo", [128, 512], F32)
        xcb_sb = _sb("xcb_sb", [128, 512], BF16)
        dts = _sb("dts", [H0, 512], F32)
        dta = _sb("dta", [H0, 512], F32)
        dtc = _sb("dtc", [H0, 512], F32)
        wfc = _sb("wfc", [H0, 512], F32)
        dts_bf = _sb("dts_bf", [H0, 512], BF16)
        wfc_bf = _sb("wfc_bf", [H0, 512], BF16)
        dcl_sb = _sb("dcl_sb", [H0, NCHUNK], F32)
        ca_hi = _sb("ca_hi", [H0, 512], BF16)
        ca_lo = _sb("ca_lo", [H0, 512], BF16)
        ca_his = _sb("ca_his", [128, H0], F32)
        cat_tm = _sb("cat_tm", [128, H0], F32)
        ecl = _sb("ecl", [H0, 8], F32)
        eclrow = _sb("eclrow", [1, 256], F32)
        eclrow_bf = _sb("eclrow_bf", [1, 256], BF16)
        e512 = _sb("e512", [128, 32, 8], F32)
        xbt = _sb("xbt", [128, 2, CONVD], BF16)
        dt_tm = _sb("dt_tm", [128, H0], BF16)
        wf_tm = _sb("wf_tm", [128, H0], BF16)
        xdt_t = _sb("xdt_t", [128, DI], BF16)
        xw_t = _sb("xw_t", [128, DI], BF16)
        S = _sb("S", [DS, DI], F32)
        ps_ip = _ps("ps_ip", [128, 2, 512], F32)
        ps_dt = _ps("ps_dt", [H0, 512], F32)
        ps_tp = _ps("ps_tp", [128, 128], BF16)
        ps_cs = _ps("ps_cs", [DS, DI], F32)
        block = _stk.enter_context(nc.Block())
        ln_done = [0] * NJ
        stg_done = [0] * NJ
        tp_done = [0] * NJ
        evdc = [0]
        ip_ev2 = [0, 0]
        dt_used = [0]
        cum_done = [0]
        e512_pe = [0]
        e512_done = [0]
        eclbf_done = [0]
        si_done = [0]
        xcb_store = [0]
        grp_ip_pe = [0] * 9
        grp_st = [0] * 9
        wf_ready = [0] * 8
        xbt_free = [0, 0]
        tm_ev = [0]
        xdt_st = [0]
        scan_done = [0]
        wfc_used_pe = [0]
        hl_used_pe = [0]
        hl_ready = [0]
        cat_st = [0]

        nc.const_aps.register(F32, EPS, cst[:, 0:1])
        nc.const_aps.register(F32, 1.0 / D, cst[:, 1:2])

        def g_init(g):
            g.dma_start(out=cst[:], in_=t["cvals"]).then_inc(s_wld.h, 16)
            g.dma_start(out=wxb_sb[:], in_=t["wtxb"].rearrange("(c p) e -> p c e", p=128)).then_inc(s_wld.h, 16)
            g.dma_start(out=wdt_sb[:], in_=t["wtdt"].rearrange("(c p) e -> p c e", p=128)).then_inc(s_wld.h, 16)
            g.dma_start(out=idn[:], in_=t["id128"]).then_inc(s_wld.h, 16)
            g.dma_start(out=d0p[:], in_=t["d0pat"]).then_inc(s_wld.h, 16)
            g.dma_start(out=ones1[:], in_=t["ones1"]).then_inc(s_wld.h, 16)
            g.dma_start(out=cw_sb[:], in_=t["cw"].rearrange("(c p) k -> p c k", p=128)).then_inc(s_wld.h, 16)
            g.dma_start(out=cb_sb[:], in_=t["cb"]).then_inc(s_wld.h, 16)
            g.dma_start(out=dtb_sb[:], in_=t["dtb"]).then_inc(s_wld.h, 16)
            g.dma_start(out=nega_sb[:], in_=t["alog"]).then_inc(s_wld.h, 16)
        block.gpsimd(g_init)
        s_wld.v += 16 * 10

        def a_nega(a):
            a.wait_ge(s_wld.h, s_wld.v)
            s_ac.inc(a.activation(nega_sb[:], nega_sb[:], AF.Exp))
        block.scalar(a_nega)
        nega_ac = s_ac.v

        def v_nega(v):
            v.wait_ge(s_ac.h, nega_ac)
            s_ve.inc(v.tensor_scalar(nega_sb[:], nega_sb[:], -1.0, None, ALU.mult))
        block.vector(v_nega)
        nega_ve = s_ve.v
        nega_ready = [nega_ve]

        def v_init(v):
            s_ve.inc(v.memset(S[:], 0.0))
        block.vector(v_init)

        def usrc(j):
            return t["uh"] if j == 0 else t["u"][(j - 1) * 128:j * 128, :]

        def sy_seed(sy):
            sy.dma_start(out=ut[:, 0], in_=usrc(0)).then_inc(s_uld.h, 16)
            sy.dma_start(out=ut[:, 1], in_=usrc(1)).then_inc(s_uld.h, 16)
        block.sync(sy_seed)
        s_uld.v += 32

        sq_free = [0, 0]

        def emit_tile(j):
            def v_ln(v):
                v.wait_ge(s_uld.h, 16 * (j + 1))
                if j == 0:
                    v.wait_ge(s_ve.h, nega_ready[0])
                uu = ut[:, j % 2]
                s_ve.inc(v.reduce_sum(cols[:, 0:1], uu[:], axis=X))
            block.vector(v_ln)

            def a_sq(a):
                a.wait_ge(s_uld.h, 16 * (j + 1))
                a.wait_ge(s_ve.h, sq_free[j % 2])
                s_ac.inc(a.activation(sq[:, j % 2], ut[:, j % 2], AF.Square))
            block.scalar(a_sq)
            sq_ac = s_ac.v

            def v_ln2(v):
                v.wait_ge(s_ac.h, sq_ac)
                s_ve.inc(v.reduce_sum(cols[:, 2:3], sq[:, j % 2], axis=X))
            block.vector(v_ln2)
            red_ve = s_ve.v
            sq_free[j % 2] = red_ve

            def a_mm(a):
                a.wait_ge(s_ve.h, red_ve)
                a.activation(cols[:, 1:2], cols[:, 0:1], AF.Copy, scale=1.0 / D)
                s_ac.inc(a.activation(cols[:, 3:4], cols[:, 2:3], AF.Copy, scale=1.0 / D))
            block.scalar(a_mm)
            mm_ac = s_ac.v

            def v_var(v):
                v.wait_ge(s_ac.h, mm_ac)
                s_ve.inc(v.tensor_tensor(cols[:, 4:5], cols[:, 1:2], cols[:, 1:2], ALU.mult))
            block.vector(v_var)
            m2_ve = s_ve.v

            def v_var2(v):
                v.wait_ge(s_ve.h, m2_ve)
                s_ve.inc(v.tensor_tensor(cols[:, 5:6], cols[:, 3:4], cols[:, 4:5], ALU.subtract))
            block.vector(v_var2)
            var_ve = s_ve.v

            def a_rstd(a):
                a.wait_ge(s_ve.h, var_ve)
                s_ac.inc(a.activation(cols[:, 6:7], cols[:, 5:6], AF.Sqrt, bias=EPS))
            block.scalar(a_rstd)
            rstd_ac = s_ac.v

            def v_rec(v):
                v.wait_ge(s_ac.h, rstd_ac)
                s_ve.inc(v.reciprocal(cols[:, 7:8], cols[:, 6:7]))
            block.vector(v_rec)
            rec_ve = s_ve.v

            def v_xn(v):
                v.wait_ge(s_ve.h, rec_ve)
                if j >= 2:
                    v.wait_ge(s_pe.h, tp_done[j - 2])
                s_ve.inc(v.tensor_scalar(xnb[:, j % 2], ut[:, j % 2], cols[:, 1:2],
                                         cols[:, 7:8], ALU.subtract, ALU.mult))
            block.vector(v_xn)
            ln_done[j] = s_ve.v

            if j + 2 < NJ:
                free_at = ln_done[j]

                def sy_pf(sy):
                    sy.wait_ge(s_ve.h, free_at)
                    sy.dma_start(out=ut[:, (j + 2) % 2], in_=usrc(j + 2)).then_inc(s_uld.h, 16)
                block.sync(sy_pf)
                s_uld.v += 16

            g = (j - 1) // 4 if j >= 1 else -1
            for dc in range(8):
                def t_tp(tt, dc=dc):
                    if dc == 0:
                        tt.wait_ge(s_ve.h, ln_done[j])
                    tt.wait_ge(s_ve.h, evdc[0])
                    s_pe.inc(tt.transpose(ps_tp[:], xnb[:, j % 2, dc * 128:(dc + 1) * 128], idn[:]))
                block.tensor(t_tp)
                pe_now = s_pe.v

                def v_ev(v, dc=dc, pe_now=pe_now, g=g):
                    v.wait_ge(s_pe.h, pe_now)
                    if g >= 2 and dc == 0:
                        v.wait_ge(s_pe.h, grp_ip_pe[g - 2 + 1])
                        v.wait_ge(s_st.h, grp_st[g - 2 + 1])
                    if j == 0:
                        dst = stgh[:, dc, :]
                    else:
                        dst = stg[:, g % 2, dc, ((j - 1) % 4) * 128:((j - 1) % 4) * 128 + 128]
                    s_ve.inc(v.tensor_copy(dst, ps_tp[:]))
                block.vector(v_ev)
                evdc[0] = s_ve.v
            tp_done[j] = s_pe.v
            stg_done[j] = s_ve.v

        def emit_group(g, n_tok, stg_buf, col0):
            last_tile = 0 if g < 0 else 4 * g + 4
            def gg_st(gg):
                gg.wait_ge(s_ve.h, stg_done[last_tile])
                s_st.inc(gg.dma_start(
                    out=t["xn_fm"].rearrange("(c p) tt -> p c tt", p=128)[:, :, col0:col0 + n_tok],
                    in_=stg_buf), 16)
            block.gpsimd(gg_st)

            if g < 0:
                emit_xb_chunks(g, n_tok, stg_buf, last_tile)
                grp_ip_pe[g + 1] = s_pe.v
                grp_st[g + 1] = s_st.v
                return

            def t_dt(tt):
                tt.wait_ge(s_ve.h, stg_done[last_tile])
                tt.wait_ge(s_ac.h, dt_used[0])
                tt.wait_ge(s_wld.h, s_wld.v)
                for dc in range(8):
                    ins = tt.matmul(ps_dt[:, 0:n_tok], wdt_sb[:, dc, :], stg_buf[:, dc, :],
                                    start=(dc == 0), stop=(dc == 7))
                s_pe.inc(ins)
            block.tensor(t_dt)
            pe_dt = s_pe.v

            def a_sp(a):
                a.wait_ge(s_pe.h, pe_dt)
                a.wait_ge(s_pe.h, wfc_used_pe[0])
                a.wait_ge(s_st.h, grp_st[g] if g >= 0 else 0)
                a.activation(dta[:, 0:n_tok], ps_dt[:, 0:n_tok], AF.Abs, bias=dtb_sb[:, 0:1])
                a.activation(dta[:, 0:n_tok], dta[:, 0:n_tok], AF.Exp, scale=-1.0)
                a.activation(dta[:, 0:n_tok], dta[:, 0:n_tok], AF.Ln, bias=1.0)
                s_ac.inc(a.activation(dts[:, 0:n_tok], ps_dt[:, 0:n_tok], AF.Relu, bias=dtb_sb[:, 0:1]))
            block.scalar(a_sp)
            dt_used[0] = s_ac.v

            def v_dt(v):
                v.wait_ge(s_ac.h, dt_used[0])
                v.wait_ge(s_pe.h, wfc_used_pe[0])
                v.tensor_tensor(dts[:, 0:n_tok], dts[:, 0:n_tok], dta[:, 0:n_tok], ALU.add)
                v.tensor_copy(dts_bf[:, 0:n_tok], dts[:, 0:n_tok])
                v.tensor_scalar(dta[:, 0:n_tok], dts[:, 0:n_tok], nega_sb[:, 0:1], None, ALU.mult)
                s_ve.inc(v.tensor_tensor_scan(dtc[:, 0:n_tok], d0p[:, 0:n_tok],
                                              dta[:, 0:n_tok], 0.0, ALU.mult, ALU.add))
            block.vector(v_dt)
            cum_done[0] = s_ve.v

            if g >= 0:
                def gg_ca(gg):
                    gg.wait_ge(s_ve.h, cum_done[0])
                    s_st.inc(gg.dma_start(out=t["cumA_fm"][:, g * 512:(g + 1) * 512],
                                          in_=dtc[:, 0:512]), 16)
                block.gpsimd(gg_ca)

                def v_hl(v):
                    v.wait_ge(s_ve.h, cum_done[0])
                    v.wait_ge(s_pe.h, hl_used_pe[0])
                    v.tensor_copy(ca_hi[:, 0:512], dtc[:, 0:512])
                    s_ve.inc(v.tensor_tensor(ca_lo[:, 0:512], dtc[:, 0:512],
                                             ca_hi[:, 0:512], ALU.subtract))
                block.vector(v_hl)
                hl_ready[0] = s_ve.v

                def v_ecl(v):
                    v.wait_ge(s_ve.h, cum_done[0])
                    v.wait_ge(s_pe.h, e512_pe[0])
                    v.tensor_copy(dcl_sb[:, g * 8:(g + 1) * 8],
                                  dtc[:, 0:512].rearrange("h (c l) -> h c l", l=CH)[:, :, CH - 1])
                    s_ve.inc(v.tensor_copy(ecl[:, 0:8],
                                           dtc[:, 0:512].rearrange("h (c l) -> h c l", l=CH)[:, :, CH - 1]))
                block.vector(v_ecl)
                ecl_ve = s_ve.v

                def a_ecl(a):
                    a.wait_ge(s_ve.h, ecl_ve)
                    s_ac.inc(a.activation(ecl[:, 0:8], ecl[:, 0:8], AF.Exp))
                block.scalar(a_ecl)
                ecl_ac = s_ac.v

                def gg_ecl(gg):
                    gg.wait_ge(s_ac.h, ecl_ac)
                    gg.wait_ge(s_ve.h, eclbf_done[0])
                    base = s_ecl.v
                    gg.dma_start(out=t["ecl_d"][:, g * 8:(g + 1) * 8],
                                 in_=ecl[:, 0:8]).then_inc(s_ecl.h, 16)
                    gg.wait_ge(s_ecl.h, base + 16)
                    gg.dma_start(out=eclrow[:].rearrange("one (h c) -> one h c", c=8),
                                 in_=t["ecl_d"][:, g * 8:(g + 1) * 8].unsqueeze(0)
                                 ).then_inc(s_ecl.h, 16)
                block.gpsimd(gg_ecl)
                s_ecl.v += 32

                def v_eclbf(v):
                    v.wait_ge(s_ecl.h, s_ecl.v)
                    s_ve.inc(v.tensor_copy(eclrow_bf[:], eclrow[:]))
                block.vector(v_eclbf)
                eclbf_done[0] = s_ve.v

                def t_e512(tt):
                    tt.wait_ge(s_ve.h, eclbf_done[0])
                    tt.wait_ge(s_ve.h, ip_ev2[1])
                    s_pe.inc(tt.matmul(ps_ip[:, 1, 0:256], ones1[:], eclrow_bf[:],
                                       start=True, stop=True))
                block.tensor(t_e512)
                e512_pe[0] = s_pe.v

                def v_e512(v):
                    v.wait_ge(s_pe.h, e512_pe[0])
                    v.wait_ge(s_ve.h, scan_done[0])
                    s_ve.inc(v.tensor_copy(e512[:].rearrange("p h c -> p (h c)"),
                                           ps_ip[:, 1, 0:256]))
                block.vector(v_e512)
                ip_ev2[1] = s_ve.v
                e512_done[0] = s_ve.v

            emit_xb_chunks(g, n_tok, stg_buf, last_tile)
            grp_ip_pe[g + 1] = s_pe.v
            grp_st[g + 1] = s_st.v

        def emit_xb_chunks(g, n_tok, stg_buf, last_tile):
            for ec in range(17):
                par = ec % 2

                def t_ip(tt, ec=ec, par=par):
                    if ec == 0:
                        tt.wait_ge(s_ve.h, stg_done[last_tile])
                        tt.wait_ge(s_wld.h, s_wld.v)
                    tt.wait_ge(s_ve.h, ip_ev2[par])
                    for dc in range(8):
                        ins = tt.matmul(ps_ip[:, par, 0:n_tok],
                                        wxb_sb[:, dc, ec * 128:(ec + 1) * 128],
                                        stg_buf[:, dc, :], start=(dc == 0), stop=(dc == 7))
                    s_pe.inc(ins)
                block.tensor(t_ip)
                pe_ip = s_pe.v

                def v_pc(v, ec=ec, par=par):
                    v.wait_ge(s_pe.h, pe_ip)
                    v.wait_ge(s_ac.h, si_done[0])
                    if g < 0:
                        ins0 = v.memset(pc[:, 0:3], 0.0)
                    else:
                        ins0 = v.tensor_copy(pc[:, 0:3], carry[:, ec, :])
                    sv0 = s_ve.inc(ins0)
                    sv1 = s_ve.inc(v.tensor_copy(pc[:, 3:3 + n_tok], ps_ip[:, par, 0:n_tok]))
                    v.wait_ge(s_ve.h, sv1)
                    sv2 = s_ve.inc(v.tensor_copy(carry[:, ec, :], pc[:, n_tok:n_tok + 3]))
                    v.wait_ge(s_ve.h, sv2)
                    v.tensor_scalar(cvo[:, 0:n_tok], pc[:, 0:n_tok], cw_sb[:, ec, 0:1], None, ALU.mult)
                    for k in (1, 2, 3):
                        ins = v.scalar_tensor_tensor(cvo[:, 0:n_tok], pc[:, k:k + n_tok],
                                                     cw_sb[:, ec, k:k + 1], cvo[:, 0:n_tok],
                                                     ALU.mult, ALU.add)
                    s_ve.inc(ins)
                block.vector(v_pc)
                ip_ev2[par] = s_ve.v
                cv_done = s_ve.v

                if g >= 0:
                    def a_si(a, ec=ec, cv_done=cv_done):
                        a.wait_ge(s_ve.h, cv_done)
                        a.wait_ge(s_st.h, xcb_store[0])
                        s_ac.inc(a.activation(xcb_sb[:, 0:n_tok], cvo[:, 0:n_tok], AF.Silu,
                                              bias=cb_sb[:, ec:ec + 1]))
                    block.scalar(a_si)
                    si_done[0] = s_ac.v

                    def gg_xcb(gg, ec=ec):
                        gg.wait_ge(s_ac.h, si_done[0])
                        s_st.inc(gg.dma_start(out=t["xcb_fm"][ec * 128:(ec + 1) * 128,
                                                              g * 512:(g + 1) * 512],
                                              in_=xcb_sb[:, 0:512]), 16)
                    block.gpsimd(gg_xcb)
                    xcb_store[0] = s_st.v

        def emit_tm(g, sub):
            jj = 4 * g + sub + 1
            tcol = g * 512 + sub * 128
            par = (jj - 1) % 2

            def sy_tb(sy):
                sy.wait_ge(s_st.h, grp_st[g + 1])
                sy.wait_ge(s_ve.h, xbt_free[par])
                sy.dma_start_transpose(xbt[:, par], t["xcb_fm"][:, tcol:tcol + 128]).then_inc(s_tb.h, 16)
            block.sync(sy_tb)
            s_tb.v += 16
            tb_now = s_tb.v

            def t_s1(tt):
                tt.wait_ge(s_ve.h, wf_ready[g])
                tt.wait_ge(s_ve.h, evdc[0])
                tt.wait_ge(s_ve.h, tm_ev[0])
                s_pe.inc(tt.transpose(ps_tp[:, 0:H0], dts_bf[:, sub * 128:(sub + 1) * 128],
                                      idn[0:H0, 0:H0]))
            block.tensor(t_s1)
            pe1 = s_pe.v

            def v_dtm(v):
                v.wait_ge(s_pe.h, pe1)
                s_ve.inc(v.tensor_copy(dt_tm[:], ps_tp[:, 0:H0]))
            block.vector(v_dtm)
            ev1 = s_ve.v

            def t_s2(tt):
                tt.wait_ge(s_ve.h, ev1)
                s_pe.inc(tt.transpose(ps_tp[:, 0:H0], wfc_bf[:, sub * 128:(sub + 1) * 128],
                                      idn[0:H0, 0:H0]))
            block.tensor(t_s2)
            pe2 = s_pe.v
            wfc_used_pe[0] = pe2

            def v_wtm(v):
                v.wait_ge(s_pe.h, pe2)
                s_ve.inc(v.tensor_copy(wf_tm[:], ps_tp[:, 0:H0]))
            block.vector(v_wtm)

            def t_s3(tt, hv=s_ve.v):
                tt.wait_ge(s_ve.h, hl_ready[0])
                tt.wait_ge(s_ve.h, hv)
                s_pe.inc(tt.transpose(ps_tp[:, 0:H0], ca_hi[:, sub * 128:(sub + 1) * 128],
                                      idn[0:H0, 0:H0]))
            block.tensor(t_s3)
            pe3 = s_pe.v

            def v_hi(v):
                v.wait_ge(s_pe.h, pe3)
                s_ve.inc(v.tensor_copy(ca_his[:], ps_tp[:, 0:H0]))
            block.vector(v_hi)
            hi_ve = s_ve.v

            def t_s4(tt):
                tt.wait_ge(s_ve.h, hi_ve)
                s_pe.inc(tt.transpose(ps_tp[:, 0:H0], ca_lo[:, sub * 128:(sub + 1) * 128],
                                      idn[0:H0, 0:H0]))
            block.tensor(t_s4)
            pe4 = s_pe.v
            hl_used_pe[0] = pe4

            def v_lo(v):
                v.wait_ge(s_pe.h, pe4)
                v.wait_ge(s_st.h, cat_st[0])
                s_ve.inc(v.tensor_tensor(cat_tm[:], ca_his[:], ps_tp[:, 0:H0], ALU.add))
            block.vector(v_lo)
            cat_ve = s_ve.v

            def gg_cat(gg):
                gg.wait_ge(s_ve.h, cat_ve)
                s_st.inc(gg.dma_start(out=t["cumA_tm"][tcol:tcol + 128, :], in_=cat_tm[:]), 16)
            block.gpsimd(gg_cat)
            cat_st[0] = s_st.v

            tm_ev[0] = s_ve.v
            evdc[0] = s_ve.v

            def v_xdt(v):
                v.wait_ge(s_tb.h, tb_now)
                v.wait_ge(s_st.h, xdt_st[0])
                xv = _hq(xbt[:, par, 0:DI], P0)
                v.tensor_tensor(_hq(xdt_t[:], P0), xv,
                                _b(dt_tm[:].unsqueeze(2), [128, H0, P0]), ALU.mult)
                s_ve.inc(v.tensor_tensor(_hq(xw_t[:], P0), _hq(xdt_t[:], P0),
                                         _b(wf_tm[:].unsqueeze(2), [128, H0, P0]),
                                         ALU.mult))
            block.vector(v_xdt)
            xdt_done = s_ve.v

            def gg_xdt(gg):
                gg.wait_ge(s_ve.h, xdt_done)
                gg.dma_start(out=t["xdt_tm"][tcol:tcol + 128, :], in_=xdt_t[:]).then_inc(s_st.h, 16)
                gg.dma_start(out=t["xw_tm"][tcol:tcol + 128, :], in_=xw_t[:]).then_inc(s_st.h, 16)
            block.gpsimd(gg_xdt)
            s_st.v += 32
            xdt_st[0] = s_st.v

            for cc in range(2):
                coff = cc * 64
                ci = (sub * 2 + cc)

                def t_cs(tt, coff=coff):
                    tt.wait_ge(s_ve.h, xdt_done)
                    tt.wait_ge(s_ve.h, scan_done[0])
                    for nn in range(4):
                        ins = tt.matmul(ps_cs[:, nn * 512:(nn + 1) * 512],
                                        xbt[coff:coff + 64, par, DI:DI + DS],
                                        xw_t[coff:coff + 64, nn * 512:(nn + 1) * 512],
                                        start=True, stop=True)
                    s_pe.inc(ins)
                block.tensor(t_cs)
                pe_cs = s_pe.v

                def v_scan(v, ci=ci, pe_cs=pe_cs):
                    v.wait_ge(s_pe.h, pe_cs)
                    v.wait_ge(s_ve.h, e512_done[0])
                    v.tensor_tensor(_hq(S[:], P0), _hq(S[:], P0),
                                    _b(e512[:, :, ci:ci + 1], [DS, H0, P0]), ALU.mult)
                    s_ve.inc(v.tensor_tensor(S[:], S[:], ps_cs[:], ALU.add))
                block.vector(v_scan)
                scan_done[0] = s_ve.v
            xbt_free[par] = scan_done[0]

        emit_tile(0)
        emit_group(-1, 128, stgh[:], 0)
        for g in range(8):
            for sub in range(4):
                emit_tile(4 * g + sub + 1)
            emit_group(g, 512, stg[:, g % 2], 128 + g * 512)

            def a_wf(a, g=g):
                a.wait_ge(s_ve.h, cum_done[0])
                a.wait_ge(s_pe.h, wfc_used_pe[0])
                for c in range(8):
                    cl = dtc[:, c * 64 + 63:c * 64 + 64]
                    ins = a.activation(wfc[:, c * 64:(c + 1) * 64], dtc[:, c * 64:(c + 1) * 64],
                                       AF.Exp, scale=-1.0, bias=cl)
                s_ac.inc(ins)
            block.scalar(a_wf)
            wf_ac = s_ac.v

            def v_wf(v, wf_ac=wf_ac):
                v.wait_ge(s_ac.h, wf_ac)
                v.wait_ge(s_pe.h, wfc_used_pe[0])
                s_ve.inc(v.tensor_copy(wfc_bf[:], wfc[:]))
            block.vector(v_wf)
            wf_ready[g] = s_ve.v

            for sub in range(4):
                emit_tm(g, sub)

        def gg_sloc(gg):
            gg.wait_ge(s_ve.h, scan_done[0])
            s_st.inc(gg.dma_start(out=t["S_loc"], in_=S[:]), 16)
        block.gpsimd(gg_sloc)

        def v_dl(v):
            v.wait_ge(s_ve.h, scan_done[0])
            s_ve.inc(v.reduce_sum(dta[:, 0:1], dcl_sb[:], axis=X))
        block.vector(v_dl)
        dl_ve = s_ve.v

        def gg_fin(gg):
            gg.wait_ge(s_ve.h, dl_ve)
            s_st.inc(gg.dma_start(out=t["D_loc"], in_=dta[:, 0:1]), 16)
            gg.wait_ge(s_st.h, s_st.v)
            gg.wait_ge(s_tb.h, s_tb.v)
            gg.wait_ge(s_uld.h, s_uld.v)
            gg.wait_ge(s_ecl.h, s_ecl.v)
        block.gpsimd(gg_fin)

    return nc



def _tr(a):
    return np.ascontiguousarray(np.asarray(a).T)


def prep_l0a_inputs(inputs):
    u = inputs["L0_attn_u"]
    W = inputs["L0_attn_in_proj_w"]
    wtxb = _tr(W[DI:DI + CONVD])
    wtdt = _tr(W[4352:4384])
    cw = np.ascontiguousarray(inputs["L0_attn_conv_w"][:CONVD, 0, :])
    cb = np.ascontiguousarray(inputs["L0_attn_conv_b"][:CONVD].reshape(17, 128).T)
    dtb = np.ascontiguousarray(inputs["L0_attn_dt_bias"][:, None])
    alog = np.ascontiguousarray(inputs["L0_attn_A_log"][:, None])
    maps = []
    for c in range(NCORES):
        b, q = divmod(c, 4)
        t0 = q * T
        useg = np.ascontiguousarray(u[b, t0:t0 + T])
        if q == 0:
            uh = np.zeros((128, D), np.float32)
        else:
            uh = np.ascontiguousarray(u[b, t0 - 128:t0])
        maps.append(dict(u=useg, uh=uh, wtxb=wtxb, wtdt=wtdt, cw=cw, cb=cb,
                         dtb=dtb, alog=alog))
    return maps



def build_moe():
    nc = bass.Bass("TRN2", target_bir_lowering=False, detect_race_conditions=False)
    tn = {}
    def inp(name, shape, dt=F32):
        tn[name] = nc.dram_tensor(name, shape, dt, kind="ExternalInput").ap()
    inp("xn_fm", [D, 128 + T], BF16)
    inp("router", [D, 8])
    inp("w1", [8, D, 2816]); inp("v1", [8, D, 2816]); inp("w2", [8, 2816, D])
    inp("bias_rep", [128, D])
    tn["g_out"] = nc.dram_tensor("g_out", [T, D], BF16, kind="ExternalOutput").ap()
    tn["gacc"] = nc.dram_tensor("gacc", [T, D], F32, kind="Internal").ap()
    emit_moe(nc, tn)
    return nc


def emit_moe(nc, t):
    FC = 22
    s_ld = Sem(nc, "m_ld")
    s_wl = Sem(nc, "m_wl")
    s_pe = Sem(nc, "m_pe")
    s_ve = Sem(nc, "m_ve")
    s_ac = Sem(nc, "m_ac")
    s_ga = Sem(nc, "m_ga")

    import contextlib
    with contextlib.ExitStack() as _stk:
        _pfx = f"s{_NAME_SEQ[0]}_"
        _NAME_SEQ[0] += 1
        def _sb(nm, *a):
            return _stk.enter_context(nc.sbuf_tensor(_pfx + nm, *a))
        def _ps(nm, *a):
            return _stk.enter_context(nc.psum_tensor(_pfx + nm, *a))
        wr_sb = _sb("wr_sb", [128, 8, 8], BF16)
        w1e = _sb("w1e", [128, 8, 2816], BF16)
        v1e = _sb("v1e", [128, 8, 2816], BF16)
        w2e = _sb("w2e", [128, FC, D], BF16)
        xnt = _sb("xnt", [128, 2, 8, 512], BF16)
        h_sb = _sb("h_sb", [128, FC, 512], BF16)
        gtmp = _sb("gtmp", [128, 512], F32)
        ysb = _sb("ysb", [128, 2, D], F32)
        scw = _sb("scw", [128, 256], F32)
        sccol = _sb("sccol", [128, 16], F32)
        scex = _sb("scex", [128, 8], F32)
        bias_sb = _sb("bias_sb", [128, D], F32)
        fin = _sb("fin", [128, D], F32)
        gfin = _sb("gfin", [128, D], BF16)
        ps_r = _ps("ps_r", [128, 8], F32)
        ps_h1 = _ps("ps_h1", [128, 512], F32)
        ps_h2 = _ps("ps_h2", [128, 512], F32)
        ps_y = _ps("ps_y", [128, 2, D], F32)
        block = _stk.enter_context(nc.Block())

        xn_r = t["xn_fm"].rearrange("(c p) tt -> p c tt", p=128)

        def g_init(g):
            g.dma_start(out=wr_sb[:], in_=t["router"].rearrange("(c p) e -> p c e", p=128)).then_inc(s_wl.h, 16)
            g.dma_start(out=bias_sb[:], in_=t["bias_rep"]).then_inc(s_wl.h, 16)
        block.gpsimd(g_init)
        s_wl.v += 32
        winit = s_wl.v

        sc_ve = [0]
        sc_pe = [0]
        for tj in range(32):
            par = tj % 2

            def sy_x(sy, tj=tj, par=par):
                if tj >= 2:
                    sy.wait_ge(s_pe.h, sc_pe[0])
                sy.dma_start(out=xnt[:, par, :, 0:128],
                             in_=xn_r[:, :, 128 + tj * 128:128 + tj * 128 + 128]
                             ).then_inc(s_ld.h, 16)
            block.sync(sy_x)
            s_ld.v += 16
            ld_now = s_ld.v

            def t_r(tt, par=par, ld_now=ld_now):
                tt.wait_ge(s_ld.h, ld_now)
                tt.wait_ge(s_wl.h, winit)
                if sc_ve[0]:
                    tt.wait_ge(s_ve.h, sc_ve[0])
                for dc in range(8):
                    ins = tt.matmul(ps_r[:], xnt[:, par, dc, 0:128], wr_sb[:, dc, :],
                                    start=(dc == 0), stop=(dc == 7))
                s_pe.inc(ins)
            block.tensor(t_r)
            sc_pe[0] = s_pe.v
            pe_now = s_pe.v

            def v_m1(v, pe_now=pe_now):
                v.wait_ge(s_pe.h, pe_now)
                s_ve.inc(v.reduce_max(sccol[:, 0:1], ps_r[:], axis=X, negate=True))
            block.vector(v_m1)
            m1_ve = s_ve.v

            def a_ex(a, m1_ve=m1_ve):
                a.wait_ge(s_ve.h, m1_ve)
                s_ac.inc(a.activation(scex[:], ps_r[:], AF.Exp, bias=sccol[:, 0:1]))
            block.scalar(a_ex)
            ex_ac = s_ac.v

            def v_sm(v, tj=tj, ex_ac=ex_ac):
                v.wait_ge(s_ac.h, ex_ac)
                sv = s_ve.inc(v.reduce_sum(sccol[:, 1:2], scex[:], axis=X))
                v.wait_ge(s_ve.h, sv)
                sv = s_ve.inc(v.reciprocal(sccol[:, 2:3], sccol[:, 1:2]))
                v.wait_ge(s_ve.h, sv)
                sv = s_ve.inc(v.tensor_scalar(sccol[:, 4:12], scex[:], sccol[:, 2:3], None, ALU.mult))
                v.wait_ge(s_ve.h, sv)
                sv = s_ve.inc(v.reduce_max(sccol[:, 3:4], sccol[:, 4:12], axis=X))
                v.wait_ge(s_ve.h, sv)
                sv = s_ve.inc(v.tensor_scalar(scex[:], sccol[:, 4:12], sccol[:, 3:4], None, ALU.is_ge))
                v.wait_ge(s_ve.h, sv)
                sv = s_ve.inc(v.scalar_tensor_tensor(scex[:], scex[:], -1e30, sccol[:, 4:12],
                                                     ALU.mult, ALU.add))
                v.wait_ge(s_ve.h, sv)
                sv = s_ve.inc(v.reduce_max(sccol[:, 13:14], scex[:], axis=X))
                v.wait_ge(s_ve.h, sv)
                sv = s_ve.inc(v.tensor_scalar(scex[:], sccol[:, 4:12], sccol[:, 13:14], None, ALU.is_ge))
                v.wait_ge(s_ve.h, sv)
                sv = s_ve.inc(v.tensor_tensor(scw[:, tj * 8:(tj + 1) * 8], scex[:],
                                              sccol[:, 4:12], ALU.mult))
                v.wait_ge(s_ve.h, sv)
            block.vector(v_sm)
            sc_ve[0] = s_ve.v

        h_pe = [0]
        y_pe = [0]
        hev_ve = [0]
        hev_ac = [0]
        yev_ve = [0, 0]
        ysb_free = [0, 0]
        ga_tile = [0] * 32
        wl_w1v1 = [0]
        wl_w2 = [0]
        xn_ld = [s_pe.v, s_pe.v]

        def load_w1v1(e):
            def g_w(g, e=e, hp=h_pe[0]):
                g.wait_ge(s_pe.h, hp)
                g.dma_start(out=w1e[:], in_=t["w1"][e].rearrange("(c p) f -> p c f", p=128)).then_inc(s_wl.h, 16)
                g.dma_start(out=v1e[:], in_=t["v1"][e].rearrange("(c p) f -> p c f", p=128)).then_inc(s_wl.h, 16)
            block.gpsimd(g_w)
            s_wl.v += 32
            wl_w1v1[0] = s_wl.v

        def load_w2(e):
            def g_w(g, e=e, yp=y_pe[0]):
                g.wait_ge(s_pe.h, yp)
                g.dma_start(out=w2e[:], in_=t["w2"][e].rearrange("(c p) d -> p c d", p=128)).then_inc(s_wl.h, 16)
            block.gpsimd(g_w)
            s_wl.v += 16
            wl_w2[0] = s_wl.v

        load_w1v1(0)
        load_w2(0)
        for e in range(8):
            we_h = wl_w1v1[0]
            we_y = wl_w2[0]
            for ti in range(8):
                par = ti % 2

                def sy_x2(sy, ti=ti, par=par, xl=xn_ld[par]):
                    sy.wait_ge(s_pe.h, xl)
                    sy.dma_start(out=xnt[:, par],
                                 in_=xn_r[:, :, 128 + ti * 512:128 + (ti + 1) * 512]
                                 ).then_inc(s_ld.h, 16)
                block.sync(sy_x2)
                s_ld.v += 16
                ld_now = s_ld.v

                for fc in range(FC):
                    def t_h(tt, fc=fc, par=par, ld_now=ld_now, we_h=we_h,
                            ha=hev_ac[0], hv=hev_ve[0]):
                        if fc == 0:
                            tt.wait_ge(s_ld.h, ld_now)
                            tt.wait_ge(s_wl.h, we_h)
                        tt.wait_ge(s_ac.h, ha)
                        for dc in range(8):
                            tt.matmul(ps_h1[:], w1e[:, dc, fc * 128:(fc + 1) * 128],
                                      xnt[:, par, dc, :], start=(dc == 0), stop=(dc == 7))
                        tt.wait_ge(s_ve.h, hv)
                        for dc in range(8):
                            ins = tt.matmul(ps_h2[:], v1e[:, dc, fc * 128:(fc + 1) * 128],
                                            xnt[:, par, dc, :], start=(dc == 0), stop=(dc == 7))
                        s_pe.inc(ins)
                    block.tensor(t_h)
                    hm_pe = s_pe.v

                    def a_g(a, hm_pe=hm_pe):
                        a.wait_ge(s_pe.h, hm_pe)
                        s_ac.inc(a.activation(gtmp[:], ps_h1[:], AF.Gelu))
                    block.scalar(a_g)
                    hev_ac[0] = s_ac.v
                    gac = s_ac.v

                    def v_h(v, fc=fc, gac=gac):
                        v.wait_ge(s_ac.h, gac)
                        s_ve.inc(v.tensor_tensor(h_sb[:, fc, :], gtmp[:], ps_h2[:], ALU.mult))
                    block.vector(v_h)
                    hev_ve[0] = s_ve.v
                h_pe[0] = s_pe.v
                h_ve = s_ve.v
                xn_ld[par] = s_pe.v

                if ti == 7 and e < 7:
                    load_w1v1(e + 1)

                for tq in range(4):
                    ypar = tq % 2

                    def t_y(tt, tq=tq, ypar=ypar, h_ve=h_ve, we_y=we_y, yv=yev_ve[ypar]):
                        if tq == 0:
                            tt.wait_ge(s_ve.h, h_ve)
                            tt.wait_ge(s_wl.h, we_y)
                        tt.wait_ge(s_ve.h, yv)
                        for fc in range(FC):
                            for dh in range(2):
                                ins = tt.matmul(ps_y[:, ypar, dh * 512:(dh + 1) * 512],
                                                h_sb[:, fc, tq * 128:(tq + 1) * 128],
                                                w2e[:, fc, dh * 512:(dh + 1) * 512],
                                                start=(fc == 0), stop=(fc == FC - 1))
                        s_pe.inc(ins)
                    block.tensor(t_y)
                    y_pe[0] = s_pe.v
                    ym_pe = s_pe.v

                    def v_y(v, ti=ti, tq=tq, ypar=ypar, e=e, ym_pe=ym_pe, yf=ysb_free[ypar]):
                        v.wait_ge(s_pe.h, ym_pe)
                        v.wait_ge(s_ga.h, yf)
                        col = (ti * 4 + tq) * 8 + e
                        s_ve.inc(v.tensor_scalar(ysb[:, ypar], ps_y[:, ypar], scw[:, col:col + 1],
                                                 None, ALU.mult))
                    block.vector(v_y)
                    yev_ve[ypar] = s_ve.v
                    yv2 = s_ve.v

                    def g_acc(g, ti=ti, tq=tq, ypar=ypar, e=e, yv2=yv2, gt=ga_tile[ti * 4 + tq]):
                        g.wait_ge(s_ve.h, yv2)
                        g.wait_ge(s_ga.h, gt)
                        tile = ti * 4 + tq
                        ins = g.dma_start(out=t["gacc"][tile * 128:(tile + 1) * 128, :],
                                          in_=ysb[:, ypar],
                                          accum_op=(ALU.bypass if e == 0 else ALU.add))
                        s_ga.inc(ins, 16)
                    block.gpsimd(g_acc)
                    ga_tile[ti * 4 + tq] = s_ga.v
                    ysb_free[ypar] = s_ga.v
            load_w2(e + 1) if e < 7 else None

        fin_free = [0]
        gfin_free = [0]
        for tk in range(32):
            def sy_f(sy, tk=tk, ff=fin_free[0]):
                sy.wait_ge(s_ga.h, ga_tile[tk])
                if ff:
                    sy.wait_ge(s_ac.h, ff)
                sy.dma_start(out=fin[:],
                             in_=t["gacc"][tk * 128:(tk + 1) * 128, :]).then_inc(s_ld.h, 16)
            block.sync(sy_f)
            s_ld.v += 16
            fl = s_ld.v

            def v_f(v, fl=fl):
                v.wait_ge(s_ld.h, fl)
                s_ve.inc(v.tensor_tensor(fin[:], fin[:], bias_sb[:], ALU.add))
            block.vector(v_f)
            add_ve = s_ve.v

            def a_f(a, add_ve=add_ve, gf=gfin_free[0]):
                a.wait_ge(s_ve.h, add_ve)
                if gf:
                    a.wait_ge(s_ga.h, gf)
                s_ac.inc(a.activation(gfin[:], fin[:], AF.Silu))
            block.scalar(a_f)
            fin_free[0] = s_ac.v
            si_ac = s_ac.v

            def g_f(g, tk=tk, si_ac=si_ac):
                g.wait_ge(s_ac.h, si_ac)
                ins = g.dma_start(out=t["g_out"][tk * 128:(tk + 1) * 128, :], in_=gfin[:])
                s_ga.inc(ins, 16)
            block.gpsimd(g_f)
            gfin_free[0] = s_ga.v

        def g_end(g):
            g.wait_ge(s_ga.h, s_ga.v)
            g.wait_ge(s_ld.h, s_ld.v)
            g.wait_ge(s_wl.h, s_wl.v)
        block.gpsimd(g_end)

    return nc


def prep_moe_inputs(inputs, xn_fms):
    router = np.ascontiguousarray(inputs["L0_ffn_router_w"])
    w1 = inputs["L0_ffn_w1"]; v1 = inputs["L0_ffn_v1"]; w2 = inputs["L0_ffn_w2"]
    bias_rep = np.ascontiguousarray(np.broadcast_to(inputs["L0_ffn_bias"][None, :], (128, D)))
    return [dict(xn_fm=xn_fms[c], router=router, w1=w1, v1=v1, w2=w2,
                 bias_rep=bias_rep) for c in range(NCORES)]



def build_l0b():
    nc = bass.Bass("TRN2", target_bir_lowering=False, detect_race_conditions=False)
    tn = {}
    def inp(name, shape, dt=F32):
        tn[name] = nc.dram_tensor(name, shape, dt, kind="ExternalInput").ap()
    inp("xn_fm", [D, 128 + T], BF16)
    inp("xcb_fm", [CONVD, T], BF16)
    inp("xdt_tm", [T, DI], BF16)
    inp("xw_tm", [T, DI], BF16)
    inp("cumA_fm", [H0, T])
    inp("cumA_tm", [T, H0])
    inp("S_all", [8, DS, DI])
    inp("D_tall", [8, H0])
    inp("Mt", [8, 8])
    inp("pm_rep", [H0, 8])
    inp("wtz", [D, DI]); inp("wtc", [D, DS])
    inp("cw_c", [DS, 4]); inp("cb_c", [DS, 1])
    inp("d_rep", [128, H0])
    inp("nw_rep", [128, DI])
    inp("wto", [DI, D])
    tn["s_tm"] = nc.dram_tensor("s_tm", [T, D], F32, kind="ExternalOutput").ap()
    tn["tg_d"] = nc.dram_tensor("tg_d", [T, DI], BF16, kind="Internal").ap()
    tn["ecl_d"] = nc.dram_tensor("ecl_d2", [H0, NCHUNK], F32, kind="Internal").ap()
    tn["f_d"] = nc.dram_tensor("f_d", [H0, 8], F32, kind="Internal").ap()
    tn["id128"] = nc.inline_tensor(np.eye(128, dtype=BF), name="id128b").ap()
    tn["ones1"] = nc.inline_tensor(np.ones((1, 128), dtype=BF), name="ones1b").ap()
    tn["ones1f"] = nc.inline_tensor(np.ones((1, 64), np.float32), name="ones1f_c").ap()
    tn["mask_su"] = nc.inline_tensor(np.triu(np.ones((64, 64), np.float32)), name="mask_su_c").ap()
    tn["cvals"] = nc.inline_tensor(np.tile(np.array([EPS, 1.0 / DI], np.float32), (128, 1)),
                                   name="cvalsb").ap()
    emit_l0b(nc, tn)
    return nc


def emit_l0b(nc, t):
    s_ld = Sem(nc, "b_ld")
    s_tb = Sem(nc, "b_tb")
    s_wl = Sem(nc, "b_wl")
    s_pe = Sem(nc, "b_pe")
    s_ve = Sem(nc, "b_ve")
    s_ac = Sem(nc, "b_ac")
    s_po = Sem(nc, "b_po")
    s_st = Sem(nc, "b_st")

    import contextlib
    with contextlib.ExitStack() as _stk:
        _pfx = f"s{_NAME_SEQ[0]}_"
        _NAME_SEQ[0] += 1
        def _sb(nm, *a):
            return _stk.enter_context(nc.sbuf_tensor(_pfx + nm, *a))
        def _ps(nm, *a):
            return _stk.enter_context(nc.psum_tensor(_pfx + nm, *a))
        wz_sb = _sb("wz_sb", [128, 8, DI], BF16)
        wc_sb = _sb("wc_sb", [128, 8, DS], BF16)
        ones1 = _sb("ones1", [1, 128], BF16)
        ones1f = _sb("ones1f", [1, 64], F32)
        msu = _sb("msu", [64, 64], F32)
        cst = _sb("cst", [128, 2], F32)
        cwc_sb = _sb("cwc_sb", [128, 4], F32)
        cbc_sb = _sb("cbc_sb", [128, 1], F32)
        drep = _sb("drep", [128, H0], F32)
        nwrep = _sb("nwrep", [128, DI], F32)
        pmrep = _sb("pmrep", [H0, 8], F32)
        dtall = _sb("dtall", [8, H0], F32)
        mt_sb = _sb("mt_sb", [8, 8], F32)
        fh_sb = _sb("fh_sb", [H0, 8], F32)
        fbc = _sb("fbc", [128, H0, 8], F32)
        S = _sb("S", [DS, DI], F32)
        S_bf = _sb("S_bf", [DS, DI], BF16)
        xnt = _sb("xnt", [128, 2, 8, 512], BF16)
        ca_sb = _sb("ca_sb", [H0, 512], F32)
        ecl = _sb("ecl", [H0, 8], F32)
        eclrow = _sb("eclrow", [1, 256], F32)
        eclrow_bf = _sb("eclrow_bf", [1, 256], BF16)
        e512 = _sb("e512", [128, H0, 8], F32)
        car = _sb("car", [1, DI], F32)
        erow_bf = _sb("erow_bf", [1, DI], BF16)
        c_sb = _sb("c_sb", [128, 512], BF16)
        b_sb = _sb("b_sb", [128, 512], BF16)
        pcc = _sb("pcc", [128, 515], F32)
        carc = _sb("carc", [128, 3], F32)
        cvoc = _sb("cvoc", [128, 512], F32)
        z_sb = _sb("z_sb", [128, DI], F32)
        xbt = _sb("xbt", [128, 2176], BF16)
        xdtc = _sb("xdtc", [64, DI], BF16)
        xw_t = _sb("xw_t", [128, DI], BF16)
        cat_tm = _sb("cat_tm", [128, H0], F32)
        b2_sb = _sb("b2_sb", [64, DI], F32)
        d_sb = _sb("d_sb", [64, DI], F32)
        gm_sb = _sb("gm_sb", [64, 64], F32)
        rhs_bf = _sb("rhs_bf", [64, DI], BF16)
        cs_bf = _sb("cs_bf", [128, H0, 64], BF16)
        y_t = _sb("y_t", [128, DI], F32)
        t1_t = _sb("t1_t", [128, DI], F32)
        sq_t = _sb("sq_t", [128, DI], F32)
        tg_sb = _sb("tg_sb", [128, DI], BF16)
        cols = _sb("cols", [128, 4], F32)
        tgf = _sb("tgf", [128, 16, 512], BF16)
        s_outb = _sb("s_outb", [128, D], F32)
        ps_a = _ps("ps_a", [128, DI], F32)
        ps_b = _ps("ps_b", [128, DI], F32)
        block = _stk.enter_context(nc.Block())

        nc.const_aps.register(F32, EPS, cst[:, 0:1])
        nc.const_aps.register(F32, 1.0 / DI, cst[:, 1:2])

        xn_r = t["xn_fm"].rearrange("(c p) tt -> p c tt", p=128)
        ca_r = t["cumA_fm"]

        def g_init(g):
            for o_, i_ in [(wz_sb[:], t["wtz"].rearrange("(c p) e -> p c e", p=128)),
                           (wc_sb[:], t["wtc"].rearrange("(c p) e -> p c e", p=128)),
                           (ones1[:], t["ones1"]),
                           (ones1f[:], t["ones1f"]), (msu[:], t["mask_su"]),
                           (cst[:], t["cvals"]), (cwc_sb[:], t["cw_c"]),
                           (cbc_sb[:], t["cb_c"]), (drep[:], t["d_rep"]),
                           (nwrep[:], t["nw_rep"]), (pmrep[:], t["pm_rep"]),
                           (dtall[:], t["D_tall"]), (mt_sb[:], t["Mt"])]:
                g.dma_start(out=o_, in_=i_).then_inc(s_wl.h, 16)
        block.gpsimd(g_init)
        s_wl.v += 16 * 13
        winit = s_wl.v

        def t_f(tt):
            tt.wait_ge(s_wl.h, winit)
            s_pe.inc(tt.matmul(ps_b[0:H0, 0:8], dtall[:], mt_sb[:], start=True, stop=True))
        block.tensor(t_f)
        f_pe = s_pe.v

        def a_f(a):
            a.wait_ge(s_pe.h, f_pe)
            s_ac.inc(a.activation(fh_sb[:], ps_b[0:H0, 0:8], AF.Exp))
        block.scalar(a_f)
        f_ac = s_ac.v

        def v_f(v):
            v.wait_ge(s_ac.h, f_ac)
            sv = s_ve.inc(v.tensor_tensor(fh_sb[:], fh_sb[:], pmrep[:], ALU.mult))
            v.wait_ge(s_ve.h, sv)
        block.vector(v_f)
        f_ve = s_ve.v

        def g_fd(g):
            g.wait_ge(s_ve.h, f_ve)
            base = s_wl.v
            g.dma_start(out=t["f_d"], in_=fh_sb[:]).then_inc(s_wl.h, 16)
            g.wait_ge(s_wl.h, base + 16)
            g.dma_start(out=eclrow[0:1, 0:256],
                        in_=t["f_d"].rearrange("h j -> (h j)").unsqueeze(0)).then_inc(s_wl.h, 16)
        block.gpsimd(g_fd)
        s_wl.v += 32
        frow_ld = s_wl.v

        def v_frow(v):
            v.wait_ge(s_wl.h, frow_ld)
            sv = s_ve.inc(v.tensor_copy(eclrow_bf[0:1, 0:256], eclrow[0:1, 0:256]))
            v.wait_ge(s_ve.h, sv)
        block.vector(v_frow)
        frow_ve = s_ve.v

        def t_fb(tt):
            tt.wait_ge(s_ve.h, frow_ve)
            s_pe.inc(tt.matmul(ps_a[:, 0:256], ones1[:], eclrow_bf[0:1, 0:256],
                               start=True, stop=True))
        block.tensor(t_fb)
        fb_pe = s_pe.v

        def v_fbc(v):
            v.wait_ge(s_pe.h, fb_pe)
            v.tensor_copy(fbc[:].rearrange("p h j -> p (h j)"), ps_a[:, 0:256])
            s_ve.inc(v.memset(S[:], 0.0))
        block.vector(v_fbc)
        sin_ve = [s_ve.v]

        for j in range(8):
            def sy_sj(sy, j=j, prev=sin_ve[0]):
                sy.wait_ge(s_ve.h, prev)
                sy.dma_start(out=y_t[:], in_=t["S_all"][j]).then_inc(s_ld.h, 16)
            block.sync(sy_sj)
            s_ld.v += 16
            sj_ld = s_ld.v

            def v_sj(v, j=j, sj_ld=sj_ld):
                v.wait_ge(s_ld.h, sj_ld)
                v.tensor_tensor(_hq(y_t[0:DS, :], P0), _hq(y_t[0:DS, :], P0),
                                _b(fbc[:, :, j:j + 1], [DS, H0, P0]), ALU.mult)
                s_ve.inc(v.tensor_tensor(S[:], S[:], y_t[0:DS, :], ALU.add))
            block.vector(v_sj)
            sin_ve[0] = s_ve.v

        def v_sbf(v, p=sin_ve[0]):
            v.wait_ge(s_ve.h, p)
            s_ve.inc(v.tensor_copy(S_bf[:], S[:]))
        block.vector(v_sbf)
        scan_ve = [s_ve.v]

        xn_free = [s_pe.v, s_pe.v]
        ca_used = [0]
        ecl_free = [0]
        eclbf_ve = [0]
        e512_pe = [0]
        e512_ve = [0]
        psa_free = [s_ve.v]
        psb_yfree = [0]
        d_ve = [0]
        carc_ve = [0]
        sic_ac = [0]
        csb_pe = [0]
        csb_ve = [0]
        car_pe = [0]
        car_ac = [0]
        gm_ve = [0]
        rhs_ve = [0]
        cs_ve = [0]
        xbt_users = [0, 0]
        xdt_users = [0]
        z_ac = [0]
        z_users = [0]
        yt_done = [0]
        gat_ve = [0]
        tg_st = [0]
        cat_users = [0]
        erow_pe = [0]
        xdtc_pe = [0]

        def sy_h(sy):
            sy.dma_start(out=xnt[:, 0], in_=xn_r[:, :, 0:512]).then_inc(s_ld.h, 16)
        block.sync(sy_h)
        s_ld.v += 16
        hl = s_ld.v

        def t_ch(tt):
            tt.wait_ge(s_ld.h, hl)
            tt.wait_ge(s_wl.h, winit)
            tt.wait_ge(s_ve.h, psa_free[0])
            for dc in range(8):
                ins = tt.matmul(ps_a[0:DS, 0:128], wc_sb[:, dc, :], xnt[:, 0, dc, 0:128],
                                start=(dc == 0), stop=(dc == 7))
            s_pe.inc(ins)
        block.tensor(t_ch)
        pe_ch = s_pe.v
        xn_free[0] = s_pe.v

        def v_ch(v):
            v.wait_ge(s_pe.h, pe_ch)
            sv = s_ve.inc(v.tensor_copy(pcc[:, 3:131], ps_a[0:DS, 0:128]))
            v.wait_ge(s_ve.h, sv)
            sv = s_ve.inc(v.tensor_copy(carc[:], pcc[:, 128:131]))
            v.wait_ge(s_ve.h, sv)
        block.vector(v_ch)
        carc_ve[0] = s_ve.v
        psa_free[0] = s_ve.v

        for g in range(8):
            par = g % 2

            def sy_xn(sy, g=g, par=par, xf=xn_free[par]):
                sy.wait_ge(s_pe.h, xf)
                sy.dma_start(out=xnt[:, par],
                             in_=xn_r[:, :, 128 + g * 512:128 + (g + 1) * 512]).then_inc(s_ld.h, 16)
            block.sync(sy_xn)
            s_ld.v += 16
            xl = s_ld.v

            def sy_ca(sy, g=g, cu=ca_used[0]):
                if cu:
                    sy.wait_ge(s_ve.h, cu)
                sy.dma_start(out=ca_sb[:], in_=ca_r[:, g * 512:(g + 1) * 512]).then_inc(s_ld.h, 16)
            block.sync(sy_ca)
            s_ld.v += 16
            cal = s_ld.v

            def v_ecl(v, cal=cal, ef=ecl_free[0]):
                v.wait_ge(s_ld.h, cal)
                if ef:
                    v.wait_ge(s_wl.h, ef)
                sv = s_ve.inc(v.tensor_copy(ecl[:, 0:8],
                                            ca_sb[:].rearrange("h (c l) -> h c l", l=CH)[:, :, CH - 1]))
                v.wait_ge(s_ve.h, sv)
            block.vector(v_ecl)
            ca_used[0] = s_ve.v
            ecl_ve = s_ve.v

            def a_ecl(a, ecl_ve=ecl_ve):
                a.wait_ge(s_ve.h, ecl_ve)
                s_ac.inc(a.activation(ecl[:, 0:8], ecl[:, 0:8], AF.Exp))
            block.scalar(a_ecl)
            ecl_ac = s_ac.v

            def g_ecl(g_, g2=g, ecl_ac=ecl_ac, ebv=eclbf_ve[0]):
                g_.wait_ge(s_ac.h, ecl_ac)
                if ebv:
                    g_.wait_ge(s_ve.h, ebv)
                base = s_wl.v
                g_.dma_start(out=t["ecl_d"][:, g2 * 8:(g2 + 1) * 8], in_=ecl[:, 0:8]).then_inc(s_wl.h, 16)
                g_.wait_ge(s_wl.h, base + 16)
                g_.dma_start(out=eclrow[:].rearrange("one (h c) -> one h c", c=8),
                             in_=t["ecl_d"][:, g2 * 8:(g2 + 1) * 8].unsqueeze(0)).then_inc(s_wl.h, 16)
            block.gpsimd(g_ecl)
            s_wl.v += 32
            ecl_free[0] = s_wl.v
            erow_wl = s_wl.v

            def v_ebf(v, wv=erow_wl):
                v.wait_ge(s_wl.h, wv)
                sv = s_ve.inc(v.tensor_copy(eclrow_bf[:], eclrow[:]))
                v.wait_ge(s_ve.h, sv)
            block.vector(v_ebf)
            eclbf_ve[0] = s_ve.v

            def t_e5(tt, ev=eclbf_ve[0], pf=psa_free[0]):
                tt.wait_ge(s_ve.h, ev)
                tt.wait_ge(s_ve.h, pf)
                s_pe.inc(tt.matmul(ps_a[:, 0:256], ones1[:], eclrow_bf[:], start=True, stop=True))
            block.tensor(t_e5)
            e512_pe[0] = s_pe.v

            def v_e5(v, pv=e512_pe[0], sc=scan_ve[0]):
                v.wait_ge(s_pe.h, pv)
                v.wait_ge(s_ve.h, sc)
                s_ve.inc(v.tensor_copy(e512[:].rearrange("p h c -> p (h c)"), ps_a[:, 0:256]))
            block.vector(v_e5)
            e512_ve[0] = s_ve.v
            psa_free[0] = s_ve.v

            def t_c(tt, par=par, xl=xl, pf=psa_free[0]):
                tt.wait_ge(s_ld.h, xl)
                tt.wait_ge(s_ve.h, pf)
                for dc in range(8):
                    ins = tt.matmul(ps_a[0:DS, 0:512], wc_sb[:, dc, :], xnt[:, par, dc, :],
                                    start=(dc == 0), stop=(dc == 7))
                s_pe.inc(ins)
            block.tensor(t_c)
            pe_c = s_pe.v

            def v_c(v, pe_c=pe_c, sic=sic_ac[0], cv=carc_ve[0]):
                v.wait_ge(s_pe.h, pe_c)
                if sic:
                    v.wait_ge(s_ac.h, sic)
                v.wait_ge(s_ve.h, cv)
                sv = s_ve.inc(v.tensor_copy(pcc[:, 0:3], carc[:]))
                sv = s_ve.inc(v.tensor_copy(pcc[:, 3:515], ps_a[0:DS, 0:512]))
                v.wait_ge(s_ve.h, sv)
                sv = s_ve.inc(v.tensor_copy(carc[:], pcc[:, 512:515]))
                v.wait_ge(s_ve.h, sv)
                v.tensor_scalar(cvoc[:], pcc[:, 0:512], cwc_sb[:, 0:1], None, ALU.mult)
                for k in (1, 2, 3):
                    ins = v.scalar_tensor_tensor(cvoc[:], pcc[:, k:k + 512], cwc_sb[:, k:k + 1],
                                                 cvoc[:], ALU.mult, ALU.add)
                s_ve.inc(ins)
            block.vector(v_c)
            carc_ve[0] = s_ve.v
            cv_ve = s_ve.v
            psa_free[0] = s_ve.v

            def a_sic(a, cv_ve=cv_ve, cp=csb_pe[0], cvv=csb_ve[0]):
                a.wait_ge(s_ve.h, cv_ve)
                if cp:
                    a.wait_ge(s_pe.h, cp)
                if cvv:
                    a.wait_ge(s_ve.h, cvv)
                s_ac.inc(a.activation(c_sb[:], cvoc[:], AF.Silu, bias=cbc_sb[:, 0:1]))
            block.scalar(a_sic)
            sic_ac[0] = s_ac.v
            csb_ac = s_ac.v

            def sy_b(sy, g=g, bp=csb_pe[0]):
                if bp:
                    sy.wait_ge(s_pe.h, bp)
                sy.dma_start(out=b_sb[:], in_=t["xcb_fm"][DI:DI + DS, g * 512:(g + 1) * 512]
                             ).then_inc(s_ld.h, 16)
            block.sync(sy_b)
            s_ld.v += 16
            bl = s_ld.v

            for sub in range(4):
                tcol = g * 512 + sub * 128

                def sy_t(sy, tcol=tcol, xu=xbt_users[:], du=xdt_users[0], cu=cat_users[0]):
                    sy.wait_ge(s_pe.h, xu[0])
                    sy.wait_ge(s_ve.h, xu[1])
                    sy.dma_start_transpose(xbt[:], t["xcb_fm"][:, tcol:tcol + 128]).then_inc(s_tb.h, 16)
                    sy.wait_ge(s_pe.h, du)
                    sy.dma_start(out=xw_t[:], in_=t["xw_tm"][tcol:tcol + 128, :]).then_inc(s_tb.h, 16)
                    sy.wait_ge(s_ve.h, cu)
                    sy.dma_start(out=cat_tm[:], in_=t["cumA_tm"][tcol:tcol + 128, :]).then_inc(s_tb.h, 16)
                block.sync(sy_t)
                s_tb.v += 48
                tl = s_tb.v

                def t_z(tt, par=par, sub=sub, pf=psa_free[0], xl=xl):
                    tt.wait_ge(s_ld.h, xl)
                    tt.wait_ge(s_ve.h, pf)
                    for nn in range(4):
                        for dc in range(8):
                            ins = tt.matmul(ps_a[:, nn * 512:(nn + 1) * 512],
                                            xnt[:, par, dc, sub * 128:(sub + 1) * 128],
                                            wz_sb[:, dc, nn * 512:(nn + 1) * 512],
                                            start=(dc == 0), stop=(dc == 7))
                    s_pe.inc(ins)
                block.tensor(t_z)
                z_pe = s_pe.v
                xn_free[par] = s_pe.v

                def a_z(a, z_pe=z_pe, zu=z_users[0]):
                    a.wait_ge(s_pe.h, z_pe)
                    if zu:
                        a.wait_ge(s_ve.h, zu)
                    s_ac.inc(a.activation(z_sb[:], ps_a[:], AF.Silu))
                block.scalar(a_z)
                z_ac[0] = s_ac.v
                psa_free_ac = s_ac.v

                for cc in range(2):
                    coff = cc * 64
                    ci = sub * 2 + cc

                    def sy_car(sy, g=g, ci=ci, tcol=tcol, coff=coff, cp=car_pe[0],
                               ca2=car_ac[0], xp=xdtc_pe[0]):
                        if cp:
                            sy.wait_ge(s_pe.h, cp)
                        if ca2:
                            sy.wait_ge(s_ac.h, ca2)
                        sy.dma_start(out=car[0:1].rearrange("one (h l) -> one h l", l=CH),
                                     in_=ca_r[:, g * 512 + ci * 64:g * 512 + (ci + 1) * 64].unsqueeze(0)
                                     ).then_inc(s_tb.h, 16)
                        if xp:
                            sy.wait_ge(s_pe.h, xp)
                        sy.dma_start(out=xdtc[:],
                                     in_=t["xdt_tm"][tcol + coff:tcol + coff + 64, :]).then_inc(s_tb.h, 16)
                    block.sync(sy_car)
                    s_tb.v += 32
                    carl = s_tb.v

                    def t_gt(tt, ci=ci, bl=bl, ca=csb_ac, zac=psa_free_ac, tl=tl,
                             pf=psa_free[0]):
                        tt.wait_ge(s_ld.h, bl)
                        tt.wait_ge(s_ac.h, max(ca, zac))
                        tt.wait_ge(s_ve.h, pf)
                        tt.wait_ge(s_tb.h, tl)
                        s_pe.inc(tt.matmul(ps_a[0:64, 0:64], b_sb[:, ci * 64:(ci + 1) * 64],
                                           c_sb[:, ci * 64:(ci + 1) * 64], start=True, stop=True))
                    block.tensor(t_gt)
                    gt_pe = s_pe.v
                    csb_pe[0] = s_pe.v

                    def v_gm(v, gt_pe=gt_pe, gu=rhs_ve[0]):
                        v.wait_ge(s_pe.h, gt_pe)
                        if gu:
                            v.wait_ge(s_ve.h, gu)
                        sv = s_ve.inc(v.tensor_tensor(gm_sb[:], ps_a[0:64, 0:64], msu[:], ALU.mult))
                        v.wait_ge(s_ve.h, sv)
                    block.vector(v_gm)
                    gm_done = s_ve.v

                    def t_b1(tt, carl=carl, ya=psb_yfree[0], dv=d_ve[0]):
                        tt.wait_ge(s_tb.h, carl)
                        if ya:
                            tt.wait_ge(s_ac.h, ya)
                        if dv:
                            tt.wait_ge(s_ve.h, dv)
                        for nn in range(4):
                            ins = tt.matmul(ps_b[0:64, nn * 512:(nn + 1) * 512], ones1f[:],
                                            car[0:1, nn * 512:(nn + 1) * 512], start=True, stop=True)
                        s_pe.inc(ins)
                    block.tensor(t_b1)
                    b1_pe = s_pe.v
                    car_pe[0] = s_pe.v

                    def p_b2(p, coff=coff, tl=tl, lu=d_ve[0]):
                        p.wait_ge(s_tb.h, tl)
                        if lu:
                            p.wait_ge(s_ve.h, lu)
                        ins = p.tensor_copy(b2_sb[:].rearrange("s (h l) -> s h l", l=P0),
                                            _b(cat_tm[coff:coff + 64, :].unsqueeze(2), [64, H0, P0]))
                        s_po.inc(ins)
                    block.gpsimd(p_b2)
                    b2_po = s_po.v

                    def v_d(v, b1_pe=b1_pe, b2_po=b2_po, le=rhs_ve[0]):
                        v.wait_ge(s_pe.h, b1_pe)
                        v.wait_ge(s_po.h, b2_po)
                        if le:
                            v.wait_ge(s_ve.h, le)
                        sv = s_ve.inc(v.tensor_tensor(d_sb[:], ps_b[0:64, :], b2_sb[:], ALU.subtract))
                        v.wait_ge(s_ve.h, sv)
                        sv = s_ve.inc(v.tensor_scalar(d_sb[:], d_sb[:], 0.0, None, ALU.min))
                        v.wait_ge(s_ve.h, sv)
                    block.vector(v_d)
                    d_ve[0] = s_ve.v
                    dmin_ve = s_ve.v

                    def a_lx(a, dmin_ve=dmin_ve, ru=rhs_ve[0]):
                        a.wait_ge(s_ve.h, dmin_ve)
                        s_ac.inc(a.activation(d_sb[:], d_sb[:], AF.Exp))
                    block.scalar(a_lx)
                    lx_ac = s_ac.v

                    def v_rhs(v, lx_ac=lx_ac, gm_done=gm_done, yp=yt_done[0]):
                        v.wait_ge(s_ac.h, lx_ac)
                        s_ve.inc(v.tensor_tensor(rhs_bf[:].rearrange("s (h l) -> s h l", l=P0),
                                                 d_sb[:].rearrange("s (h l) -> s h l", l=P0),
                                                 _b(gm_sb[:].unsqueeze(1), [64, H0, P0]), ALU.mult))
                    block.vector(v_rhs)
                    rhs_ve[0] = s_ve.v
                    rhs_done = s_ve.v

                    def a_er(a, carl=carl, ep=erow_pe[0]):
                        a.wait_ge(s_tb.h, carl)
                        if ep:
                            a.wait_ge(s_pe.h, ep)
                        s_ac.inc(a.activation(erow_bf[:], car[:], AF.Exp))
                    block.scalar(a_er)
                    car_ac[0] = s_ac.v
                    er_ac = s_ac.v

                    def t_eb(tt, er_ac=er_ac, zac=psa_free_ac, gmv=gm_done, csv=cs_ve[0],
                             pf=psa_free[0]):
                        tt.wait_ge(s_ac.h, max(er_ac, zac))
                        tt.wait_ge(s_ve.h, max(gmv, pf))
                        if csv:
                            tt.wait_ge(s_ve.h, csv)
                        for nn in range(4):
                            ins = tt.matmul(ps_a[:, nn * 512:(nn + 1) * 512], ones1[:],
                                            erow_bf[0:1, nn * 512:(nn + 1) * 512], start=True, stop=True)
                        s_pe.inc(ins)
                    block.tensor(t_eb)
                    eb_pe = s_pe.v
                    erow_pe[0] = s_pe.v

                    def v_cs(v, ci=ci, eb_pe=eb_pe, ca=csb_ac):
                        v.wait_ge(s_pe.h, eb_pe)
                        v.wait_ge(s_ac.h, ca)
                        s_ve.inc(v.tensor_tensor(cs_bf[:],
                                                 _b(c_sb[:, ci * 64:(ci + 1) * 64].unsqueeze(1), [128, H0, P0]),
                                                 ps_a[:].rearrange("p (h l) -> p h l", l=P0), ALU.mult))
                    block.vector(v_cs)
                    cs_ve[0] = s_ve.v
                    csb_ve[0] = s_ve.v
                    cs_done = s_ve.v

                    def t_y(tt, coff=coff, cc=cc, rhs_done=rhs_done, cs_done=cs_done,
                            sc=scan_ve[0], tl=tl, ya=psb_yfree[0]):
                        tt.wait_ge(s_ve.h, max(rhs_done, cs_done, sc))
                        tt.wait_ge(s_tb.h, tl)
                        if ya:
                            tt.wait_ge(s_ac.h, ya)
                        base = cc * 64
                        for h in range(H0):
                            tt.matmul(ps_b[base:base + 64, h * P0:(h + 1) * P0],
                                      rhs_bf[:, h * P0:(h + 1) * P0],
                                      xdtc[:, h * P0:(h + 1) * P0],
                                      start=True, stop=False)
                            ins = tt.matmul(ps_b[base:base + 64, h * P0:(h + 1) * P0],
                                            cs_bf[:, h, :], S_bf[:, h * P0:(h + 1) * P0],
                                            start=False, stop=True)
                        s_pe.inc(ins)
                    block.tensor(t_y)
                    y_pe = s_pe.v
                    xdtc_pe[0] = s_pe.v

                    def a_y(a, cc=cc, y_pe=y_pe, gv=gat_ve[0]):
                        a.wait_ge(s_pe.h, y_pe)
                        if gv:
                            a.wait_ge(s_ve.h, gv)
                        base = cc * 64
                        s_ac.inc(a.activation(y_t[base:base + 64, :], ps_b[base:base + 64, :], AF.Copy))
                    block.scalar(a_y)
                    psb_yfree[0] = s_ac.v
                    yt_done[0] = s_ac.v

                    def t_cs2(tt, coff=coff, cs_done=cs_done, tl=tl, sc=scan_ve[0]):
                        tt.wait_ge(s_ve.h, max(cs_done, sc))
                        tt.wait_ge(s_tb.h, tl)
                        for nn in range(4):
                            ins = tt.matmul(ps_a[:, nn * 512:(nn + 1) * 512],
                                            xbt[coff:coff + 64, DI:DI + DS],
                                            xw_t[coff:coff + 64, nn * 512:(nn + 1) * 512],
                                            start=True, stop=True)
                        s_pe.inc(ins)
                    block.tensor(t_cs2)
                    cs2_pe = s_pe.v
                    xdt_users[0] = s_pe.v
                    xbt_users[0] = s_pe.v

                    def v_scan(v, g=g, ci=ci, cs2_pe=cs2_pe, ev=e512_ve[0]):
                        v.wait_ge(s_pe.h, cs2_pe)
                        v.wait_ge(s_ve.h, ev)
                        v.tensor_tensor(_hq(S[:], P0), _hq(S[:], P0),
                                        _b(e512[:, :, ci:ci + 1], [DS, H0, P0]), ALU.mult)
                        v.tensor_tensor(S[:], S[:], ps_a[:], ALU.add)
                        s_ve.inc(v.tensor_copy(S_bf[:], S[:]))
                    block.vector(v_scan)
                    scan_ve[0] = s_ve.v
                    psa_free[0] = s_ve.v

                def v_gat(v, yd=yt_done[0], za=z_ac[0], tl=tl, ts_=tg_st[0]):
                    v.wait_ge(s_ac.h, max(yd, za))
                    v.wait_ge(s_tb.h, tl)
                    v.tensor_tensor(_hq(sq_t[:], P0), _hq(xbt[:, 0:DI], P0),
                                    _b(drep[:].unsqueeze(2), [128, H0, P0]), ALU.mult)
                    v.tensor_tensor(y_t[:], y_t[:], sq_t[:], ALU.add)
                    sv = s_ve.inc(v.tensor_tensor(t1_t[:], y_t[:], z_sb[:], ALU.mult))
                    v.wait_ge(s_ve.h, sv)
                block.vector(v_gat)
                gat1 = s_ve.v
                z_users[0] = s_ve.v
                xbt_users[1] = s_ve.v

                def a_sq2(a, gat1=gat1):
                    a.wait_ge(s_ve.h, gat1)
                    s_ac.inc(a.activation(sq_t[:], t1_t[:], AF.Square))
                block.scalar(a_sq2)
                sq_ac = s_ac.v

                def v_rs(v, sq_ac=sq_ac):
                    v.wait_ge(s_ac.h, sq_ac)
                    sv = s_ve.inc(v.reduce_sum(cols[:, 0:1], sq_t[:], axis=X))
                    v.wait_ge(s_ve.h, sv)
                block.vector(v_rs)
                rs_ve = s_ve.v

                def a_rstd(a, rs_ve=rs_ve):
                    a.wait_ge(s_ve.h, rs_ve)
                    s_ac.inc(a.activation(cols[:, 1:2], cols[:, 0:1], AF.Sqrt,
                                          scale=1.0 / DI, bias=EPS))
                block.scalar(a_rstd)
                rst_ac = s_ac.v

                def v_tg(v, tcol=tcol, rst_ac=rst_ac, ts_=tg_st[0]):
                    v.wait_ge(s_ac.h, rst_ac)
                    sv = s_ve.inc(v.reciprocal(cols[:, 2:3], cols[:, 1:2]))
                    v.wait_ge(s_ve.h, sv)
                    v.tensor_scalar(t1_t[:], t1_t[:], cols[:, 2:3], None, ALU.mult)
                    if ts_:
                        v.wait_ge(s_st.h, ts_)
                    s_ve.inc(v.tensor_tensor(tg_sb[:], t1_t[:], nwrep[:], ALU.mult))
                block.vector(v_tg)
                gat_ve[0] = s_ve.v
                tg_ve = s_ve.v

                def g_tg(g_, tcol=tcol, tg_ve=tg_ve):
                    g_.wait_ge(s_ve.h, tg_ve)
                    ins = g_.dma_start(out=t["tg_d"][tcol:tcol + 128, :], in_=tg_sb[:])
                    s_st.inc(ins, 16)
                block.gpsimd(g_tg)
                tg_st[0] = s_st.v

        wo_v = wz_sb[:].rearrange("p c e -> p (c e)")

        def g_wo(g_, pv=s_pe.v):
            g_.wait_ge(s_pe.h, pv)
            g_.dma_start(out=wo_v.rearrange("p (c e) -> p c e", e=D),
                         in_=t["wto"].rearrange("(c p) e -> p c e", p=128)).then_inc(s_wl.h, 16)
        block.gpsimd(g_wo)
        s_wl.v += 16
        wo_ld = s_wl.v

        tgf_pe = [0]
        so_st = [0]
        for g in range(8):
            def sy_tg(sy, g=g, pf=tgf_pe[0]):
                sy.wait_ge(s_st.h, tg_st[0])
                if pf:
                    sy.wait_ge(s_pe.h, pf)
                for zc in range(16):
                    sy.dma_start_transpose(tgf[:, zc, :],
                                           t["tg_d"][g * 512:(g + 1) * 512, zc * 128:(zc + 1) * 128]
                                           ).then_inc(s_tb.h, 16)
            block.sync(sy_tg)
            s_tb.v += 16 * 16
            tfl = s_tb.v

            for sub in range(4):
                def t_o(tt, sub=sub, tfl=tfl, pf=psa_free[0], wl=wo_ld):
                    tt.wait_ge(s_tb.h, tfl)
                    tt.wait_ge(s_ve.h, pf)
                    tt.wait_ge(s_wl.h, wl)
                    for nn in range(2):
                        for zc in range(16):
                            ins = tt.matmul(ps_a[:, nn * 512:(nn + 1) * 512],
                                            tgf[:, zc, sub * 128:(sub + 1) * 128],
                                            wo_v.rearrange("p (c e) -> p c e", e=D)[:, zc, nn * 512:(nn + 1) * 512],
                                            start=(zc == 0), stop=(zc == 15))
                    s_pe.inc(ins)
                block.tensor(t_o)
                tgf_pe[0] = s_pe.v
                o_pe = s_pe.v

                def v_o(v, o_pe=o_pe, sos=so_st[0]):
                    v.wait_ge(s_pe.h, o_pe)
                    if sos:
                        v.wait_ge(s_st.h, sos)
                    s_ve.inc(v.tensor_copy(s_outb[:], ps_a[:, 0:D]))
                block.vector(v_o)
                psa_free[0] = s_ve.v
                ov = s_ve.v

                def g_o(g_, g2=g, sub=sub, ov=ov):
                    g_.wait_ge(s_ve.h, ov)
                    ins = g_.dma_start(out=t["s_tm"][g2 * 512 + sub * 128:g2 * 512 + sub * 128 + 128, :],
                                       in_=s_outb[:])
                    s_st.inc(ins, 16)
                block.gpsimd(g_o)
                so_st[0] = s_st.v

        def g_end(g_):
            g_.wait_ge(s_st.h, s_st.v)
            g_.wait_ge(s_tb.h, s_tb.v)
            g_.wait_ge(s_ld.h, s_ld.v)
            g_.wait_ge(s_wl.h, s_wl.v)
            g_.wait_ge(s_po.h, s_po.v)
        block.gpsimd(g_end)

    return nc


def prep_l0b_inputs(inputs, A_outs):
    W = inputs["L0_attn_in_proj_w"]
    wtz = _tr(W[0:DI])
    wtc = _tr(W[DI + CONVD - DS + 0:0] if False else W[4224:4352])
    cw_c = np.ascontiguousarray(inputs["L0_attn_conv_w"][CONVD:CONVD + DS, 0, :])
    cb_c = np.ascontiguousarray(inputs["L0_attn_conv_b"][CONVD:CONVD + DS, None])
    d_rep = np.ascontiguousarray(np.broadcast_to(inputs["L0_attn_D"][None, :], (128, H0)))
    nw_rep = np.ascontiguousarray(np.broadcast_to(inputs["L0_attn_norm_w"][None, :], (128, DI)))
    wto = _tr(inputs["L0_attn_out_proj_w"])
    S_all_b = [np.stack([A_outs[b * 4 + j]["S_loc"] for j in range(4)] +
                        [np.zeros((DS, DI), np.float32)] * 4) for b in range(2)]
    D_tall_b = [np.stack([np.asarray(A_outs[b * 4 + j]["D_loc"])[:, 0] for j in range(4)] +
                         [np.zeros(H0, np.float32)] * 4) for b in range(2)]
    maps = []
    for c in range(NCORES):
        b, q = divmod(c, 4)
        Mt = np.zeros((8, 8), np.float32)
        pm = np.zeros(8, np.float32)
        for j in range(q):
            pm[j] = 1.0
            for m in range(j + 1, q):
                Mt[m, j] = 1.0
        pm_rep = np.ascontiguousarray(np.broadcast_to(pm[None, :], (H0, 8)))
        A = A_outs[c]
        maps.append(dict(
            xn_fm=np.asarray(A["xn_fm"]), xcb_fm=np.asarray(A["xcb_fm"]),
            xdt_tm=np.asarray(A["xdt_tm"]), xw_tm=np.asarray(A["xw_tm"]),
            cumA_fm=np.asarray(A["cumA_fm"]), cumA_tm=np.asarray(A["cumA_tm"]),
            S_all=S_all_b[b], D_tall=D_tall_b[b], Mt=Mt, pm_rep=pm_rep,
            wtz=wtz, wtc=wtc, cw_c=cw_c, cb_c=cb_c, d_rep=d_rep,
            nw_rep=nw_rep, wto=wto))
    return maps



def build_c1():
    nc = bass.Bass("TRN2", target_bir_lowering=False, detect_race_conditions=False)
    tn = {}
    def inp(name, shape, dt=F32):
        tn[name] = nc.dram_tensor(name, shape, dt, kind="ExternalInput").ap()
    inp("u", [T, D])
    inp("s_tm", [T, D])
    inp("g_out", [T, D], BF16)
    tn["x1_tm"] = nc.dram_tensor("x1_tm", [T, D], F32, kind="ExternalOutput").ap()
    tn["xn1_ext"] = nc.dram_tensor("xn1_ext", [D, 1 + T], BF16, kind="ExternalOutput").ap()
    tn["id128"] = nc.inline_tensor(np.eye(128, dtype=BF), name="id128c").ap()
    tn["cvals"] = nc.inline_tensor(np.tile(np.array([EPS, 1.0 / D], np.float32), (128, 1)),
                                   name="cvalsc").ap()
    emit_c1(nc, tn)
    return nc


def emit_c1(nc, t):
    s_ld = Sem(nc, "c_ld")
    s_wl = Sem(nc, "c_wl")
    s_pe = Sem(nc, "c_pe")
    s_ve = Sem(nc, "c_ve")
    s_ac = Sem(nc, "c_ac")
    s_st = Sem(nc, "c_st")
    import contextlib
    with contextlib.ExitStack() as _stk:
        _pfx = f"s{_NAME_SEQ[0]}_"
        _NAME_SEQ[0] += 1
        def _sb(nm, *a):
            return _stk.enter_context(nc.sbuf_tensor(_pfx + nm, *a))
        def _ps(nm, *a):
            return _stk.enter_context(nc.psum_tensor(_pfx + nm, *a))
        idn = _sb("idn", [128, 128], BF16)
        cst = _sb("cst", [128, 2], F32)
        ut = _sb("ut", [128, 2, D], F32)
        st_ = _sb("st_", [128, 2, D], F32)
        gt_ = _sb("gt_", [128, 2, D], BF16)
        x1 = _sb("x1", [128, D], F32)
        sq = _sb("sq", [128, D], F32)
        xnb = _sb("xnb", [128, D], BF16)
        stg = _sb("stg", [128, 2, 8, 128], BF16)
        cols = _sb("cols", [128, 8], F32)
        ps_tp = _ps("ps_tp", [128, 128], BF16)
        block = _stk.enter_context(nc.Block())

        nc.const_aps.register(F32, EPS, cst[:, 0:1])
        nc.const_aps.register(F32, 1.0 / D, cst[:, 1:2])

        def g_init(g):
            g.dma_start(out=idn[:], in_=t["id128"]).then_inc(s_wl.h, 16)
            g.dma_start(out=cst[:], in_=t["cvals"]).then_inc(s_wl.h, 16)
        block.gpsimd(g_init)
        s_wl.v += 32
        winit = s_wl.v

        xo_r = t["xn1_ext"].rearrange("(c p) tt -> p c tt", p=128)

        x1_done = [0] * 33
        stg_st = [0, 0]
        tp_done = [0] * 33
        evdc = [0]
        x1st = [0]

        for j in range(32):
            par = j % 2

            def sy_l(sy, j=j, par=par, fa=x1_done[j - 2] if j >= 2 else 0):
                if fa:
                    sy.wait_ge(s_ve.h, fa)
                sy.dma_start(out=ut[:, par], in_=t["u"][j * 128:(j + 1) * 128, :]).then_inc(s_ld.h, 16)
                sy.dma_start(out=st_[:, par], in_=t["s_tm"][j * 128:(j + 1) * 128, :]).then_inc(s_ld.h, 16)
                sy.dma_start(out=gt_[:, par], in_=t["g_out"][j * 128:(j + 1) * 128, :]).then_inc(s_ld.h, 16)
            block.sync(sy_l)
            s_ld.v += 48
            ldn = s_ld.v

            def v_x1(v, j=j, par=par, ldn=ldn, xs=x1st[0], av=s_ac.v):
                v.wait_ge(s_ld.h, ldn)
                if xs:
                    v.wait_ge(s_st.h, xs)
                if av:
                    v.wait_ge(s_ac.h, av)
                v.tensor_tensor(x1[:], st_[:, par], gt_[:, par], ALU.mult)
                sv = s_ve.inc(v.tensor_tensor(x1[:], x1[:], ut[:, par], ALU.add))
                v.wait_ge(s_ve.h, sv)
                s_ve.inc(v.reduce_sum(cols[:, 0:1], x1[:], axis=X))
            block.vector(v_x1)
            x1v = s_ve.v

            def g_x1(g, j=j, x1v=x1v):
                g.wait_ge(s_ve.h, x1v)
                ins = g.dma_start(out=t["x1_tm"][j * 128:(j + 1) * 128, :], in_=x1[:])
                s_st.inc(ins, 16)
            block.gpsimd(g_x1)
            x1st[0] = s_st.v

            def a_sq(a, x1v=x1v):
                a.wait_ge(s_ve.h, x1v)
                s_ac.inc(a.activation(sq[:], x1[:], AF.Square))
            block.scalar(a_sq)
            sq_ac = s_ac.v

            def v_l2(v, sq_ac=sq_ac):
                v.wait_ge(s_ac.h, sq_ac)
                sv = s_ve.inc(v.reduce_sum(cols[:, 1:2], sq[:], axis=X))
                v.wait_ge(s_ve.h, sv)
            block.vector(v_l2)
            r2 = s_ve.v

            def a_mm(a, r2=r2):
                a.wait_ge(s_ve.h, r2)
                a.activation(cols[:, 2:3], cols[:, 0:1], AF.Copy, scale=1.0 / D)
                s_ac.inc(a.activation(cols[:, 3:4], cols[:, 1:2], AF.Copy, scale=1.0 / D))
            block.scalar(a_mm)
            mm_ac = s_ac.v

            def v_var(v, mm_ac=mm_ac):
                v.wait_ge(s_ac.h, mm_ac)
                sv = s_ve.inc(v.tensor_tensor(cols[:, 4:5], cols[:, 2:3], cols[:, 2:3], ALU.mult))
                v.wait_ge(s_ve.h, sv)
                sv = s_ve.inc(v.tensor_tensor(cols[:, 5:6], cols[:, 3:4], cols[:, 4:5], ALU.subtract))
                v.wait_ge(s_ve.h, sv)
            block.vector(v_var)
            var_ve = s_ve.v

            def a_rs(a, var_ve=var_ve):
                a.wait_ge(s_ve.h, var_ve)
                s_ac.inc(a.activation(cols[:, 6:7], cols[:, 5:6], AF.Sqrt, bias=EPS))
            block.scalar(a_rs)
            rs_ac = s_ac.v

            def v_xn(v, j=j, rs_ac=rs_ac, tp=tp_done[j - 1] if j >= 1 else 0):
                v.wait_ge(s_ac.h, rs_ac)
                sv = s_ve.inc(v.reciprocal(cols[:, 7:8], cols[:, 6:7]))
                v.wait_ge(s_ve.h, sv)
                if tp:
                    v.wait_ge(s_pe.h, tp)
                s_ve.inc(v.tensor_scalar(xnb[:], x1[:], cols[:, 2:3], cols[:, 7:8],
                                         ALU.subtract, ALU.mult))
            block.vector(v_xn)
            x1_done[j] = s_ve.v
            ln_ve = s_ve.v

            for dc in range(8):
                def t_tp(tt, dc=dc, ln_ve=ln_ve, ev=evdc[0]):
                    if dc == 0:
                        tt.wait_ge(s_ve.h, ln_ve)
                        tt.wait_ge(s_wl.h, winit)
                    tt.wait_ge(s_ve.h, ev)
                    s_pe.inc(tt.transpose(ps_tp[:], xnb[:, dc * 128:(dc + 1) * 128], idn[:]))
                block.tensor(t_tp)
                pe_now = s_pe.v

                def v_ev(v, j=j, par=par, dc=dc, pe_now=pe_now, ss=stg_st[par]):
                    v.wait_ge(s_pe.h, pe_now)
                    if dc == 0 and ss:
                        v.wait_ge(s_st.h, ss)
                    s_ve.inc(v.tensor_copy(stg[:, par, dc, :], ps_tp[:]))
                block.vector(v_ev)
                evdc[0] = s_ve.v
            tp_done[j] = s_pe.v
            sd = s_ve.v

            def g_st(g, j=j, par=par, sd=sd):
                g.wait_ge(s_ve.h, sd)
                ins = g.dma_start(out=xo_r[:, :, 1 + j * 128:1 + (j + 1) * 128], in_=stg[:, par])
                s_st.inc(ins, 16)
            block.gpsimd(g_st)
            stg_st[par] = s_st.v

        def v_z(v):
            s_ve.inc(v.memset(cols[:], 0.0))
        block.vector(v_z)
        zv = s_ve.v

        def g_z(g):
            g.wait_ge(s_ve.h, zv)
            ins = g.dma_start(out=xo_r[:, :, 0:1], in_=cols[:, 0:8].unsqueeze(2),
                              allow_slow_non_contiguous=True)
            s_st.inc(ins, 16)
        block.gpsimd(g_z)

        def g_end(g):
            g.wait_ge(s_st.h, s_st.v)
            g.wait_ge(s_ld.h, s_ld.v)
        block.gpsimd(g_end)
    return nc



def build_l1c():
    nc = bass.Bass("TRN2", target_bir_lowering=False, detect_race_conditions=False)
    tn = {}
    def inp(name, shape, dt=F32):
        tn[name] = nc.dram_tensor(name, shape, dt, kind="ExternalInput").ap()
    inp("xn1_ext", [D, 4224], BF16)
    inp("wtw", [D, D]); inp("wtg", [D, H1]); inp("wtb", [D, H1]); inp("wta", [D, H1])
    inp("alog1", [H1, 1]); inp("dtb1", [H1, 1])
    tn["vb_tm"] = nc.dram_tensor("vb_tm", [T, D], BF16, kind="ExternalOutput").ap()
    tn["dec_fm"] = nc.dram_tensor("dec_fm", [H1, T], F32, kind="ExternalOutput").ap()
    tn["dec_tm"] = nc.dram_tensor("dec_tm", [T, H1], F32, kind="ExternalOutput").ap()
    tn["dwf_tm"] = nc.dram_tensor("dwf_tm", [T, H1], BF16, kind="ExternalOutput").ap()
    tn["gate_tm"] = nc.dram_tensor("gate_tm", [T, H1], F32, kind="ExternalOutput").ap()
    tn["S1_loc"] = nc.dram_tensor("S1_loc", [64, 1024], F32, kind="ExternalOutput").ap()
    tn["D1_loc"] = nc.dram_tensor("D1_loc", [H1, 1], F32, kind="ExternalOutput").ap()
    tn["id128"] = nc.inline_tensor(np.eye(128, dtype=BF), name="id128d").ap()
    d0 = np.ones((H1, 512), np.float32)
    d0[:, 0::CH] = 0.0
    tn["d0pat1"] = nc.inline_tensor(d0, name="d0pat1").ap()
    tn["ones1"] = nc.inline_tensor(np.ones((1, 128), dtype=BF), name="ones1d").ap()
    tn["ecl1_d"] = nc.dram_tensor("ecl1_d", [H1, NCHUNK], F32, kind="Internal").ap()
    emit_l1c(nc, tn)
    return nc


def emit_l1c(nc, t):
    s_ld = Sem(nc, "d_ld")
    s_tb = Sem(nc, "d_tb")
    s_wl = Sem(nc, "d_wl")
    s_pe = Sem(nc, "d_pe")
    s_ve = Sem(nc, "d_ve")
    s_ac = Sem(nc, "d_ac")
    s_st = Sem(nc, "d_st")
    import contextlib
    with contextlib.ExitStack() as _stk:
        _pfx = f"s{_NAME_SEQ[0]}_"
        _NAME_SEQ[0] += 1
        def _sb(nm, *a):
            return _stk.enter_context(nc.sbuf_tensor(_pfx + nm, *a))
        def _ps(nm, *a):
            return _stk.enter_context(nc.psum_tensor(_pfx + nm, *a))
        ww_sb = _sb("ww_sb", [128, 8, D], BF16)
        wg_sb = _sb("wg_sb", [128, 8, H1], BF16)
        wb_sb = _sb("wb_sb", [128, 8, H1], BF16)
        wa_sb = _sb("wa_sb", [128, 8, H1], BF16)
        idn = _sb("idn", [128, 128], BF16)
        ones1 = _sb("ones1", [1, 128], BF16)
        d0p = _sb("d0p", [H1, 512], F32)
        alog = _sb("alog", [H1, 1], F32)
        dtb = _sb("dtb", [H1, 1], F32)
        nega = _sb("nega", [H1, 1], F32)
        xnt = _sb("xnt", [128, 2, 8, 513], BF16)
        spa = _sb("spa", [H1, 512], F32)
        spb = _sb("spb", [H1, 512], F32)
        decf = _sb("decf", [H1, 512], F32)
        dh_hi = _sb("dh_hi", [H1, 512], BF16)
        dh_lo = _sb("dh_lo", [H1, 512], BF16)
        dwff = _sb("dwff", [H1, 512], F32)
        dwfb = _sb("dwfb", [H1, 512], BF16)
        dcl1 = _sb("dcl1", [H1, NCHUNK], F32)
        ecl1 = _sb("ecl1", [H1, 8], F32)
        e1rows = _sb("e1rows", [1, 128], F32)
        e1rows_bf = _sb("e1rows_bf", [1, 128], BF16)
        e1full = _sb("e1full", [64, H1, 8], F32)
        his = _sb("his", [128, H1], F32)
        dtm = _sb("dtm", [128, H1], F32)
        gtm = _sb("gtm", [128, H1], F32)
        btm = _sb("btm", [128, H1], F32)
        dwtm = _sb("dwtm", [128, H1], BF16)
        vbt = _sb("vbt", [128, D], BF16)
        wkt = _sb("wkt", [128, D], BF16)
        vbc = _sb("vbc", [64, D], BF16)
        dwfc = _sb("dwfc", [64, H1], BF16)
        wkp = _sb("wkp", [64, D], BF16)
        S1 = _sb("S1", [64, 1024], F32)
        cols = _sb("cols", [H1, 4], F32)
        ps_a = _ps("ps_a", [128, D], F32)
        ps_b = _ps("ps_b", [128, 1024], F32)
        ps_t = _ps("ps_t", [128, 128], BF16)
        block = _stk.enter_context(nc.Block())

        xn_r = t["xn1_ext"].rearrange("(c p) tt -> p c tt", p=128)

        def g_init(g):
            for o_, i_ in [(ww_sb[:], t["wtw"].rearrange("(c p) e -> p c e", p=128)),
                           (wg_sb[:], t["wtg"].rearrange("(c p) e -> p c e", p=128)),
                           (wb_sb[:], t["wtb"].rearrange("(c p) e -> p c e", p=128)),
                           (wa_sb[:], t["wta"].rearrange("(c p) e -> p c e", p=128)),
                           (idn[:], t["id128"]), (ones1[:], t["ones1"]),
                           (d0p[:], t["d0pat1"]), (alog[:], t["alog1"]),
                           (dtb[:], t["dtb1"])]:
                g.dma_start(out=o_, in_=i_).then_inc(s_wl.h, 16)
        block.gpsimd(g_init)
        s_wl.v += 16 * 9
        winit = s_wl.v

        def a_neg(a):
            a.wait_ge(s_wl.h, winit)
            s_ac.inc(a.activation(nega[:], alog[:], AF.Exp))
        block.scalar(a_neg)
        neg_ac = s_ac.v

        def v_neg(v):
            v.wait_ge(s_ac.h, neg_ac)
            sv = s_ve.inc(v.tensor_scalar(nega[:], nega[:], -1.0, None, ALU.mult))
            v.wait_ge(s_ve.h, sv)
            s_ve.inc(v.memset(S1[:], 0.0))
        block.vector(v_neg)
        neg_ve = s_ve.v

        alpha_ac = [0]
        e1_pe = [0]
        e1_ve = [0]
        ecl_wl = [0]
        eclbf_ve = [0]
        scan_ve = [neg_ve]
        vb_st = [0]
        sp_used = [0]
        dec_used = [0]
        dec_st = [0]
        dtm_st = [0]
        gtm_st = [0]
        xn_pe = [0, 0]
        psb_ve = [0]
        wkp_pe = [0]
        wkt_ve = [0]
        hi_ready = [0]
        dwf_ready = [0]
        tp_pe_all = [0]
        tp_last_ve = [0]
        vb_ac = [0]
        gb_ac_t = [0]

        for g in range(8):
            par = g % 2

            def sy_x(sy, g=g, par=par, xp=xn_pe[par]):
                if xp:
                    sy.wait_ge(s_pe.h, xp)
                sy.dma_start(out=xnt[:, par],
                             in_=xn_r[:, :, g * 512:g * 512 + 513]).then_inc(s_ld.h, 16)
            block.sync(sy_x)
            s_ld.v += 16
            xl = s_ld.v

            def t_al(tt, par=par, xl=xl, au=alpha_ac[0], vv=vb_ac[0]):
                tt.wait_ge(s_ld.h, xl)
                tt.wait_ge(s_wl.h, winit)
                if au:
                    tt.wait_ge(s_ac.h, au)
                if vv:
                    tt.wait_ge(s_ve.h, vv)
                for dc in range(8):
                    ins = tt.matmul(ps_a[0:H1, 0:512], wa_sb[:, dc, :],
                                    xnt[:, par, dc, 1:513], start=(dc == 0), stop=(dc == 7))
                s_pe.inc(ins)
            block.tensor(t_al)
            al_pe = s_pe.v

            def a_sp(a, al_pe=al_pe, su=sp_used[0]):
                a.wait_ge(s_pe.h, al_pe)
                a.wait_ge(s_ve.h, neg_ve)
                if su:
                    a.wait_ge(s_ve.h, su)
                a.activation(spa[:], ps_a[0:H1, 0:512], AF.Abs, bias=dtb[:, 0:1])
                a.activation(spa[:], spa[:], AF.Exp, scale=-1.0)
                a.activation(spa[:], spa[:], AF.Ln, bias=1.0)
                s_ac.inc(a.activation(spb[:], ps_a[0:H1, 0:512], AF.Relu, bias=dtb[:, 0:1]))
            block.scalar(a_sp)
            alpha_ac[0] = s_ac.v
            sp_ac = s_ac.v

            def v_dec(v, g=g, sp_ac=sp_ac, du=dec_used[0], ds=dec_st[0], tp=tp_pe_all[0]):
                v.wait_ge(s_ac.h, sp_ac)
                if du:
                    v.wait_ge(s_ac.h, du)
                if ds:
                    v.wait_ge(s_st.h, ds)
                if tp:
                    v.wait_ge(s_pe.h, tp)
                v.tensor_tensor(spb[:], spb[:], spa[:], ALU.add)
                v.tensor_scalar(spa[:], spb[:], nega[:, 0:1], None, ALU.mult)
                sv = s_ve.inc(v.tensor_tensor_scan(decf[:], d0p[:], spa[:], 0.0,
                                                   ALU.mult, ALU.add))
                v.wait_ge(s_ve.h, sv)
                v.tensor_copy(dh_hi[:], decf[:])
                sv = s_ve.inc(v.tensor_tensor(dh_lo[:], decf[:], dh_hi[:], ALU.subtract))
                v.wait_ge(s_ve.h, sv)
                s_ve.inc(v.tensor_copy(dcl1[:, g * 8:(g + 1) * 8],
                                       decf[:].rearrange("h (c l) -> h c l", l=CH)[:, :, CH - 1]))
            block.vector(v_dec)
            dec_ve = s_ve.v
            hi_ready[0] = s_ve.v

            def g_dec(g_, g2=g, dec_ve=dec_ve):
                g_.wait_ge(s_ve.h, dec_ve)
                ins = g_.dma_start(out=t["dec_fm"][:, g2 * 512:(g2 + 1) * 512], in_=decf[:])
                s_st.inc(ins, 16)
            block.gpsimd(g_dec)
            dec_st[0] = s_st.v

            def a_dwf(a, dec_ve=dec_ve, dw=dwf_ready[0]):
                a.wait_ge(s_ve.h, dec_ve)
                if dw:
                    a.wait_ge(s_ve.h, dw)
                for c in range(8):
                    cl = decf[:, c * 64 + 63:c * 64 + 64]
                    ins = a.activation(dwff[:, c * 64:(c + 1) * 64], decf[:, c * 64:(c + 1) * 64],
                                       AF.Exp, scale=-1.0, bias=cl)
                s_ac.inc(ins)
            block.scalar(a_dwf)
            dwf_ac = s_ac.v
            dec_used[0] = s_ac.v

            def v_dwf(v, dwf_ac=dwf_ac, tp=tp_pe_all[0]):
                v.wait_ge(s_ac.h, dwf_ac)
                if tp:
                    v.wait_ge(s_pe.h, tp)
                s_ve.inc(v.tensor_copy(dwfb[:], dwff[:]))
            block.vector(v_dwf)
            dwf_ready[0] = s_ve.v
            sp_used[0] = s_ve.v

            def v_e1(v, g=g, dec_ve=dec_ve, ew=ecl_wl[0]):
                v.wait_ge(s_ve.h, dec_ve)
                if ew:
                    v.wait_ge(s_wl.h, ew)
                sv = s_ve.inc(v.tensor_copy(ecl1[:, 0:8],
                                            decf[:].rearrange("h (c l) -> h c l", l=CH)[:, :, CH - 1]))
                v.wait_ge(s_ve.h, sv)
            block.vector(v_e1)
            e1v = s_ve.v

            def a_e1(a, e1v=e1v):
                a.wait_ge(s_ve.h, e1v)
                s_ac.inc(a.activation(ecl1[:, 0:8], ecl1[:, 0:8], AF.Exp))
            block.scalar(a_e1)
            e1a = s_ac.v

            def g_e1(g_, g2=g, e1a=e1a, ebv=eclbf_ve[0]):
                g_.wait_ge(s_ac.h, e1a)
                if ebv:
                    g_.wait_ge(s_ve.h, ebv)
                base = s_wl.v
                g_.dma_start(out=t["ecl1_d"][:, g2 * 8:(g2 + 1) * 8], in_=ecl1[:, 0:8]).then_inc(s_wl.h, 16)
                g_.wait_ge(s_wl.h, base + 16)
                g_.dma_start(out=e1rows[0:1, 0:128].rearrange("one (a c) -> one a c", c=8),
                             in_=t["ecl1_d"][:, g2 * 8:(g2 + 1) * 8].unsqueeze(0)).then_inc(s_wl.h, 16)
            block.gpsimd(g_e1)
            s_wl.v += 32
            ecl_wl[0] = s_wl.v

            def v_eb(v, wv=s_wl.v):
                v.wait_ge(s_wl.h, wv)
                sv = s_ve.inc(v.tensor_copy(e1rows_bf[:], e1rows[:]))
                v.wait_ge(s_ve.h, sv)
            block.vector(v_eb)
            eclbf_ve[0] = s_ve.v

            def t_e1(tt, ev=eclbf_ve[0], pv=psb_ve[0]):
                tt.wait_ge(s_ve.h, ev)
                if pv:
                    tt.wait_ge(s_ve.h, pv)
                s_pe.inc(tt.matmul(ps_b[0:64, 0:128], ones1[0:1, 0:64], e1rows_bf[0:1, 0:128],
                                   start=True, stop=True))
            block.tensor(t_e1)
            e1_pe[0] = s_pe.v

            def v_e1c(v, pv=e1_pe[0], sc=scan_ve[0]):
                v.wait_ge(s_pe.h, pv)
                v.wait_ge(s_ve.h, sc)
                s_ve.inc(v.tensor_copy(e1full[:].rearrange("p a c -> p (a c)"), ps_b[0:64, 0:128]))
            block.vector(v_e1c)
            e1_ve[0] = s_ve.v
            psb_ve[0] = s_ve.v

            for sub in range(4):
                tcol = g * 512 + sub * 128

                def t_gb(tt, par=par, sub=sub, xl=xl, vv=vb_ac[0], ga=gb_ac_t[0]):
                    tt.wait_ge(s_ld.h, xl)
                    if vv:
                        tt.wait_ge(s_ve.h, vv)
                    if ga:
                        tt.wait_ge(s_ac.h, ga)
                    for dc in range(8):
                        tt.matmul(ps_a[:, 0:H1], xnt[:, par, dc, 1 + sub * 128:1 + (sub + 1) * 128],
                                  wg_sb[:, dc, :], start=(dc == 0), stop=(dc == 7))
                    for dc in range(8):
                        ins = tt.matmul(ps_a[:, 64:64 + H1],
                                        xnt[:, par, dc, 1 + sub * 128:1 + (sub + 1) * 128],
                                        wb_sb[:, dc, :], start=(dc == 0), stop=(dc == 7))
                    s_pe.inc(ins)
                block.tensor(t_gb)
                gb_pe = s_pe.v

                def a_gb(a, gb_pe=gb_pe, gs=gtm_st[0]):
                    a.wait_ge(s_pe.h, gb_pe)
                    if gs:
                        a.wait_ge(s_st.h, gs)
                    a.activation(gtm[:], ps_a[:, 0:H1], AF.Sigmoid)
                    s_ac.inc(a.activation(btm[:], ps_a[:, 64:64 + H1], AF.Sigmoid))
                block.scalar(a_gb)
                gb_ac = s_ac.v
                gb_ac_t[0] = s_ac.v

                def g_gt(g_, tcol=tcol, gb_ac=gb_ac):
                    g_.wait_ge(s_ac.h, gb_ac)
                    ins = g_.dma_start(out=t["gate_tm"][tcol:tcol + 128, :], in_=gtm[:])
                    s_st.inc(ins, 16)
                block.gpsimd(g_gt)
                gtm_st[0] = s_st.v

                def t_tp1(tt, sub=sub, hr=hi_ready[0], dw=dwf_ready[0], tv=tp_last_ve[0]):
                    tt.wait_ge(s_ve.h, max(hr, dw))
                    if tv:
                        tt.wait_ge(s_ve.h, tv)
                    s_pe.inc(tt.transpose(ps_t[:, 0:H1], dh_hi[:, sub * 128:(sub + 1) * 128],
                                          idn[0:H1, 0:H1]))
                block.tensor(t_tp1)
                p1 = s_pe.v

                def v_hi(v, p1=p1):
                    v.wait_ge(s_pe.h, p1)
                    s_ve.inc(v.tensor_copy(his[:, 0:H1], ps_t[:, 0:H1]))
                block.vector(v_hi)
                hv = s_ve.v

                def t_tp2(tt, sub=sub, hv=hv):
                    tt.wait_ge(s_ve.h, hv)
                    s_pe.inc(tt.transpose(ps_t[:, 0:H1], dh_lo[:, sub * 128:(sub + 1) * 128],
                                          idn[0:H1, 0:H1]))
                block.tensor(t_tp2)
                p2 = s_pe.v

                def v_dtm(v, p2=p2, ds=dtm_st[0]):
                    v.wait_ge(s_pe.h, p2)
                    if ds:
                        v.wait_ge(s_st.h, ds)
                    s_ve.inc(v.tensor_tensor(dtm[:, 0:H1], his[:, 0:H1], ps_t[:, 0:H1], ALU.add))
                block.vector(v_dtm)
                dv = s_ve.v

                def t_tp3(tt, sub=sub, dv=dv):
                    tt.wait_ge(s_ve.h, dv)
                    s_pe.inc(tt.transpose(ps_t[:, 0:H1], dwfb[:, sub * 128:(sub + 1) * 128],
                                          idn[0:H1, 0:H1]))
                block.tensor(t_tp3)
                p3 = s_pe.v
                tp_pe_all[0] = s_pe.v

                def v_dwtm(v, p3=p3):
                    v.wait_ge(s_pe.h, p3)
                    s_ve.inc(v.tensor_copy(dwtm[:, 0:H1], ps_t[:, 0:H1]))
                block.vector(v_dwtm)
                dwv = s_ve.v
                tp_last_ve[0] = s_ve.v

                def g_dtm(g_, tcol=tcol, dwv=dwv):
                    g_.wait_ge(s_ve.h, dwv)
                    g_.dma_start(out=t["dec_tm"][tcol:tcol + 128, :], in_=dtm[:, 0:H1]).then_inc(s_st.h, 16)
                    g_.dma_start(out=t["dwf_tm"][tcol:tcol + 128, :], in_=dwtm[:, 0:H1]).then_inc(s_st.h, 16)
                block.gpsimd(g_dtm)
                s_st.v += 32
                dtm_st[0] = s_st.v

                def t_v(tt, par=par, sub=sub, xl=xl, va=vb_ac[0], ga=gb_ac):
                    tt.wait_ge(s_ac.h, ga)
                    if va:
                        tt.wait_ge(s_ve.h, va)
                    for nn in range(2):
                        for dc in range(8):
                            ins = tt.matmul(ps_a[:, nn * 512:(nn + 1) * 512],
                                            xnt[:, par, dc, 1 + sub * 128:1 + (sub + 1) * 128],
                                            ww_sb[:, dc, nn * 512:(nn + 1) * 512],
                                            start=(dc == 0), stop=(dc == 7))
                    s_pe.inc(ins)
                block.tensor(t_v)
                v_pe = s_pe.v
                xn_pe[par] = s_pe.v

                def v_vb(v, v_pe=v_pe, gb_ac=gb_ac, vs=vb_st[0]):
                    v.wait_ge(s_pe.h, v_pe)
                    v.wait_ge(s_ac.h, gb_ac)
                    if vs:
                        v.wait_ge(s_st.h, vs)
                    s_ve.inc(v.tensor_tensor(vbt[:].rearrange("p (h q) -> p h q", q=DD),
                                             ps_a[:].rearrange("p (h q) -> p h q", q=DD),
                                             _b(btm[:, 0:H1].unsqueeze(2), [128, H1, DD]), ALU.mult))
                block.vector(v_vb)
                vb_ac[0] = s_ve.v
                vbv = s_ve.v

                def g_vb(g_, tcol=tcol, vbv=vbv):
                    g_.wait_ge(s_ve.h, vbv)
                    ins = g_.dma_start(out=t["vb_tm"][tcol:tcol + 128, :], in_=vbt[:])
                    s_st.inc(ins, 16)
                block.gpsimd(g_vb)
                vb_st[0] = s_st.v
                vbst_now = s_st.v

                for cc in range(2):
                    coff = cc * 64
                    ci = sub * 2 + cc
                    tch = tcol + coff

                    def sy_wk(sy, tch=tch, wp=wkp_pe[0], wv=wkt_ve[0], vb_=vbst_now,
                              dws=dtm_st[0]):
                        if wv:
                            sy.wait_ge(s_ve.h, wv)
                        sy.dma_start_transpose(wkt[:], t["xn1_ext"][:, tch:tch + 128]).then_inc(s_tb.h, 16)
                        if wp:
                            sy.wait_ge(s_pe.h, wp)
                        sy.wait_ge(s_st.h, max(vb_, dws))
                        sy.dma_start(out=vbc[:], in_=t["vb_tm"][tch:tch + 64, :]).then_inc(s_tb.h, 16)
                        sy.dma_start(out=dwfc[:], in_=t["dwf_tm"][tch:tch + 64, :]).then_inc(s_tb.h, 16)
                    block.sync(sy_wk)
                    s_tb.v += 48
                    wkl = s_tb.v

                    def v_wkp(v, wkl=wkl, wp=wkp_pe[0]):
                        v.wait_ge(s_tb.h, wkl)
                        if wp:
                            v.wait_ge(s_pe.h, wp)
                        s_ve.inc(v.tensor_tensor(wkp[:].rearrange("p (h q) -> p h q", q=DD),
                                                 wkt[0:64, :].rearrange("p (h q) -> p h q", q=DD),
                                                 _b(dwfc[:].unsqueeze(2), [64, H1, DD]), ALU.mult))
                    block.vector(v_wkp)
                    wkp_ve = s_ve.v
                    wkt_ve[0] = s_ve.v

                    def t_ds(tt, wkp_ve=wkp_ve, pv=psb_ve[0]):
                        tt.wait_ge(s_ve.h, wkp_ve)
                        tt.wait_ge(s_ve.h, pv)
                        for h in range(H1):
                            ins = tt.matmul(ps_b[0:64, h * DD:(h + 1) * DD],
                                            wkp[:, h * DD:(h + 1) * DD],
                                            vbc[:, h * DD:(h + 1) * DD],
                                            start=True, stop=True)
                        s_pe.inc(ins)
                    block.tensor(t_ds)
                    ds_pe = s_pe.v
                    wkp_pe[0] = s_pe.v

                    def v_sc(v, ci=ci, ds_pe=ds_pe, ev=e1_ve[0]):
                        v.wait_ge(s_pe.h, ds_pe)
                        v.wait_ge(s_ve.h, ev)
                        v.tensor_tensor(S1[:].rearrange("p (a q) -> p a q", q=DD),
                                        S1[:].rearrange("p (a q) -> p a q", q=DD),
                                        _b(e1full[:, :, ci:ci + 1], [64, H1, DD]), ALU.mult)
                        s_ve.inc(v.tensor_tensor(S1[:], S1[:], ps_b[0:64, :], ALU.add))
                    block.vector(v_sc)
                    scan_ve[0] = s_ve.v
                    psb_ve[0] = s_ve.v

        def v_d1(v, sc=scan_ve[0]):
            v.wait_ge(s_ve.h, sc)
            sv = s_ve.inc(v.reduce_sum(cols[:, 0:1], dcl1[:], axis=X))
            v.wait_ge(s_ve.h, sv)
        block.vector(v_d1)
        d1v = s_ve.v

        def g_f(g_, d1v=d1v):
            g_.wait_ge(s_ve.h, d1v)
            g_.dma_start(out=t["D1_loc"], in_=cols[:, 0:1]).then_inc(s_st.h, 16)
            g_.dma_start(out=t["S1_loc"], in_=S1[:]).then_inc(s_st.h, 16)
        block.gpsimd(g_f)
        s_st.v += 32

        def g_end(g_):
            g_.wait_ge(s_st.h, s_st.v)
            g_.wait_ge(s_tb.h, s_tb.v)
            g_.wait_ge(s_ld.h, s_ld.v)
            g_.wait_ge(s_wl.h, s_wl.v)
        block.gpsimd(g_end)
    return nc



def build_l1f():
    nc = bass.Bass("TRN2", target_bir_lowering=False, detect_race_conditions=False)
    tn = {}
    def inp(name, shape, dt=F32):
        tn[name] = nc.dram_tensor(name, shape, dt, kind="ExternalInput").ap()
    inp("xn1_ext", [D, 4224], BF16)
    inp("wfcT", [D, 4096]); inp("bfc_fm", [128, 32])
    inp("afcT", [D, H1]); inp("bfcT", [H1, 4096])
    inp("wprT", [4096, D]); inp("bpr_rep", [128, D])
    inp("aprT", [4096, H1]); inp("bprT", [H1, D])
    tn["g1_out"] = nc.dram_tensor("g1_out", [T, D], BF16, kind="ExternalOutput").ap()
    tn["wqf_d"] = nc.dram_tensor("wqf_d", [D, 4096], BF16, kind="Internal").ap()
    tn["colb_d"] = nc.dram_tensor("colb_d", [128, 1], F32, kind="Internal").ap()
    tn["ones1"] = nc.inline_tensor(np.ones((1, 128), dtype=BF), name="ones1f2").ap()
    tn["ones1f"] = nc.inline_tensor(np.ones((1, 128), np.float32), name="ones1f3").ap()
    emit_l1f(nc, tn)
    return nc


def emit_l1f(nc, t):
    FC = 32
    s_ld = Sem(nc, "f_ld")
    s_wl = Sem(nc, "f_wl")
    s_pe = Sem(nc, "f_pe")
    s_ve = Sem(nc, "f_ve")
    s_ac = Sem(nc, "f_ac")
    s_st = Sem(nc, "f_st")
    import contextlib
    with contextlib.ExitStack() as _stk:
        _pfx = f"s{_NAME_SEQ[0]}_"
        _NAME_SEQ[0] += 1
        def _sb(nm, *a):
            return _stk.enter_context(nc.sbuf_tensor(_pfx + nm, *a))
        def _ps(nm, *a):
            return _stk.enter_context(nc.psum_tensor(_pfx + nm, *a))
        ones1 = _sb("ones1", [1, 128], BF16)
        ones1f = _sb("ones1f", [1, 128], F32)
        wtile = _sb("wtile", [128, 4096], F32)
        wqtile = _sb("wqtile", [128, 4096], BF16)
        wqp = _sb("wqp", [128, 32, 1024], BF16)
        wfc_c = _sb("wfc_c", [128, 2, 8, 128], BF16)
        afc = _sb("afc", [128, 8, H1], BF16)
        bfcT_sb = _sb("bfcT_sb", [H1, 4096], BF16)
        aprT_sb = _sb("aprT_sb", [128, 32, H1], BF16)
        bprT_sb = _sb("bprT_sb", [H1, D], BF16)
        bfc_sb = _sb("bfc_sb", [128, 32], F32)
        bpr_sb = _sb("bpr_sb", [128, D], F32)
        xnt = _sb("xnt", [128, 2, 8, 512], BF16)
        h_sb = _sb("h_sb", [128, FC, 512], BF16)
        lor1 = _sb("lor1", [H1, 512], BF16)
        lor2 = _sb("lor2", [H1, 512], BF16)
        o2f = _sb("o2f", [128, D], F32)
        g1t = _sb("g1t", [128, D], BF16)
        mcol = _sb("mcol", [128, 4], F32)
        mrow = _sb("mrow", [1, 128], F32)
        ps_h = _ps("ps_h", [128, 512], F32)
        ps_l = _ps("ps_l", [H1, 512], F32)
        ps_o = _ps("ps_o", [128, 2, D], F32)
        block = _stk.enter_context(nc.Block())

        xn_r = t["xn1_ext"].rearrange("(c p) tt -> p c tt", p=128)

        def g_init(g):
            for o_, i_ in [(ones1[:], t["ones1"]), (ones1f[:], t["ones1f"]),
                           (afc[:], t["afcT"].rearrange("(c p) e -> p c e", p=128)),
                           (bfcT_sb[:], t["bfcT"]),
                           (aprT_sb[:], t["aprT"].rearrange("(c p) e -> p c e", p=128)),
                           (bprT_sb[:], t["bprT"]), (bfc_sb[:], t["bfc_fm"]),
                           (bpr_sb[:], t["bpr_rep"])]:
                g.dma_start(out=o_, in_=i_).then_inc(s_wl.h, 16)
        block.gpsimd(g_init)
        s_wl.v += 16 * 8
        winit = s_wl.v

        def emit_quant(wsrc, n_tiles, wq_dst, resident_dst):
            F = wsrc.shape[1]
            def v_m0(v, mv=s_ve.v):
                v.wait_ge(s_wl.h, winit)
                if mv:
                    v.wait_ge(s_ve.h, mv)
                s_ve.inc(v.memset(mcol[:], 0.0))
            block.vector(v_m0)
            for i in range(n_tiles):
                def sy_w(sy, i=i, wv=s_ve.v):
                    sy.wait_ge(s_ve.h, wv)
                    sy.dma_start(out=wtile[:, 0:F], in_=wsrc[i * 128:(i + 1) * 128, :]).then_inc(s_ld.h, 16)
                block.sync(sy_w)
                s_ld.v += 16
                wl = s_ld.v

                def v_m(v, i=i, wl=wl):
                    v.wait_ge(s_ld.h, wl)
                    sv = s_ve.inc(v.reduce_max(mcol[:, 1:2], wtile[:, 0:F], axis=X,
                                               apply_absolute_value=True))
                    v.wait_ge(s_ve.h, sv)
                    sv = s_ve.inc(v.tensor_tensor(mcol[:, 0:1], mcol[:, 0:1], mcol[:, 1:2], ALU.max))
                    v.wait_ge(s_ve.h, sv)
                block.vector(v_m)
            def g_mb(g_, mv=s_ve.v):
                g_.wait_ge(s_ve.h, mv)
                base = s_wl.v
                g_.dma_start(out=t["colb_d"], in_=mcol[:, 0:1]).then_inc(s_wl.h, 16)
                g_.wait_ge(s_wl.h, base + 16)
                g_.dma_start(out=mrow[:], in_=t["colb_d"].rearrange("p one -> (p one)").unsqueeze(0)
                             ).then_inc(s_wl.h, 16)
            block.gpsimd(g_mb)
            s_wl.v += 32
            mb_wl = s_wl.v

            def v_gm(v, mb_wl=mb_wl):
                v.wait_ge(s_wl.h, mb_wl)
                sv = s_ve.inc(v.reduce_max(mrow[0:1, 0:1], mrow[:], axis=X))
                v.wait_ge(s_ve.h, sv)
                sv = s_ve.inc(v.tensor_copy(mrow[0:1, 1:3], mrow[0:1, 0:2]))
                v.wait_ge(s_ve.h, sv)
            block.vector(v_gm)
            gm_ve = s_ve.v

            def t_mb(tt, gm_ve=gm_ve, hp=s_pe.v):
                tt.wait_ge(s_ve.h, gm_ve)
                s_pe.inc(tt.matmul(ps_l[0:128, 0:1] if False else ps_h[:, 0:1],
                                   ones1f[:], mrow[0:1, 0:1], start=True, stop=True))
            block.tensor(t_mb)
            mb_pe = s_pe.v

            def v_sc2(v, mb_pe=mb_pe):
                v.wait_ge(s_pe.h, mb_pe)
                sv = s_ve.inc(v.tensor_copy(mcol[:, 0:1], ps_h[:, 0:1]))
                v.wait_ge(s_ve.h, sv)
                sv = s_ve.inc(v.reciprocal(mcol[:, 1:2], mcol[:, 0:1]))
                v.wait_ge(s_ve.h, sv)
            block.vector(v_sc2)
            sc_ve = s_ve.v

            def a_sc(a, sc_ve=sc_ve):
                a.wait_ge(s_ve.h, sc_ve)
                a.activation(mcol[:, 2:3], mcol[:, 0:1], AF.Copy, scale=1.0 / 127.0)
                s_ac.inc(a.activation(mcol[:, 3:4], mcol[:, 1:2], AF.Copy, scale=127.0))
            block.scalar(a_sc)
            sc_ac = s_ac.v

            for i in range(n_tiles):
                def sy_w2(sy, i=i, qv=s_ve.v):
                    sy.wait_ge(s_ve.h, qv)
                    sy.dma_start(out=wtile[:, 0:F], in_=wsrc[i * 128:(i + 1) * 128, :]).then_inc(s_ld.h, 16)
                block.sync(sy_w2)
                s_ld.v += 16
                wl = s_ld.v

                def v_q(v, i=i, wl=wl, sc_ac=sc_ac, qs=s_st.v):
                    v.wait_ge(s_ld.h, wl)
                    v.wait_ge(s_ac.h, sc_ac)
                    if qs:
                        v.wait_ge(s_st.h, qs)
                    v.tensor_scalar(wtile[:, 0:F], wtile[:, 0:F], mcol[:, 3:4], MAGIC,
                                    ALU.mult, ALU.add)
                    s_ve.inc(v.tensor_scalar(wqtile[:, 0:F], wtile[:, 0:F], MAGIC, mcol[:, 2:3],
                                             ALU.subtract, ALU.mult))
                block.vector(v_q)
                qv = s_ve.v

                if wq_dst is not None:
                    def g_q(g_, i=i, qv=qv):
                        g_.wait_ge(s_ve.h, qv)
                        ins = g_.dma_start(out=wq_dst[i * 128:(i + 1) * 128, :], in_=wqtile[:, 0:F])
                        s_st.inc(ins, 16)
                    block.gpsimd(g_q)
                else:
                    def v_qc(v, i=i, qv=qv):
                        s_ve.inc(v.tensor_copy(resident_dst[:, i, :], wqtile[:, 0:F]))
                    block.vector(v_qc)

        emit_quant(t["wfcT"], 8, t["wqf_d"], None)
        emit_quant(t["wprT"], 32, None, wqp)
        q_ve = s_ve.v
        q_st = s_st.v

        xn_pe = [0, 0]
        hev_ac = [0]
        l1_ve = [0]
        l2_ve = [0]
        wfc_pe = [0, 0]
        yo_ve = [0, 0]
        g1_st = [0]
        hread_pe = [0]

        for ti in range(8):
            par = ti % 2

            def sy_x(sy, ti=ti, par=par, xp=xn_pe[par]):
                if xp:
                    sy.wait_ge(s_pe.h, xp)
                sy.dma_start(out=xnt[:, par],
                             in_=xn_r[:, :, 1 + ti * 512:1 + (ti + 1) * 512]).then_inc(s_ld.h, 16)
            block.sync(sy_x)
            s_ld.v += 16
            xl = s_ld.v

            def t_l1(tt, par=par, xl=xl, lv=l1_ve[0], qv=q_ve, qs=q_st):
                tt.wait_ge(s_ld.h, xl)
                tt.wait_ge(s_ve.h, max(qv, lv))
                tt.wait_ge(s_st.h, qs)
                for dc in range(8):
                    ins = tt.matmul(ps_l[:, 0:512], afc[:, dc, :], xnt[:, par, dc, :],
                                    start=(dc == 0), stop=(dc == 7))
                s_pe.inc(ins)
            block.tensor(t_l1)
            l1_pe = s_pe.v

            def a_l1(a, l1_pe=l1_pe):
                a.wait_ge(s_pe.h, l1_pe)
                s_ac.inc(a.activation(lor1[:], ps_l[:, 0:512], AF.Copy, scale=2.0))
            block.scalar(a_l1)
            l1_ac = s_ac.v
            l1_ve[0] = 0

            for fc in range(FC):
                fpar = fc % 2

                def sy_wf(sy, ti=ti, fc=fc, fpar=fpar, wp=wfc_pe[fpar], qs=q_st):
                    sy.wait_ge(s_st.h, qs)
                    if wp:
                        sy.wait_ge(s_pe.h, wp)
                    sy.dma_start(out=wfc_c[:, fpar],
                                 in_=t["wqf_d"].rearrange("(c p) f -> p c f", p=128)[:, :, fc * 128:(fc + 1) * 128]
                                 ).then_inc(s_ld.h, 16)
                block.sync(sy_wf)
                s_ld.v += 16
                wfl = s_ld.v

                def t_h(tt, fc=fc, par=par, fpar=fpar, wfl=wfl, ha=hev_ac[0], l1_ac=l1_ac):
                    tt.wait_ge(s_ld.h, wfl)
                    if ha:
                        tt.wait_ge(s_ac.h, ha)
                    for dc in range(8):
                        tt.matmul(ps_h[:], wfc_c[:, fpar, dc, :], xnt[:, par, dc, :],
                                  start=(dc == 0), stop=False)
                    tt.wait_ge(s_ac.h, l1_ac)
                    ins = tt.matmul(ps_h[:], bfcT_sb[:, fc * 128:(fc + 1) * 128], lor1[:],
                                    start=False, stop=True)
                    s_pe.inc(ins)
                block.tensor(t_h)
                h_pe = s_pe.v
                wfc_pe[fpar] = s_pe.v

                def a_h(a, fc=fc, h_pe=h_pe, hr=hread_pe[0]):
                    a.wait_ge(s_pe.h, h_pe)
                    if hr:
                        a.wait_ge(s_pe.h, hr)
                    s_ac.inc(a.activation(h_sb[:, fc, :], ps_h[:], AF.Gelu,
                                          bias=bfc_sb[:, fc:fc + 1]))
                block.scalar(a_h)
                hev_ac[0] = s_ac.v
            h_ac = s_ac.v
            xn_pe[par] = s_pe.v

            def t_l2(tt, h_ac=h_ac, lv=l2_ve[0]):
                tt.wait_ge(s_ac.h, h_ac)
                if lv:
                    tt.wait_ge(s_ve.h, lv)
                for fc in range(FC):
                    ins = tt.matmul(ps_l[:, 0:512], aprT_sb[:, fc, :], h_sb[:, fc, :],
                                    start=(fc == 0), stop=(fc == FC - 1))
                s_pe.inc(ins)
            block.tensor(t_l2)
            l2_pe = s_pe.v

            def a_l2(a, l2_pe=l2_pe):
                a.wait_ge(s_pe.h, l2_pe)
                s_ac.inc(a.activation(lor2[:], ps_l[:, 0:512], AF.Copy, scale=2.0))
            block.scalar(a_l2)
            l2_ac = s_ac.v
            l2_ve[0] = 0

            for tq in range(4):
                ypar = tq % 2

                def t_o(tt, tq=tq, ypar=ypar, l2_ac=l2_ac, yv=yo_ve[ypar]):
                    tt.wait_ge(s_ac.h, l2_ac)
                    if yv:
                        tt.wait_ge(s_ve.h, yv)
                    for fc in range(FC):
                        for nn in range(2):
                            tt.matmul(ps_o[:, ypar, nn * 512:(nn + 1) * 512],
                                      h_sb[:, fc, tq * 128:(tq + 1) * 128],
                                      wqp[:, fc, nn * 512:(nn + 1) * 512],
                                      start=(fc == 0), stop=False)
                    for nn in range(2):
                        ins = tt.matmul(ps_o[:, ypar, nn * 512:(nn + 1) * 512],
                                        lor2[:, tq * 128:(tq + 1) * 128],
                                        bprT_sb[:, nn * 512:(nn + 1) * 512],
                                        start=False, stop=True)
                    s_pe.inc(ins)
                block.tensor(t_o)
                o_pe = s_pe.v
                hread_pe[0] = s_pe.v

                def v_o(v, ypar=ypar, o_pe=o_pe, gs=g1_st[0]):
                    v.wait_ge(s_pe.h, o_pe)
                    if gs:
                        v.wait_ge(s_st.h, gs)
                    s_ve.inc(v.tensor_tensor(o2f[:], ps_o[:, ypar], bpr_sb[:], ALU.add))
                block.vector(v_o)
                yo_ve[ypar] = s_ve.v
                ov = s_ve.v

                def a_g1(a, ov=ov):
                    a.wait_ge(s_ve.h, ov)
                    s_ac.inc(a.activation(g1t[:], o2f[:], AF.Silu))
                block.scalar(a_g1)
                g1a = s_ac.v

                def g_g1(g_, ti=ti, tq=tq, g1a=g1a):
                    g_.wait_ge(s_ac.h, g1a)
                    ins = g_.dma_start(out=t["g1_out"][ti * 512 + tq * 128:ti * 512 + (tq + 1) * 128, :],
                                       in_=g1t[:])
                    s_st.inc(ins, 16)
                block.gpsimd(g_g1)
                g1_st[0] = s_st.v

        def g_end(g_):
            g_.wait_ge(s_st.h, s_st.v)
            g_.wait_ge(s_ld.h, s_ld.v)
            g_.wait_ge(s_wl.h, s_wl.v)
        block.gpsimd(g_end)
    return nc



def build_l1d():
    nc = bass.Bass("TRN2", target_bir_lowering=False, detect_race_conditions=False)
    tn = {}
    def inp(name, shape, dt=F32):
        tn[name] = nc.dram_tensor(name, shape, dt, kind="ExternalInput").ap()
    inp("xn1_ext", [D, 4224], BF16)
    inp("vb_tm", [T, D], BF16)
    inp("dec_fm", [H1, T]); inp("dec_tm", [T, H1])
    inp("dwf_tm", [T, H1], BF16)
    inp("gate_tm", [T, H1])
    inp("S1_all", [8, 64, 1024]); inp("D1_tall", [8, H1])
    inp("Mt", [8, 8]); inp("pm_rep1", [H1, 8])
    inp("wtout", [D, D])
    inp("x1_tm", [T, D]); inp("g1_out", [T, D], BF16)
    tn["x2"] = nc.dram_tensor("x2", [T, D], F32, kind="ExternalOutput").ap()
    tn["o_d"] = nc.dram_tensor("o_d", [T, D], BF16, kind="Internal").ap()
    tn["f1_d"] = nc.dram_tensor("f1_d", [H1, 8], F32, kind="Internal").ap()
    tn["ones1"] = nc.inline_tensor(np.ones((1, 128), dtype=BF), name="ones1e").ap()
    tn["ones1f"] = nc.inline_tensor(np.ones((1, 128), np.float32), name="ones1f4").ap()
    tn["mask_st"] = nc.inline_tensor(np.triu(np.ones((64, 64), np.float32), 1), name="mask_st_c").ap()
    emit_l1d(nc, tn)
    return nc


def emit_l1d(nc, t):
    s_ld = Sem(nc, "e_ld")
    s_tb = Sem(nc, "e_tb")
    s_wl = Sem(nc, "e_wl")
    s_pe = Sem(nc, "e_pe")
    s_ve = Sem(nc, "e_ve")
    s_ac = Sem(nc, "e_ac")
    s_st = Sem(nc, "e_st")
    import contextlib
    with contextlib.ExitStack() as _stk:
        _pfx = f"s{_NAME_SEQ[0]}_"
        _NAME_SEQ[0] += 1
        def _sb(nm, *a):
            return _stk.enter_context(nc.sbuf_tensor(_pfx + nm, *a))
        def _ps(nm, *a):
            return _stk.enter_context(nc.psum_tensor(_pfx + nm, *a))
        wo_sb = _sb("wo_sb", [128, 8, D], BF16)
        ones1 = _sb("ones1", [1, 128], BF16)
        ones1f = _sb("ones1f", [1, 128], F32)
        mstr = _sb("mstr", [64, 64], F32)
        pmrep = _sb("pmrep", [H1, 8], F32)
        dtall = _sb("dtall", [8, H1], F32)
        mt_sb = _sb("mt_sb", [8, 8], F32)
        fh_sb = _sb("fh_sb", [H1, 8], F32)
        f1bc = _sb("f1bc", [64, H1, 8], F32)
        S1 = _sb("S1", [64, 1024], F32)
        S1_bf = _sb("S1_bf", [64, 1024], BF16)
        sjt = _sb("sjt", [64, 1024], F32)
        xnf = _sb("xnf", [64, 2, H1, 129], BF16)
        xnf2 = _sb("xnf2", [64, 2, H1, 129], BF16)
        vbc = _sb("vbc", [64, D], BF16)
        wkt = _sb("wkt", [128, D], BF16)
        wkp = _sb("wkp", [64, D], BF16)
        dwfc = _sb("dwfc", [64, H1], BF16)
        decc = _sb("decc", [64, H1], F32)
        edecc = _sb("edecc", [64, H1], F32)
        gatec = _sb("gatec", [64, H1], F32)
        edecf = _sb("edecf", [64, H1], F32)
        edrow = _sb("edrow", [1, 16], F32)
        b2_sb = _sb("b2_sb", [64, 1024], F32)
        d1row = _sb("d1row", [1, 1024], F32)
        intra = _sb("intra", [64, 1024], BF16)
        o_half = _sb("o_half", [64, 1024], F32)
        o_bf = _sb("o_bf", [64, 1024], BF16)
        ofm = _sb("ofm", [128, 8, 512], BF16)
        xn1t = _sb("xn1t", [128, D], BF16)
        x1t = _sb("x1t", [128, D], F32)
        g1t = _sb("g1t", [128, D], BF16)
        x2t = _sb("x2t", [128, D], F32)
        ps_w = _ps("ps_w", [128, 1024], F32)
        ps_b1 = _ps("ps_b1", [128, 1024], F32)
        ps_o1 = _ps("ps_o1", [64, 1024], F32)
        ps_o2 = _ps("ps_o2", [64, 1024], F32)
        block = _stk.enter_context(nc.Block())

        xn_r2 = t["xn1_ext"].rearrange("(h p) tt -> p h tt", p=64)

        def g_init(g):
            for o_, i_ in [(wo_sb[:], t["wtout"].rearrange("(c p) e -> p c e", p=128)),
                           (ones1[:], t["ones1"]), (ones1f[:], t["ones1f"]),
                           (mstr[:], t["mask_st"]), (pmrep[:], t["pm_rep1"]),
                           (dtall[:], t["D1_tall"]), (mt_sb[:], t["Mt"])]:
                g.dma_start(out=o_, in_=i_).then_inc(s_wl.h, 16)
        block.gpsimd(g_init)
        s_wl.v += 16 * 7
        winit = s_wl.v

        def t_f(tt):
            tt.wait_ge(s_wl.h, winit)
            s_pe.inc(tt.matmul(ps_w[0:H1, 0:8], dtall[:], mt_sb[:], start=True, stop=True))
        block.tensor(t_f)
        f_pe = s_pe.v

        def a_f(a):
            a.wait_ge(s_pe.h, f_pe)
            s_ac.inc(a.activation(fh_sb[:], ps_w[0:H1, 0:8], AF.Exp))
        block.scalar(a_f)
        f_ac = s_ac.v

        def v_f(v):
            v.wait_ge(s_ac.h, f_ac)
            sv = s_ve.inc(v.tensor_tensor(fh_sb[:], fh_sb[:], pmrep[:], ALU.mult))
            v.wait_ge(s_ve.h, sv)
        block.vector(v_f)
        f_ve = s_ve.v

        def g_fd(g):
            g.wait_ge(s_ve.h, f_ve)
            base = s_wl.v
            g.dma_start(out=t["f1_d"], in_=fh_sb[:]).then_inc(s_wl.h, 16)
            g.wait_ge(s_wl.h, base + 16)
            g.dma_start(out=d1row[0:1, 0:128].rearrange("one (a j) -> one a j", j=8),
                        in_=t["f1_d"][:, :].unsqueeze(0)).then_inc(s_wl.h, 16)
        block.gpsimd(g_fd)
        s_wl.v += 32
        frow_wl = s_wl.v

        def v_frb(v):
            v.wait_ge(s_wl.h, frow_wl)
            sv = s_ve.inc(v.tensor_copy(o_half[0:1, 0:128], d1row[0:1, 0:128]))
            v.wait_ge(s_ve.h, sv)
        block.vector(v_frb)
        frb_ve = s_ve.v

        def t_fb(tt, frb_ve=frb_ve):
            tt.wait_ge(s_ve.h, frb_ve)
            s_pe.inc(tt.matmul(ps_b1[0:64, 0:128], ones1f[0:1, 0:64], o_half[0:1, 0:128],
                               start=True, stop=True))
        block.tensor(t_fb)
        fb_pe = s_pe.v

        def v_fbc(v, fb_pe=fb_pe):
            v.wait_ge(s_pe.h, fb_pe)
            sv = s_ve.inc(v.tensor_copy(f1bc[:].rearrange("p a j -> p (a j)"),
                                        ps_b1[0:64, 0:128]))
            v.wait_ge(s_ve.h, sv)
            s_ve.inc(v.memset(S1[:], 0.0))
        block.vector(v_fbc)
        sin_ve = [s_ve.v]

        for j in range(8):
            def sy_sj(sy, j=j, prev=sin_ve[0]):
                sy.wait_ge(s_ve.h, prev)
                sy.dma_start(out=sjt[:], in_=t["S1_all"][j]).then_inc(s_ld.h, 16)
            block.sync(sy_sj)
            s_ld.v += 16
            sl = s_ld.v

            def v_sj(v, j=j, sl=sl):
                v.wait_ge(s_ld.h, sl)
                v.tensor_tensor(sjt[:].rearrange("p (a q) -> p a q", q=DD),
                                sjt[:].rearrange("p (a q) -> p a q", q=DD),
                                _b(f1bc[:, :, j:j + 1], [64, H1, DD]), ALU.mult)
                s_ve.inc(v.tensor_tensor(S1[:], S1[:], sjt[:], ALU.add))
            block.vector(v_sj)
            sin_ve[0] = s_ve.v

        def v_sbf(v, p=sin_ve[0]):
            v.wait_ge(s_ve.h, p)
            s_ve.inc(v.tensor_copy(S1_bf[:], S1[:]))
        block.vector(v_sbf)
        scan_ve = [s_ve.v]

        xn_pe = [0, 0]
        w1_ve = [0]
        b1_ve = [scan_ve[0]]
        chunk_ld = [0]
        ointra_ve = [0]
        o1_ac = [0]
        o2_ve = [0]
        psw_ve = [scan_ve[0]]
        ost = [0]
        wkd_pe = [0]
        wkt_ve = [0]
        ecc_ac = [0]
        ecc_users = [0]
        edec_row_users = [0]

        for g in range(8):
            par = g % 2
            for sub in range(4):
                tcol = g * 512 + sub * 128
                jj = 4 * g + sub

                def sy_x(sy, tcol=tcol, par=par, xp=xn_pe[par]):
                    if xp:
                        sy.wait_ge(s_pe.h, xp)
                    sy.dma_start(out=xnf[:, par],
                                 in_=xn_r2[:, :, tcol:tcol + 129]).then_inc(s_ld.h, 16)
                    sy.dma_start(out=xnf2[:, par],
                                 in_=xn_r2[:, :, tcol:tcol + 129]).then_inc(s_ld.h, 16)
                block.sync(sy_x)
                s_ld.v += 32
                xl = s_ld.v

                for cc in range(2):
                    coff = cc * 64
                    ci = tcol // 64 + cc
                    tch = tcol + coff

                    def sy_c(sy, tch=tch, ci=ci, wv=wkt_ve[0], wp=wkd_pe[0],
                             ea=ecc_users[0]):
                        if wv:
                            sy.wait_ge(s_ve.h, wv)
                        sy.dma_start_transpose(wkt[:], t["xn1_ext"][:, tch:tch + 128]).then_inc(s_tb.h, 16)
                        if wp:
                            sy.wait_ge(s_pe.h, wp)
                        sy.dma_start(out=vbc[:], in_=t["vb_tm"][tch:tch + 64, :]).then_inc(s_tb.h, 16)
                        if ea:
                            sy.wait_ge(s_ve.h, ea)
                        sy.dma_start(out=dwfc[:], in_=t["dwf_tm"][tch:tch + 64, :]).then_inc(s_tb.h, 16)
                        sy.dma_start(out=decc[:], in_=t["dec_tm"][tch:tch + 64, :]).then_inc(s_tb.h, 16)
                        sy.dma_start(out=gatec[:], in_=t["gate_tm"][tch:tch + 64, :]).then_inc(s_tb.h, 16)
                        sy.dma_start(out=d1row[0:1].rearrange("one (h l) -> one h l", l=CH),
                                     in_=t["dec_fm"][:, ci * 64:(ci + 1) * 64].unsqueeze(0)
                                     ).then_inc(s_tb.h, 16)
                    block.sync(sy_c)
                    s_tb.v += 96
                    cl = s_tb.v

                    def t_w1(tt, par=par, sub=sub, coff=coff, xl=xl, wv=w1_ve[0],
                             pv=psw_ve[0]):
                        tt.wait_ge(s_ld.h, xl)
                        if wv:
                            tt.wait_ge(s_ve.h, wv)
                        tt.wait_ge(s_ve.h, pv)
                        for h in range(H1):
                            ins = tt.matmul(ps_w[0:64, h * 64:(h + 1) * 64],
                                            xnf2[:, par, h, coff:coff + 64],
                                            xnf[:, par, h, coff + 1:coff + 65],
                                            start=True, stop=True)
                        s_pe.inc(ins)
                    block.tensor(t_w1)
                    w1_pe = s_pe.v

                    def v_b2(v, cl=cl, iv=ointra_ve[0]):
                        v.wait_ge(s_tb.h, cl)
                        if iv:
                            v.wait_ge(s_ve.h, iv)
                        sv = s_ve.inc(v.tensor_copy(b2_sb[:].rearrange("s (h l) -> s h l", l=DD),
                                                    _b(decc[:].unsqueeze(2), [64, H1, DD])))
                        v.wait_ge(s_ve.h, sv)
                    block.vector(v_b2)
                    b2v = s_ve.v

                    def t_b1(tt, cl=cl, bv=b1_ve[0]):
                        tt.wait_ge(s_tb.h, cl)
                        if bv:
                            tt.wait_ge(s_ve.h, bv)
                        for nn in range(2):
                            ins = tt.matmul(ps_b1[0:64, nn * 512:(nn + 1) * 512], ones1f[0:1, 0:64],
                                            d1row[0:1, nn * 512:(nn + 1) * 512], start=True, stop=True)
                        s_pe.inc(ins)
                    block.tensor(t_b1)
                    b1_pe = s_pe.v

                    def v_dL(v, b1_pe=b1_pe, b2v=b2v):
                        v.wait_ge(s_pe.h, b1_pe)
                        sv = s_ve.inc(v.tensor_tensor(b2_sb[:], ps_b1[0:64, :], b2_sb[:], ALU.subtract))
                        v.wait_ge(s_ve.h, sv)
                        sv = s_ve.inc(v.tensor_scalar(b2_sb[:], b2_sb[:], 0.0, None, ALU.min))
                        v.wait_ge(s_ve.h, sv)
                    block.vector(v_dL)
                    b1_ve[0] = s_ve.v
                    dL_ve = s_ve.v

                    def a_lx(a, dL_ve=dL_ve):
                        a.wait_ge(s_ve.h, dL_ve)
                        s_ac.inc(a.activation(b2_sb[:], b2_sb[:], AF.Exp))
                    block.scalar(a_lx)
                    lx_ac = s_ac.v

                    def v_intra(v, lx_ac=lx_ac, w1_pe=w1_pe, iv=ointra_ve[0]):
                        v.wait_ge(s_ac.h, lx_ac)
                        v.wait_ge(s_pe.h, w1_pe)
                        v.tensor_tensor(b2_sb[:].rearrange("s (h l) -> s h l", l=DD),
                                        b2_sb[:].rearrange("s (h l) -> s h l", l=DD),
                                        _b(mstr[:].unsqueeze(1), [64, H1, DD]), ALU.mult)
                        s_ve.inc(v.tensor_tensor(intra[:], b2_sb[:], ps_w[0:64, :], ALU.mult))
                    block.vector(v_intra)
                    w1_ve[0] = s_ve.v
                    ointra_ve[0] = s_ve.v
                    intra_ve = s_ve.v

                    def a_ed(a, cl=cl, eu=ecc_users[0]):
                        a.wait_ge(s_tb.h, cl)
                        if eu:
                            a.wait_ge(s_ve.h, eu)
                        s_ac.inc(a.activation(edecc[:], decc[:], AF.Exp))
                    block.scalar(a_ed)
                    ed_ac = s_ac.v
                    ecc_ac[0] = s_ac.v

                    def t_o12(tt, par=par, sub=sub, coff=coff, intra_ve=intra_ve,
                              sc=scan_ve[0], o1a=o1_ac[0], o2v=o2_ve[0], cl=cl):
                        tt.wait_ge(s_ve.h, max(intra_ve, sc))
                        tt.wait_ge(s_tb.h, cl)
                        if o1a:
                            tt.wait_ge(s_ac.h, o1a)
                        if o2v:
                            tt.wait_ge(s_ve.h, o2v)
                        for h in range(H1):
                            tt.matmul(ps_o1[0:64, h * 64:(h + 1) * 64],
                                      xnf[:, par, h, coff + 1:coff + 65],
                                      S1_bf[:, h * 64:(h + 1) * 64],
                                      start=True, stop=True)
                            ins = tt.matmul(ps_o2[0:64, h * 64:(h + 1) * 64],
                                            intra[:, h * 64:(h + 1) * 64],
                                            vbc[:, h * 64:(h + 1) * 64],
                                            start=True, stop=True)
                        s_pe.inc(ins)
                    block.tensor(t_o12)
                    o12_pe = s_pe.v
                    xn_pe[par] = s_pe.v

                    def a_o1(a, o12_pe=o12_pe, ed_ac=ed_ac, ov=o2_ve[0]):
                        a.wait_ge(s_pe.h, o12_pe)
                        a.wait_ge(s_ac.h, ed_ac)
                        s_ac.inc(a.activation(o_half[:], ps_o1[0:64, :], AF.Copy))
                    block.scalar(a_o1)
                    o1_ac[0] = s_ac.v
                    o1a = s_ac.v

                    def v_o(v, o1a=o1a, o12_pe=o12_pe, ci=ci):
                        v.wait_ge(s_ac.h, o1a)
                        v.tensor_tensor(o_half[:].rearrange("s (h l) -> s h l", l=DD),
                                        o_half[:].rearrange("s (h l) -> s h l", l=DD),
                                        _b(edecc[:].unsqueeze(2), [64, H1, DD]), ALU.mult)
                        v.tensor_tensor(o_half[:], o_half[:], ps_o2[0:64, :], ALU.add)
                        sv = s_ve.inc(v.tensor_tensor(o_bf[:].rearrange("s (h l) -> s h l", l=DD),
                                                      o_half[:].rearrange("s (h l) -> s h l", l=DD),
                                                      _b(gatec[:].unsqueeze(2), [64, H1, DD]), ALU.mult))
                        v.wait_ge(s_ve.h, sv)
                    block.vector(v_o)
                    o2_ve[0] = s_ve.v
                    ecc_users[0] = s_ve.v
                    ov = s_ve.v

                    def g_o(g_, tch=tch, ov=ov):
                        g_.wait_ge(s_ve.h, ov)
                        ins = g_.dma_start(out=t["o_d"][tch:tch + 64, :], in_=o_bf[:])
                        s_st.inc(ins, 16)
                    block.gpsimd(g_o)
                    ost[0] = s_st.v

                    def v_wkp(v, cl=cl, wp=wkd_pe[0]):
                        v.wait_ge(s_tb.h, cl)
                        if wp:
                            v.wait_ge(s_pe.h, wp)
                        s_ve.inc(v.tensor_tensor(wkp[:].rearrange("p (h q) -> p h q", q=DD),
                                                 wkt[0:64, :].rearrange("p (h q) -> p h q", q=DD),
                                                 _b(dwfc[:].unsqueeze(2), [64, H1, DD]), ALU.mult))
                    block.vector(v_wkp)
                    wkt_ve[0] = s_ve.v
                    wkp_ve = s_ve.v

                    def t_ds(tt, wkp_ve=wkp_ve, pv=psw_ve[0], w1v=w1_ve[0]):
                        tt.wait_ge(s_ve.h, max(wkp_ve, w1v))
                        for h in range(H1):
                            ins = tt.matmul(ps_w[0:64, h * DD:(h + 1) * DD],
                                            wkp[:, h * DD:(h + 1) * DD],
                                            vbc[:, h * DD:(h + 1) * DD],
                                            start=True, stop=True)
                        s_pe.inc(ins)
                    block.tensor(t_ds)
                    ds_pe = s_pe.v
                    wkd_pe[0] = s_pe.v

                    def v_dc(v, cl=cl, ee=edec_row_users[0]):
                        v.wait_ge(s_tb.h, cl)
                        if ee:
                            v.wait_ge(s_pe.h, ee)
                        sv = s_ve.inc(v.tensor_copy(edrow[0:1, 0:16], d1row[0:1, 63:1024:64]))
                        v.wait_ge(s_ve.h, sv)
                    block.vector(v_dc)
                    edrow_ve = s_ve.v

                    def t_el(tt, ds_pe=ds_pe, bv=b1_ve[0], ev=edrow_ve):
                        if bv:
                            tt.wait_ge(s_ve.h, bv)
                        tt.wait_ge(s_ve.h, ev)
                        s_pe.inc(tt.matmul(ps_b1[0:64, 0:16], ones1f[0:1, 0:64],
                                           edrow[0:1, 0:16], start=True, stop=True))
                    block.tensor(t_el)
                    el_pe = s_pe.v
                    edec_row_users[0] = s_pe.v

                    def v_el(v, el_pe=el_pe, sc=scan_ve[0]):
                        v.wait_ge(s_pe.h, el_pe)
                        v.wait_ge(s_ve.h, sc)
                        sv = s_ve.inc(v.tensor_copy(edecf[:], ps_b1[0:64, 0:16]))
                        v.wait_ge(s_ve.h, sv)
                    block.vector(v_el)
                    el_ve = s_ve.v
                    b1_ve[0] = s_ve.v

                    def a_el(a, el_ve=el_ve):
                        a.wait_ge(s_ve.h, el_ve)
                        s_ac.inc(a.activation(edecf[:], edecf[:], AF.Exp))
                    block.scalar(a_el)
                    el_ac = s_ac.v

                    def v_sc2(v, el_ac=el_ac, ds_pe=ds_pe):
                        v.wait_ge(s_ac.h, el_ac)
                        v.wait_ge(s_pe.h, ds_pe)
                        v.tensor_tensor(S1[:].rearrange("p (a q) -> p a q", q=DD),
                                        S1[:].rearrange("p (a q) -> p a q", q=DD),
                                        _b(edecf[:].unsqueeze(2), [64, H1, DD]), ALU.mult)
                        v.tensor_tensor(S1[:], S1[:], ps_w[0:64, :], ALU.add)
                        s_ve.inc(v.tensor_copy(S1_bf[:], S1[:]))
                    block.vector(v_sc2)
                    scan_ve[0] = s_ve.v
                    psw_ve[0] = s_ve.v
                    ecc_users[0] = s_ve.v



        of_pe = [0]
        x2_st = [0]
        for g in range(8):
            for sub in range(4):
                tcol = g * 512 + sub * 128

                if sub == 0:
                    def sy_og(sy, g=g, op=of_pe[0], os_=ost[0]):
                        sy.wait_ge(s_st.h, os_)
                        if op:
                            sy.wait_ge(s_pe.h, op)
                        for dc in range(8):
                            sy.dma_start_transpose(ofm[:, dc, :],
                                                   t["o_d"][g * 512:(g + 1) * 512,
                                                            dc * 128:(dc + 1) * 128]
                                                   ).then_inc(s_tb.h, 16)
                    block.sync(sy_og)
                    s_tb.v += 16 * 8

                def sy_of(sy, tcol=tcol, pv=psw_ve[0]):
                    if pv:
                        sy.wait_ge(s_ve.h, pv)
                    sy.dma_start_transpose(xn1t[:],
                                           t["xn1_ext"][:, 1 + tcol:1 + tcol + 128]).then_inc(s_tb.h, 16)
                block.sync(sy_of)
                s_tb.v += 16
                ofl = s_tb.v

                def sy_x2(sy, tcol=tcol, xs=x2_st[0]):
                    if xs:
                        sy.wait_ge(s_st.h, xs)
                    sy.dma_start(out=x1t[:], in_=t["x1_tm"][tcol:tcol + 128, :]).then_inc(s_ld.h, 16)
                    sy.dma_start(out=g1t[:], in_=t["g1_out"][tcol:tcol + 128, :]).then_inc(s_ld.h, 16)
                block.sync(sy_x2)
                s_ld.v += 32
                xxl = s_ld.v

                def t_wo(tt, sub=sub, ofl=ofl, pv=psw_ve[0]):
                    tt.wait_ge(s_tb.h, ofl)
                    tt.wait_ge(s_ve.h, pv)
                    for nn in range(2):
                        for dc in range(8):
                            ins = tt.matmul(ps_w[:, nn * 512:(nn + 1) * 512],
                                            ofm[:, dc, sub * 128:(sub + 1) * 128],
                                            wo_sb[:, dc, nn * 512:(nn + 1) * 512],
                                            start=(dc == 0), stop=(dc == 7))
                    s_pe.inc(ins)
                block.tensor(t_wo)
                of_pe[0] = s_pe.v
                wo_pe = s_pe.v

                def v_x2(v, tcol=tcol, wo_pe=wo_pe, xxl=xxl, ofl=ofl):
                    v.wait_ge(s_pe.h, wo_pe)
                    v.wait_ge(s_ld.h, xxl)
                    v.wait_ge(s_tb.h, ofl)
                    v.tensor_tensor(x2t[:], ps_w[:, 0:D], xn1t[:], ALU.add)
                    v.tensor_tensor(x2t[:], x2t[:], g1t[:], ALU.mult)
                    s_ve.inc(v.tensor_tensor(x2t[:], x2t[:], x1t[:], ALU.add))
                block.vector(v_x2)
                psw_ve[0] = s_ve.v
                xv = s_ve.v

                def g_x2(g_, tcol=tcol, xv=xv):
                    g_.wait_ge(s_ve.h, xv)
                    ins = g_.dma_start(out=t["x2"][tcol:tcol + 128, :], in_=x2t[:])
                    s_st.inc(ins, 16)
                block.gpsimd(g_x2)
                x2_st[0] = s_st.v

        def g_end(g_):
            g_.wait_ge(s_st.h, s_st.v)
            g_.wait_ge(s_tb.h, s_tb.v)
            g_.wait_ge(s_ld.h, s_ld.v)
            g_.wait_ge(s_wl.h, s_wl.v)
        block.gpsimd(g_end)
    return nc



def prep_c1_inputs(inputs, resB, resM):
    u = inputs["L0_attn_u"]
    maps = []
    for c in range(NCORES):
        b, q = divmod(c, 4)
        maps.append(dict(u=np.ascontiguousarray(u[b, q * T:(q + 1) * T]),
                         s_tm=np.asarray(resB[c]["s_tm"]),
                         g_out=np.asarray(resM[c]["g_out"])))
    return maps


def route_xn1(resC):
    import ml_dtypes as _md
    exts = []
    for c in range(NCORES):
        e = np.zeros((D, 4224), dtype=_md.bfloat16)
        e[:, 0:1 + T] = np.asarray(resC[c]["xn1_ext"])
        exts.append(e)
    for c in range(NCORES):
        b, q = divmod(c, 4)
        if q > 0:
            exts[c][:, 0] = exts[c - 1][:, T]
    return exts


def prep_l1c_inputs(inputs, exts):
    wtw = _tr(inputs["L1_attn_W_write"])
    wtg = _tr(inputs["L1_attn_W_gate"])
    wtb = _tr(inputs["L1_attn_W_beta"])
    wta = _tr(inputs["L1_attn_W_alpha"])
    alog1 = np.ascontiguousarray(inputs["L1_attn_A_log"][:, None])
    dtb1 = np.ascontiguousarray(inputs["L1_attn_dt_bias"][:, None])
    return [dict(xn1_ext=exts[c], wtw=wtw, wtg=wtg, wtb=wtb, wta=wta,
                 alog1=alog1, dtb1=dtb1) for c in range(NCORES)]


def prep_l1f_inputs(inputs, exts):
    wfcT = _tr(inputs["L1_ffn_W_fc"])
    bfc_fm = np.ascontiguousarray(inputs["L1_ffn_b_fc"].reshape(32, 128).T)
    afcT = _tr(inputs["L1_ffn_A_fc"])
    bfcT = _tr(inputs["L1_ffn_B_fc"])
    wprT = _tr(inputs["L1_ffn_W_proj"])
    bpr_rep = np.ascontiguousarray(np.broadcast_to(inputs["L1_ffn_b_proj"][None, :], (128, D)))
    aprT = _tr(inputs["L1_ffn_A_proj"])
    bprT = _tr(inputs["L1_ffn_B_proj"])
    return [dict(xn1_ext=exts[c], wfcT=wfcT, bfc_fm=bfc_fm, afcT=afcT, bfcT=bfcT,
                 wprT=wprT, bpr_rep=bpr_rep, aprT=aprT, bprT=bprT) for c in range(NCORES)]


def prep_l1d_inputs(inputs, exts, resL1C, resL1F, resC):
    wtout = _tr(inputs["L1_attn_W_out"])
    S1_all_b = [np.stack([np.asarray(resL1C[b * 4 + j]["S1_loc"]) for j in range(4)] +
                         [np.zeros((64, 1024), np.float32)] * 4) for b in range(2)]
    D1_tall_b = [np.stack([np.asarray(resL1C[b * 4 + j]["D1_loc"])[:, 0] for j in range(4)] +
                          [np.zeros(H1, np.float32)] * 4) for b in range(2)]
    maps = []
    for c in range(NCORES):
        b, q = divmod(c, 4)
        Mt = np.zeros((8, 8), np.float32)
        pm = np.zeros(8, np.float32)
        for j in range(q):
            pm[j] = 1.0
            for m in range(j + 1, q):
                Mt[m, j] = 1.0
        pm_rep1 = np.ascontiguousarray(np.broadcast_to(pm[None, :], (H1, 8)))
        r = resL1C[c]
        maps.append(dict(
            xn1_ext=exts[c], vb_tm=np.asarray(r["vb_tm"]),
            dec_fm=np.asarray(r["dec_fm"]), dec_tm=np.asarray(r["dec_tm"]),
            dwf_tm=np.asarray(r["dwf_tm"]), gate_tm=np.asarray(r["gate_tm"]),
            S1_all=S1_all_b[b], D1_tall=D1_tall_b[b], Mt=Mt, pm_rep1=pm_rep1,
            wtout=wtout, x1_tm=np.asarray(resC[c]["x1_tm"]),
            g1_out=np.asarray(resL1F[c]["g1_out"])))
    return maps


_BUILD_CACHE = {}


def _get(name, fn):
    if name not in _BUILD_CACHE:
        _BUILD_CACHE[name] = fn()
    return _BUILD_CACHE[name]


def kernel_multi(**inputs):
    import sys
    def _p(m):
        print(f"[kernel] {m}", file=sys.stderr, flush=True)
    cores = list(range(NCORES))
    _p("l0a")
    resA = run_bass_kernel_spmd(_get("l0a", build_l0a), prep_l0a_inputs(inputs), core_ids=cores).results
    xn_fms = [np.asarray(r["xn_fm"]) for r in resA]
    _p("moe")
    resM = run_bass_kernel_spmd(_get("moe", build_moe), prep_moe_inputs(inputs, xn_fms), core_ids=cores).results
    _p("l0b")
    resB = run_bass_kernel_spmd(_get("l0b", build_l0b), prep_l0b_inputs(inputs, resA), core_ids=cores).results
    _p("c1")
    resC = run_bass_kernel_spmd(_get("c1", build_c1), prep_c1_inputs(inputs, resB, resM), core_ids=cores).results
    exts = route_xn1(resC)
    _p("l1c")
    resL1C = run_bass_kernel_spmd(_get("l1c", build_l1c), prep_l1c_inputs(inputs, exts), core_ids=cores).results
    _p("l1f")
    resL1F = run_bass_kernel_spmd(_get("l1f", build_l1f), prep_l1f_inputs(inputs, exts), core_ids=cores).results
    _p("l1d")
    resL1D = run_bass_kernel_spmd(_get("l1d", build_l1d),
                                  prep_l1d_inputs(inputs, exts, resL1C, resL1F, resC),
                                  core_ids=cores).results
    out = np.zeros((2, 4 * T, D), np.float32)
    for c in range(NCORES):
        b, q = divmod(c, 4)
        out[b, q * T:(q + 1) * T] = np.asarray(resL1D[c]["x2"])
    return out



def _barrier(nc, exclude=()):
    snap = [(s.h, s.v) for s in Sem._reg if s.v > 0 and s not in exclude]
    with nc.Block() as bb:
        for en in ("sync", "gpsimd", "tensor", "vector", "scalar"):
            def f(e, snap=snap):
                for h, v in snap:
                    e.wait_ge(h, v)
            getattr(bb, en)(f)


def build_fused():
    nc = bass.Bass("TRN2", target_bir_lowering=False, detect_race_conditions=False)
    Sem._reg = []
    RG = [list(range(NCORES))]

    def ext(name, shape, dt=F32):
        return nc.dram_tensor(name, shape, dt, kind="ExternalInput").ap()

    def itn(name, shape, dt=F32):
        return nc.dram_tensor(name, shape, dt, kind="Internal").ap()

    def il(arr, name):
        return nc.inline_tensor(arr, name=name).ap()

    eye = np.eye(128, dtype=BF)
    onesb = np.ones((1, 128), dtype=BF)
    onesf = np.ones((1, 128), np.float32)
    d0a = np.ones((H0, 512), np.float32); d0a[:, 0::CH] = 0.0
    d0c = np.ones((H1, 512), np.float32); d0c[:, 0::CH] = 0.0
    cvD = np.tile(np.array([EPS, 1.0 / D], np.float32), (128, 1))
    cvDI = np.tile(np.array([EPS, 1.0 / DI], np.float32), (128, 1))

    u = ext("u", [T, D])
    Mt = ext("Mt", [8, 8])
    xn_fm = itn("xn_fm", [D, 128 + T], BF16)
    xcb_fm = itn("xcb_fm", [CONVD, T], BF16)
    xdt_tm = itn("xdt_tm", [T, DI], BF16)
    xw_tm = itn("xw_tm", [T, DI], BF16)
    cumA_fm = itn("cumA_fm", [H0, T])
    cumA_tm = itn("cumA_tm", [T, H0])
    S_loc = itn("S_loc", [DS, DI])
    D_loc = itn("D_loc", [H0, 1])
    S_all = itn("S_all", [8, DS, DI])
    D_tall = itn("D_tall", [8, H0])
    g_out = itn("g_out", [T, D], BF16)
    s_tm = itn("s_tm", [T, D])
    x1_tm = itn("x1_tm", [T, D])
    xn1_ext = itn("xn1_ext", [D, 4224], BF16)
    bcol_src = itn("bcol_src", [D, 1], BF16)
    bcol_all = itn("bcol_all", [8, D], BF16)
    vb_tm = itn("vb_tm", [T, D], BF16)
    dec_fm = itn("dec_fm", [H1, T])
    dec_tm = itn("dec_tm", [T, H1])
    dwf_tm = itn("dwf_tm", [T, H1], BF16)
    gate_tm = itn("gate_tm", [T, H1])
    S1_loc = itn("S1_loc", [64, 1024])
    D1_loc = itn("D1_loc", [H1, 1])
    S1_all = itn("S1_all", [8, 64, 1024])
    D1_tall = itn("D1_tall", [8, H1])
    g1_out = itn("g1_out", [T, D], BF16)
    x2 = nc.dram_tensor("x2", [T, D], F32, kind="ExternalOutput").ap()

    tA = dict(u=u, uh=ext("uh", [128, D]),
              wtxb=ext("wtxb", [D, CONVD]), wtdt=ext("wtdt", [D, H0]),
              cw=ext("cw", [CONVD, 4]), cb=ext("cb", [128, 17]),
              dtb=ext("dtb", [H0, 1]), alog=ext("alog", [H0, 1]),
              xn_fm=xn_fm, xcb_fm=xcb_fm, xdt_tm=xdt_tm, xw_tm=xw_tm,
              cumA_fm=cumA_fm, cumA_tm=cumA_tm, S_loc=S_loc, D_loc=D_loc,
              id128=il(eye, "id128"), d0pat=il(d0a, "d0pat"),
              ones1=il(onesb, "ones1c"), cvals=il(cvD, "cvals"),
              ecl_d=itn("ecl_d", [H0, NCHUNK]))
    emit_l0a(nc, tA)

    cc1 = Sem(nc, "cc1")
    with nc.Block() as bb:
        def g_c1(g):
            g.collective_compute("AllGather", ALU.bypass, replica_groups=RG,
                                 ins=[S_loc.opt()], outs=[S_all.opt()]).then_inc(cc1.h, 1)
            g.collective_compute("AllGather", ALU.bypass, replica_groups=RG,
                                 ins=[D_loc.opt()], outs=[D_tall.opt()]).then_inc(cc1.h, 1)
        bb.gpsimd(g_c1)
    cc1.v += 2
    _barrier(nc, exclude=(cc1,))

    tM = dict(xn_fm=xn_fm, router=ext("router", [D, 8]),
              w1=ext("w1", [8, D, 2816]), v1=ext("v1", [8, D, 2816]),
              w2=ext("w2", [8, 2816, D]), bias_rep=ext("bias_rep", [128, D]),
              g_out=g_out, gacc=itn("gacc", [T, D]))
    emit_moe(nc, tM)
    _barrier(nc)

    tB = dict(xn_fm=xn_fm, xcb_fm=xcb_fm, xdt_tm=xdt_tm, xw_tm=xw_tm,
              cumA_fm=cumA_fm, cumA_tm=cumA_tm, S_all=S_all, D_tall=D_tall,
              Mt=Mt, pm_rep=ext("pm_rep", [H0, 8]),
              wtz=ext("wtz", [D, DI]), wtc=ext("wtc", [D, DS]),
              cw_c=ext("cw_c", [DS, 4]), cb_c=ext("cb_c", [DS, 1]),
              d_rep=ext("d_rep", [128, H0]), nw_rep=ext("nw_rep", [128, DI]),
              wto=ext("wto", [DI, D]), s_tm=s_tm,
              tg_d=itn("tg_d", [T, DI], BF16), ecl_d=itn("ecl_d2", [H0, NCHUNK]),
              f_d=itn("f_d", [H0, 8]),
              id128=il(eye, "id128b"), ones1=il(onesb, "ones1b"),
              ones1f=il(np.ones((1, 64), np.float32), "ones1f_c"),
              mask_su=il(np.triu(np.ones((64, 64), np.float32)), "mask_su_c"),
              cvals=il(cvDI, "cvalsb"))
    emit_l0b(nc, tB)
    _barrier(nc)

    tC = dict(u=u, s_tm=s_tm, g_out=g_out, x1_tm=x1_tm,
              xn1_ext=xn1_ext[:, 0:1 + T],
              id128=il(eye, "id128c"), cvals=il(cvD, "cvalsc"))
    emit_c1(nc, tC)

    cc2 = Sem(nc, "cc2")
    cbs = Sem(nc, "cbs")
    with nc.Block() as bb:
        def g_c2(g):
            g.dma_start(out=bcol_src, in_=xn1_ext[:, T:T + 1],
                        allow_slow_non_contiguous=True).then_inc(cbs.h, 16)
            g.wait_ge(cbs.h, 16)
            g.collective_compute("AllGather", ALU.bypass, replica_groups=RG,
                                 ins=[bcol_src.opt()], outs=[bcol_all.opt()]).then_inc(cc2.h, 1)
        bb.gpsimd(g_c2)
    cbs.v += 16
    cc2.v += 1
    _barrier(nc, exclude=(cc2,))

    tF = dict(xn1_ext=xn1_ext, wfcT=ext("wfcT", [D, 4096]),
              bfc_fm=ext("bfc_fm", [128, 32]), afcT=ext("afcT", [D, H1]),
              bfcT=ext("bfcT", [H1, 4096]), wprT=ext("wprT", [4096, D]),
              bpr_rep=ext("bpr_rep", [128, D]), aprT=ext("aprT", [4096, H1]),
              bprT=ext("bprT", [H1, D]), g1_out=g1_out,
              wqf_d=itn("wqf_d", [D, 4096], BF16), colb_d=itn("colb_d", [128, 1]),
              ones1=il(onesb, "ones1f2"), ones1f=il(onesf, "ones1f3"))
    emit_l1f(nc, tF)
    _barrier(nc)

    selb = ext("selb", [8, 1], BF16)
    s_sl = Sem(nc, "sl_ld")
    s_sp = Sem(nc, "sl_pe")
    s_sv = Sem(nc, "sl_ve")
    s_ss = Sem(nc, "sl_st")
    import contextlib
    with contextlib.ExitStack() as _stk:
        gsb = _stk.enter_context(nc.sbuf_tensor("gsb", [8, D], BF16))
        selc = _stk.enter_context(nc.sbuf_tensor("selc", [8, 1], BF16))
        colb = _stk.enter_context(nc.sbuf_tensor("colb", [128, 8], BF16))
        ps_s = _stk.enter_context(nc.psum_tensor("ps_s", [128, 8], F32))
        with nc.Block() as bb:
            def sy_s(sy):
                sy.wait_ge(cc2.h, cc2.v)
                sy.dma_start(out=gsb[:], in_=bcol_all).then_inc(s_sl.h, 16)
                sy.dma_start(out=selc[:], in_=selb).then_inc(s_sl.h, 16)
            bb.sync(sy_s)
            s_sl.v += 32

            def t_s(tt):
                tt.wait_ge(s_sl.h, s_sl.v)
                for k in range(8):
                    ins = tt.matmul(ps_s[:, k:k + 1], gsb[:, k * 128:(k + 1) * 128],
                                    selc[:], start=True, stop=True)
                s_sp.inc(ins)
            bb.tensor(t_s)

            def v_s(v):
                v.wait_ge(s_sp.h, s_sp.v)
                s_sv.inc(v.tensor_copy(colb[:], ps_s[:]))
            bb.vector(v_s)

            def g_s(g):
                g.wait_ge(s_sv.h, s_sv.v)
                for k in range(8):
                    g.dma_start(out=xn1_ext[k * 128:(k + 1) * 128, 0:1],
                                in_=colb[:, k:k + 1],
                                allow_slow_non_contiguous=True).then_inc(s_ss.h, 16)
            bb.gpsimd(g_s)
            s_ss.v += 16 * 8
    _barrier(nc)

    tE = dict(xn1_ext=xn1_ext, wtw=ext("wtw", [D, D]), wtg=ext("wtg", [D, H1]),
              wtb=ext("wtb", [D, H1]), wta=ext("wta", [D, H1]),
              alog1=ext("alog1", [H1, 1]), dtb1=ext("dtb1", [H1, 1]),
              vb_tm=vb_tm, dec_fm=dec_fm, dec_tm=dec_tm, dwf_tm=dwf_tm,
              gate_tm=gate_tm, S1_loc=S1_loc, D1_loc=D1_loc,
              id128=il(eye, "id128d"), d0pat1=il(d0c, "d0pat1"),
              ones1=il(onesb, "ones1d"), ecl1_d=itn("ecl1_d", [H1, NCHUNK]))
    emit_l1c(nc, tE)

    cc3 = Sem(nc, "cc3")
    with nc.Block() as bb:
        def g_c3(g):
            g.collective_compute("AllGather", ALU.bypass, replica_groups=RG,
                                 ins=[S1_loc.opt()], outs=[S1_all.opt()]).then_inc(cc3.h, 1)
            g.collective_compute("AllGather", ALU.bypass, replica_groups=RG,
                                 ins=[D1_loc.opt()], outs=[D1_tall.opt()]).then_inc(cc3.h, 1)
        bb.gpsimd(g_c3)
    cc3.v += 2
    _barrier(nc)

    tG = dict(xn1_ext=xn1_ext, vb_tm=vb_tm, dec_fm=dec_fm, dec_tm=dec_tm,
              dwf_tm=dwf_tm, gate_tm=gate_tm, S1_all=S1_all, D1_tall=D1_tall,
              Mt=Mt, pm_rep1=ext("pm_rep1", [H1, 8]), wtout=ext("wtout", [D, D]),
              x1_tm=x1_tm, g1_out=g1_out, x2=x2,
              o_d=itn("o_d", [T, D], BF16), f1_d=itn("f1_d", [H1, 8]),
              ones1=il(onesb, "ones1e"), ones1f=il(onesf, "ones1f4"),
              mask_st=il(np.triu(np.ones((64, 64), np.float32), 1), "mask_st_c"))
    emit_l1d(nc, tG)
    return nc


def prep_fused_inputs(inputs):
    import ml_dtypes as _md
    W = inputs["L0_attn_in_proj_w"]
    shared = dict(
        wtxb=_tr(W[DI:DI + CONVD]), wtdt=_tr(W[4352:4384]),
        cw=np.ascontiguousarray(inputs["L0_attn_conv_w"][:CONVD, 0, :]),
        cb=np.ascontiguousarray(inputs["L0_attn_conv_b"][:CONVD].reshape(17, 128).T),
        dtb=np.ascontiguousarray(inputs["L0_attn_dt_bias"][:, None]),
        alog=np.ascontiguousarray(inputs["L0_attn_A_log"][:, None]),
        router=np.ascontiguousarray(inputs["L0_ffn_router_w"]),
        w1=inputs["L0_ffn_w1"], v1=inputs["L0_ffn_v1"], w2=inputs["L0_ffn_w2"],
        bias_rep=np.ascontiguousarray(np.broadcast_to(inputs["L0_ffn_bias"][None, :], (128, D))),
        wtz=_tr(W[0:DI]), wtc=_tr(W[4224:4352]),
        cw_c=np.ascontiguousarray(inputs["L0_attn_conv_w"][CONVD:CONVD + DS, 0, :]),
        cb_c=np.ascontiguousarray(inputs["L0_attn_conv_b"][CONVD:CONVD + DS, None]),
        d_rep=np.ascontiguousarray(np.broadcast_to(inputs["L0_attn_D"][None, :], (128, H0))),
        nw_rep=np.ascontiguousarray(np.broadcast_to(inputs["L0_attn_norm_w"][None, :], (128, DI))),
        wto=_tr(inputs["L0_attn_out_proj_w"]),
        wtw=_tr(inputs["L1_attn_W_write"]), wtg=_tr(inputs["L1_attn_W_gate"]),
        wtb=_tr(inputs["L1_attn_W_beta"]), wta=_tr(inputs["L1_attn_W_alpha"]),
        alog1=np.ascontiguousarray(inputs["L1_attn_A_log"][:, None]),
        dtb1=np.ascontiguousarray(inputs["L1_attn_dt_bias"][:, None]),
        wfcT=_tr(inputs["L1_ffn_W_fc"]),
        bfc_fm=np.ascontiguousarray(inputs["L1_ffn_b_fc"].reshape(32, 128).T),
        afcT=_tr(inputs["L1_ffn_A_fc"]), bfcT=_tr(inputs["L1_ffn_B_fc"]),
        wprT=_tr(inputs["L1_ffn_W_proj"]),
        bpr_rep=np.ascontiguousarray(np.broadcast_to(inputs["L1_ffn_b_proj"][None, :], (128, D))),
        aprT=_tr(inputs["L1_ffn_A_proj"]), bprT=_tr(inputs["L1_ffn_B_proj"]),
        wtout=_tr(inputs["L1_attn_W_out"]))
    u = inputs["L0_attn_u"]
    maps = []
    for c in range(NCORES):
        b, q = divmod(c, 4)
        t0 = q * T
        useg = np.ascontiguousarray(u[b, t0:t0 + T])
        uh = np.ascontiguousarray(u[b, t0 - 128:t0]) if q > 0 else np.zeros((128, D), np.float32)
        Mt8 = np.zeros((8, 8), np.float32)
        pm8 = np.zeros(8, np.float32)
        for j in range(q):
            pm8[b * 4 + j] = 1.0
            for m in range(j + 1, q):
                Mt8[b * 4 + m, b * 4 + j] = 1.0
        selb = np.zeros((8, 1), _md.bfloat16)
        if q > 0:
            selb[c - 1, 0] = 1.0
        maps.append(dict(u=useg, uh=uh, Mt=Mt8,
                         pm_rep=np.ascontiguousarray(np.broadcast_to(pm8[None, :], (H0, 8))),
                         pm_rep1=np.ascontiguousarray(np.broadcast_to(pm8[None, :], (H1, 8))),
                         selb=selb, **shared))
    return maps


def kernel(**inputs):
    cores = list(range(NCORES))
    res = run_bass_kernel_spmd(_get("fused", build_fused), prep_fused_inputs(inputs),
                               core_ids=cores).results
    out = np.zeros((2, 4 * T, D), np.float32)
    for c in range(NCORES):
        b, q = divmod(c, 4)
        out[b, q * T:(q + 1) * T] = np.asarray(res[c]["x2"])
    return out


# version 04
# speedup vs baseline: 1.1720x; 1.1720x over previous
import numpy as np
import ml_dtypes
import concourse.bass as bass
import concourse.mybir as mybir
from concourse.bass_utils import run_bass_kernel_spmd

F32 = mybir.dt.float32
BF16 = mybir.dt.bfloat16
AF = mybir.ActivationFunctionType
ALU = mybir.AluOpType
X = mybir.AxisListType.X
BF = ml_dtypes.bfloat16

NCORES = 8
T = 4096
D = 1024
CH = 64
NCHUNK = T // CH
H0 = 32
P0 = 64
DI = 2048
DS = 128
CONVD = DI + DS
H1 = 16
DD = 64
MAGIC = 12582912.0
EPS = 1e-5


_NAME_SEQ = [0]


class Sem:

    _reg = []

    def __init__(self, nc, name):
        self.h = nc.alloc_semaphore(name)
        self.v = 0
        Sem._reg.append(self)

    def inc(self, instr, k=1):
        self.v += k
        instr.then_inc(self.h, k)
        return self.v


def _b(ap, shape):
    return ap.broadcast_to(shape)


def _hq(ap, q):
    return ap.rearrange("p (h q) -> p h q", q=q)



def build_l0a():
    nc = bass.Bass("TRN2", target_bir_lowering=False, detect_race_conditions=False)
    tn = {}
    def inp(name, shape, dt=F32):
        tn[name] = nc.dram_tensor(name, shape, dt, kind="ExternalInput").ap()
    def outp(name, shape, dt=F32):
        tn[name] = nc.dram_tensor(name, shape, dt, kind="ExternalOutput").ap()
    inp("u", [T, D]); inp("uh", [128, D])
    inp("wtxb", [D, CONVD]); inp("wtdt", [D, H0])
    inp("cw", [CONVD, 4]); inp("cb", [128, 17])
    inp("dtb", [H0, 1]); inp("alog", [H0, 1])
    outp("xn_fm", [D, 128 + T], BF16)
    outp("xcb_fm", [CONVD, T], BF16)
    outp("xdt_tm", [T, DI], BF16)
    outp("xw_tm", [T, DI], BF16)
    outp("cumA_fm", [H0, T])
    outp("cumA_tm", [T, H0])
    outp("S_loc", [DS, DI])
    outp("D_loc", [H0, 1])
    tn["id128"] = nc.inline_tensor(np.eye(128, dtype=BF), name="id128").ap()
    d0 = np.ones((H0, 512), np.float32)
    d0[:, 0::CH] = 0.0
    tn["d0pat"] = nc.inline_tensor(d0, name="d0pat").ap()
    tn["ones1"] = nc.inline_tensor(np.ones((1, 128), dtype=BF), name="ones1c").ap()
    tn["cvals"] = nc.inline_tensor(np.tile(np.array([EPS, 1.0 / D], np.float32), (128, 1)),
                                   name="cvals").ap()
    tn["ecl_d"] = nc.dram_tensor("ecl_d", [H0, NCHUNK], F32, kind="Internal").ap()
    emit_l0a(nc, tn)
    return nc


def emit_l0a(nc, t):
    NJ = 33
    s_uld = Sem(nc, "a_uld")
    s_wld = Sem(nc, "a_wld")
    s_ecl = Sem(nc, "a_ecl")
    s_pe = Sem(nc, "a_pe")
    s_ve = Sem(nc, "a_ve")
    s_ac = Sem(nc, "a_ac")
    s_st = Sem(nc, "a_st")
    s_tb = Sem(nc, "a_tb")

    import contextlib
    with contextlib.ExitStack() as _stk:
        _pfx = f"s{_NAME_SEQ[0]}_"
        _NAME_SEQ[0] += 1
        def _sb(nm, *a):
            return _stk.enter_context(nc.sbuf_tensor(_pfx + nm, *a))
        def _ps(nm, *a):
            return _stk.enter_context(nc.psum_tensor(_pfx + nm, *a))
        wxb_sb = _sb("wxb_sb", [128, 8, CONVD], BF16)
        wdt_sb = _sb("wdt_sb", [128, 8, H0], BF16)
        idn = _sb("idn", [128, 128], BF16)
        d0p = _sb("d0p", [H0, 512], F32)
        ones1 = _sb("ones1", [1, 128], BF16)
        cw_sb = _sb("cw_sb", [128, 17, 4], F32)
        cb_sb = _sb("cb_sb", [128, 17], F32)
        dtb_sb = _sb("dtb_sb", [H0, 1], F32)
        nega_sb = _sb("nega_sb", [H0, 1], F32)
        cst = _sb("cst", [128, 2], F32)
        ut = _sb("ut", [128, 2, D], F32)
        sq = _sb("sq", [128, 2, D], F32)
        xnb = _sb("xnb", [128, 2, D], BF16)
        stg = _sb("stg", [128, 2, 8, 512], BF16)
        stgh = _sb("stgh", [128, 8, 128], BF16)
        cols = _sb("cols", [128, 16], F32)
        pc = _sb("pc", [128, 515], F32)
        carry = _sb("carry", [128, 17, 3], F32)
        cvo = _sb("cvo", [128, 512], F32)
        xcb_sb = _sb("xcb_sb", [128, 512], BF16)
        dts = _sb("dts", [H0, 512], F32)
        dta = _sb("dta", [H0, 512], F32)
        dtc = _sb("dtc", [H0, 512], F32)
        wfc = _sb("wfc", [H0, 512], F32)
        dts_bf = _sb("dts_bf", [H0, 512], BF16)
        wfc_bf = _sb("wfc_bf", [H0, 512], BF16)
        dcl_sb = _sb("dcl_sb", [H0, NCHUNK], F32)
        ca_hi = _sb("ca_hi", [H0, 512], BF16)
        ca_lo = _sb("ca_lo", [H0, 512], BF16)
        ca_his = _sb("ca_his", [128, H0], F32)
        cat_tm = _sb("cat_tm", [128, H0], F32)
        ecl = _sb("ecl", [H0, 8], F32)
        eclrow = _sb("eclrow", [1, 256], F32)
        eclrow_bf = _sb("eclrow_bf", [1, 256], BF16)
        e512 = _sb("e512", [128, 32, 8], F32)
        xbt = _sb("xbt", [128, 2, CONVD], BF16)
        dt_tm = _sb("dt_tm", [128, H0], BF16)
        wf_tm = _sb("wf_tm", [128, H0], BF16)
        xdt_t = _sb("xdt_t", [128, DI], BF16)
        xw_t = _sb("xw_t", [128, DI], BF16)
        S = _sb("S", [DS, DI], F32)
        ps_ip = _ps("ps_ip", [128, 2, 512], F32)
        ps_dt = _ps("ps_dt", [H0, 512], F32)
        ps_tp = _ps("ps_tp", [128, 128], BF16)
        ps_cs = _ps("ps_cs", [DS, DI], F32)
        block = _stk.enter_context(nc.Block())
        ln_done = [0] * NJ
        stg_done = [0] * NJ
        tp_done = [0] * NJ
        evdc = [0]
        ip_ev2 = [0, 0]
        dt_used = [0]
        cum_done = [0]
        e512_pe = [0]
        e512_done = [0]
        eclbf_done = [0]
        si_done = [0]
        xcb_store = [0]
        grp_ip_pe = [0] * 9
        grp_st = [0] * 9
        wf_ready = [0] * 8
        xbt_free = [0, 0]
        tm_ev = [0]
        xdt_st = [0]
        scan_done = [0]
        wfc_used_pe = [0]
        hl_used_pe = [0]
        hl_ready = [0]
        cat_st = [0]

        nc.const_aps.register(F32, EPS, cst[:, 0:1])
        nc.const_aps.register(F32, 1.0 / D, cst[:, 1:2])

        def g_init(g):
            g.dma_start(out=cst[:], in_=t["cvals"]).then_inc(s_wld.h, 16)
            g.dma_start(out=wxb_sb[:], in_=t["wtxb"].rearrange("(c p) e -> p c e", p=128)).then_inc(s_wld.h, 16)
            g.dma_start(out=wdt_sb[:], in_=t["wtdt"].rearrange("(c p) e -> p c e", p=128)).then_inc(s_wld.h, 16)
            g.dma_start(out=idn[:], in_=t["id128"]).then_inc(s_wld.h, 16)
            g.dma_start(out=d0p[:], in_=t["d0pat"]).then_inc(s_wld.h, 16)
            g.dma_start(out=ones1[:], in_=t["ones1"]).then_inc(s_wld.h, 16)
            g.dma_start(out=cw_sb[:], in_=t["cw"].rearrange("(c p) k -> p c k", p=128)).then_inc(s_wld.h, 16)
            g.dma_start(out=cb_sb[:], in_=t["cb"]).then_inc(s_wld.h, 16)
            g.dma_start(out=dtb_sb[:], in_=t["dtb"]).then_inc(s_wld.h, 16)
            g.dma_start(out=nega_sb[:], in_=t["alog"]).then_inc(s_wld.h, 16)
        block.gpsimd(g_init)
        s_wld.v += 16 * 10

        def a_nega(a):
            a.wait_ge(s_wld.h, s_wld.v)
            s_ac.inc(a.activation(nega_sb[:], nega_sb[:], AF.Exp))
        block.scalar(a_nega)
        nega_ac = s_ac.v

        def v_nega(v):
            v.wait_ge(s_ac.h, nega_ac)
            s_ve.inc(v.tensor_scalar(nega_sb[:], nega_sb[:], -1.0, None, ALU.mult))
        block.vector(v_nega)
        nega_ve = s_ve.v
        nega_ready = [nega_ve]

        def v_init(v):
            s_ve.inc(v.memset(S[:], 0.0))
        block.vector(v_init)

        def usrc(j):
            return t["uh"] if j == 0 else t["u"][(j - 1) * 128:j * 128, :]

        def sy_seed(sy):
            sy.dma_start(out=ut[:, 0], in_=usrc(0)).then_inc(s_uld.h, 16)
            sy.dma_start(out=ut[:, 1], in_=usrc(1)).then_inc(s_uld.h, 16)
        block.sync(sy_seed)
        s_uld.v += 32

        sq_free = [0, 0]

        def emit_tile(j):
            def v_ln(v):
                v.wait_ge(s_uld.h, 16 * (j + 1))
                if j == 0:
                    v.wait_ge(s_ve.h, nega_ready[0])
                uu = ut[:, j % 2]
                s_ve.inc(v.reduce_sum(cols[:, 0:1], uu[:], axis=X))
            block.vector(v_ln)

            def a_sq(a):
                a.wait_ge(s_uld.h, 16 * (j + 1))
                a.wait_ge(s_ve.h, sq_free[j % 2])
                s_ac.inc(a.activation(sq[:, j % 2], ut[:, j % 2], AF.Square))
            block.scalar(a_sq)
            sq_ac = s_ac.v

            def v_ln2(v):
                v.wait_ge(s_ac.h, sq_ac)
                s_ve.inc(v.reduce_sum(cols[:, 2:3], sq[:, j % 2], axis=X))
            block.vector(v_ln2)
            red_ve = s_ve.v
            sq_free[j % 2] = red_ve

            def a_mm(a):
                a.wait_ge(s_ve.h, red_ve)
                a.activation(cols[:, 1:2], cols[:, 0:1], AF.Copy, scale=1.0 / D)
                s_ac.inc(a.activation(cols[:, 3:4], cols[:, 2:3], AF.Copy, scale=1.0 / D))
            block.scalar(a_mm)
            mm_ac = s_ac.v

            def v_var(v):
                v.wait_ge(s_ac.h, mm_ac)
                s_ve.inc(v.tensor_tensor(cols[:, 4:5], cols[:, 1:2], cols[:, 1:2], ALU.mult))
            block.vector(v_var)
            m2_ve = s_ve.v

            def v_var2(v):
                v.wait_ge(s_ve.h, m2_ve)
                s_ve.inc(v.tensor_tensor(cols[:, 5:6], cols[:, 3:4], cols[:, 4:5], ALU.subtract))
            block.vector(v_var2)
            var_ve = s_ve.v

            def a_rstd(a):
                a.wait_ge(s_ve.h, var_ve)
                s_ac.inc(a.activation(cols[:, 6:7], cols[:, 5:6], AF.Sqrt, bias=EPS))
            block.scalar(a_rstd)
            rstd_ac = s_ac.v

            def v_rec(v):
                v.wait_ge(s_ac.h, rstd_ac)
                s_ve.inc(v.reciprocal(cols[:, 7:8], cols[:, 6:7]))
            block.vector(v_rec)
            rec_ve = s_ve.v

            def v_xn(v):
                v.wait_ge(s_ve.h, rec_ve)
                if j >= 2:
                    v.wait_ge(s_pe.h, tp_done[j - 2])
                s_ve.inc(v.tensor_scalar(xnb[:, j % 2], ut[:, j % 2], cols[:, 1:2],
                                         cols[:, 7:8], ALU.subtract, ALU.mult))
            block.vector(v_xn)
            ln_done[j] = s_ve.v

            if j + 2 < NJ:
                free_at = ln_done[j]

                def sy_pf(sy):
                    sy.wait_ge(s_ve.h, free_at)
                    sy.dma_start(out=ut[:, (j + 2) % 2], in_=usrc(j + 2)).then_inc(s_uld.h, 16)
                block.sync(sy_pf)
                s_uld.v += 16

            g = (j - 1) // 4 if j >= 1 else -1
            for dc in range(8):
                def t_tp(tt, dc=dc):
                    if dc == 0:
                        tt.wait_ge(s_ve.h, ln_done[j])
                    tt.wait_ge(s_ve.h, evdc[0])
                    s_pe.inc(tt.transpose(ps_tp[:], xnb[:, j % 2, dc * 128:(dc + 1) * 128], idn[:]))
                block.tensor(t_tp)
                pe_now = s_pe.v

                def v_ev(v, dc=dc, pe_now=pe_now, g=g):
                    v.wait_ge(s_pe.h, pe_now)
                    if g >= 2 and dc == 0:
                        v.wait_ge(s_pe.h, grp_ip_pe[g - 2 + 1])
                        v.wait_ge(s_st.h, grp_st[g - 2 + 1])
                    if j == 0:
                        dst = stgh[:, dc, :]
                    else:
                        dst = stg[:, g % 2, dc, ((j - 1) % 4) * 128:((j - 1) % 4) * 128 + 128]
                    s_ve.inc(v.tensor_copy(dst, ps_tp[:]))
                block.vector(v_ev)
                evdc[0] = s_ve.v
            tp_done[j] = s_pe.v
            stg_done[j] = s_ve.v

        def emit_group(g, n_tok, stg_buf, col0):
            last_tile = 0 if g < 0 else 4 * g + 4
            def gg_st(gg):
                gg.wait_ge(s_ve.h, stg_done[last_tile])
                s_st.inc(gg.dma_start(
                    out=t["xn_fm"].rearrange("(c p) tt -> p c tt", p=128)[:, :, col0:col0 + n_tok],
                    in_=stg_buf), 16)
            block.gpsimd(gg_st)

            if g < 0:
                emit_xb_chunks(g, n_tok, stg_buf, last_tile)
                grp_ip_pe[g + 1] = s_pe.v
                grp_st[g + 1] = s_st.v
                return

            def t_dt(tt):
                tt.wait_ge(s_ve.h, stg_done[last_tile])
                tt.wait_ge(s_ac.h, dt_used[0])
                tt.wait_ge(s_wld.h, s_wld.v)
                for dc in range(8):
                    ins = tt.matmul(ps_dt[:, 0:n_tok], wdt_sb[:, dc, :], stg_buf[:, dc, :],
                                    start=(dc == 0), stop=(dc == 7))
                s_pe.inc(ins)
            block.tensor(t_dt)
            pe_dt = s_pe.v

            def a_sp(a):
                a.wait_ge(s_pe.h, pe_dt)
                a.wait_ge(s_pe.h, wfc_used_pe[0])
                a.wait_ge(s_st.h, grp_st[g] if g >= 0 else 0)
                a.activation(dta[:, 0:n_tok], ps_dt[:, 0:n_tok], AF.Abs, bias=dtb_sb[:, 0:1])
                a.activation(dta[:, 0:n_tok], dta[:, 0:n_tok], AF.Exp, scale=-1.0)
                a.activation(dta[:, 0:n_tok], dta[:, 0:n_tok], AF.Ln, bias=1.0)
                s_ac.inc(a.activation(dts[:, 0:n_tok], ps_dt[:, 0:n_tok], AF.Relu, bias=dtb_sb[:, 0:1]))
            block.scalar(a_sp)
            dt_used[0] = s_ac.v

            def v_dt(v):
                v.wait_ge(s_ac.h, dt_used[0])
                v.wait_ge(s_pe.h, wfc_used_pe[0])
                v.tensor_tensor(dts[:, 0:n_tok], dts[:, 0:n_tok], dta[:, 0:n_tok], ALU.add)
                v.tensor_copy(dts_bf[:, 0:n_tok], dts[:, 0:n_tok])
                v.tensor_scalar(dta[:, 0:n_tok], dts[:, 0:n_tok], nega_sb[:, 0:1], None, ALU.mult)
                s_ve.inc(v.tensor_tensor_scan(dtc[:, 0:n_tok], d0p[:, 0:n_tok],
                                              dta[:, 0:n_tok], 0.0, ALU.mult, ALU.add))
            block.vector(v_dt)
            cum_done[0] = s_ve.v

            if g >= 0:
                def gg_ca(gg):
                    gg.wait_ge(s_ve.h, cum_done[0])
                    s_st.inc(gg.dma_start(out=t["cumA_fm"][:, g * 512:(g + 1) * 512],
                                          in_=dtc[:, 0:512]), 16)
                block.gpsimd(gg_ca)

                def v_hl(v):
                    v.wait_ge(s_ve.h, cum_done[0])
                    v.wait_ge(s_pe.h, hl_used_pe[0])
                    v.tensor_copy(ca_hi[:, 0:512], dtc[:, 0:512])
                    s_ve.inc(v.tensor_tensor(ca_lo[:, 0:512], dtc[:, 0:512],
                                             ca_hi[:, 0:512], ALU.subtract))
                block.vector(v_hl)
                hl_ready[0] = s_ve.v

                def v_ecl(v):
                    v.wait_ge(s_ve.h, cum_done[0])
                    v.wait_ge(s_pe.h, e512_pe[0])
                    v.tensor_copy(dcl_sb[:, g * 8:(g + 1) * 8],
                                  dtc[:, 0:512].rearrange("h (c l) -> h c l", l=CH)[:, :, CH - 1])
                    s_ve.inc(v.tensor_copy(ecl[:, 0:8],
                                           dtc[:, 0:512].rearrange("h (c l) -> h c l", l=CH)[:, :, CH - 1]))
                block.vector(v_ecl)
                ecl_ve = s_ve.v

                def a_ecl(a):
                    a.wait_ge(s_ve.h, ecl_ve)
                    s_ac.inc(a.activation(ecl[:, 0:8], ecl[:, 0:8], AF.Exp))
                block.scalar(a_ecl)
                ecl_ac = s_ac.v

                def gg_ecl(gg):
                    gg.wait_ge(s_ac.h, ecl_ac)
                    gg.wait_ge(s_ve.h, eclbf_done[0])
                    base = s_ecl.v
                    gg.dma_start(out=t["ecl_d"][:, g * 8:(g + 1) * 8],
                                 in_=ecl[:, 0:8]).then_inc(s_ecl.h, 16)
                    gg.wait_ge(s_ecl.h, base + 16)
                    gg.dma_start(out=eclrow[:].rearrange("one (h c) -> one h c", c=8),
                                 in_=t["ecl_d"][:, g * 8:(g + 1) * 8].unsqueeze(0)
                                 ).then_inc(s_ecl.h, 16)
                block.gpsimd(gg_ecl)
                s_ecl.v += 32

                def v_eclbf(v):
                    v.wait_ge(s_ecl.h, s_ecl.v)
                    s_ve.inc(v.tensor_copy(eclrow_bf[:], eclrow[:]))
                block.vector(v_eclbf)
                eclbf_done[0] = s_ve.v

                def t_e512(tt):
                    tt.wait_ge(s_ve.h, eclbf_done[0])
                    tt.wait_ge(s_ve.h, ip_ev2[1])
                    s_pe.inc(tt.matmul(ps_ip[:, 1, 0:256], ones1[:], eclrow_bf[:],
                                       start=True, stop=True))
                block.tensor(t_e512)
                e512_pe[0] = s_pe.v

                def v_e512(v):
                    v.wait_ge(s_pe.h, e512_pe[0])
                    v.wait_ge(s_ve.h, scan_done[0])
                    s_ve.inc(v.tensor_copy(e512[:].rearrange("p h c -> p (h c)"),
                                           ps_ip[:, 1, 0:256]))
                block.vector(v_e512)
                ip_ev2[1] = s_ve.v
                e512_done[0] = s_ve.v

            emit_xb_chunks(g, n_tok, stg_buf, last_tile)
            grp_ip_pe[g + 1] = s_pe.v
            grp_st[g + 1] = s_st.v

        def emit_xb_chunks(g, n_tok, stg_buf, last_tile):
            for ec in range(17):
                par = ec % 2

                def t_ip(tt, ec=ec, par=par):
                    if ec == 0:
                        tt.wait_ge(s_ve.h, stg_done[last_tile])
                        tt.wait_ge(s_wld.h, s_wld.v)
                    tt.wait_ge(s_ve.h, ip_ev2[par])
                    for dc in range(8):
                        ins = tt.matmul(ps_ip[:, par, 0:n_tok],
                                        wxb_sb[:, dc, ec * 128:(ec + 1) * 128],
                                        stg_buf[:, dc, :], start=(dc == 0), stop=(dc == 7))
                    s_pe.inc(ins)
                block.tensor(t_ip)
                pe_ip = s_pe.v

                def v_pc(v, ec=ec, par=par):
                    v.wait_ge(s_pe.h, pe_ip)
                    v.wait_ge(s_ac.h, si_done[0])
                    if g < 0:
                        ins0 = v.memset(pc[:, 0:3], 0.0)
                    else:
                        ins0 = v.tensor_copy(pc[:, 0:3], carry[:, ec, :])
                    sv0 = s_ve.inc(ins0)
                    sv1 = s_ve.inc(v.tensor_copy(pc[:, 3:3 + n_tok], ps_ip[:, par, 0:n_tok]))
                    v.wait_ge(s_ve.h, sv1)
                    sv2 = s_ve.inc(v.tensor_copy(carry[:, ec, :], pc[:, n_tok:n_tok + 3]))
                    v.wait_ge(s_ve.h, sv2)
                    v.tensor_scalar(cvo[:, 0:n_tok], pc[:, 0:n_tok], cw_sb[:, ec, 0:1], None, ALU.mult)
                    for k in (1, 2, 3):
                        ins = v.scalar_tensor_tensor(cvo[:, 0:n_tok], pc[:, k:k + n_tok],
                                                     cw_sb[:, ec, k:k + 1], cvo[:, 0:n_tok],
                                                     ALU.mult, ALU.add)
                    s_ve.inc(ins)
                block.vector(v_pc)
                ip_ev2[par] = s_ve.v
                cv_done = s_ve.v

                if g >= 0:
                    def a_si(a, ec=ec, cv_done=cv_done):
                        a.wait_ge(s_ve.h, cv_done)
                        a.wait_ge(s_st.h, xcb_store[0])
                        s_ac.inc(a.activation(xcb_sb[:, 0:n_tok], cvo[:, 0:n_tok], AF.Silu,
                                              bias=cb_sb[:, ec:ec + 1]))
                    block.scalar(a_si)
                    si_done[0] = s_ac.v

                    def gg_xcb(gg, ec=ec):
                        gg.wait_ge(s_ac.h, si_done[0])
                        s_st.inc(gg.dma_start(out=t["xcb_fm"][ec * 128:(ec + 1) * 128,
                                                              g * 512:(g + 1) * 512],
                                              in_=xcb_sb[:, 0:512]), 16)
                    block.gpsimd(gg_xcb)
                    xcb_store[0] = s_st.v

        def emit_tm(g, sub):
            jj = 4 * g + sub + 1
            tcol = g * 512 + sub * 128
            par = (jj - 1) % 2

            def sy_tb(sy):
                sy.wait_ge(s_st.h, grp_st[g + 1])
                sy.wait_ge(s_ve.h, xbt_free[par])
                sy.dma_start_transpose(xbt[:, par], t["xcb_fm"][:, tcol:tcol + 128]).then_inc(s_tb.h, 16)
            block.sync(sy_tb)
            s_tb.v += 16
            tb_now = s_tb.v

            def t_s1(tt):
                tt.wait_ge(s_ve.h, wf_ready[g])
                tt.wait_ge(s_ve.h, evdc[0])
                tt.wait_ge(s_ve.h, tm_ev[0])
                s_pe.inc(tt.transpose(ps_tp[:, 0:H0], dts_bf[:, sub * 128:(sub + 1) * 128],
                                      idn[0:H0, 0:H0]))
            block.tensor(t_s1)
            pe1 = s_pe.v

            def v_dtm(v):
                v.wait_ge(s_pe.h, pe1)
                s_ve.inc(v.tensor_copy(dt_tm[:], ps_tp[:, 0:H0]))
            block.vector(v_dtm)
            ev1 = s_ve.v

            def t_s2(tt):
                tt.wait_ge(s_ve.h, ev1)
                s_pe.inc(tt.transpose(ps_tp[:, 0:H0], wfc_bf[:, sub * 128:(sub + 1) * 128],
                                      idn[0:H0, 0:H0]))
            block.tensor(t_s2)
            pe2 = s_pe.v
            wfc_used_pe[0] = pe2

            def v_wtm(v):
                v.wait_ge(s_pe.h, pe2)
                s_ve.inc(v.tensor_copy(wf_tm[:], ps_tp[:, 0:H0]))
            block.vector(v_wtm)

            def t_s3(tt, hv=s_ve.v):
                tt.wait_ge(s_ve.h, hl_ready[0])
                tt.wait_ge(s_ve.h, hv)
                s_pe.inc(tt.transpose(ps_tp[:, 0:H0], ca_hi[:, sub * 128:(sub + 1) * 128],
                                      idn[0:H0, 0:H0]))
            block.tensor(t_s3)
            pe3 = s_pe.v

            def v_hi(v):
                v.wait_ge(s_pe.h, pe3)
                s_ve.inc(v.tensor_copy(ca_his[:], ps_tp[:, 0:H0]))
            block.vector(v_hi)
            hi_ve = s_ve.v

            def t_s4(tt):
                tt.wait_ge(s_ve.h, hi_ve)
                s_pe.inc(tt.transpose(ps_tp[:, 0:H0], ca_lo[:, sub * 128:(sub + 1) * 128],
                                      idn[0:H0, 0:H0]))
            block.tensor(t_s4)
            pe4 = s_pe.v
            hl_used_pe[0] = pe4

            def v_lo(v):
                v.wait_ge(s_pe.h, pe4)
                v.wait_ge(s_st.h, cat_st[0])
                s_ve.inc(v.tensor_tensor(cat_tm[:], ca_his[:], ps_tp[:, 0:H0], ALU.add))
            block.vector(v_lo)
            cat_ve = s_ve.v

            def gg_cat(gg):
                gg.wait_ge(s_ve.h, cat_ve)
                s_st.inc(gg.dma_start(out=t["cumA_tm"][tcol:tcol + 128, :], in_=cat_tm[:]), 16)
            block.gpsimd(gg_cat)
            cat_st[0] = s_st.v

            tm_ev[0] = s_ve.v
            evdc[0] = s_ve.v

            def v_xdt(v):
                v.wait_ge(s_tb.h, tb_now)
                v.wait_ge(s_st.h, xdt_st[0])
                xv = _hq(xbt[:, par, 0:DI], P0)
                v.tensor_tensor(_hq(xdt_t[:], P0), xv,
                                _b(dt_tm[:].unsqueeze(2), [128, H0, P0]), ALU.mult)
                s_ve.inc(v.tensor_tensor(_hq(xw_t[:], P0), _hq(xdt_t[:], P0),
                                         _b(wf_tm[:].unsqueeze(2), [128, H0, P0]),
                                         ALU.mult))
            block.vector(v_xdt)
            xdt_done = s_ve.v

            def gg_xdt(gg):
                gg.wait_ge(s_ve.h, xdt_done)
                gg.dma_start(out=t["xdt_tm"][tcol:tcol + 128, :], in_=xdt_t[:]).then_inc(s_st.h, 16)
                gg.dma_start(out=t["xw_tm"][tcol:tcol + 128, :], in_=xw_t[:]).then_inc(s_st.h, 16)
            block.gpsimd(gg_xdt)
            s_st.v += 32
            xdt_st[0] = s_st.v

            for cc in range(2):
                coff = cc * 64
                ci = (sub * 2 + cc)

                def t_cs(tt, coff=coff):
                    tt.wait_ge(s_ve.h, xdt_done)
                    tt.wait_ge(s_ve.h, scan_done[0])
                    for nn in range(4):
                        ins = tt.matmul(ps_cs[:, nn * 512:(nn + 1) * 512],
                                        xbt[coff:coff + 64, par, DI:DI + DS],
                                        xw_t[coff:coff + 64, nn * 512:(nn + 1) * 512],
                                        start=True, stop=True)
                    s_pe.inc(ins)
                block.tensor(t_cs)
                pe_cs = s_pe.v

                def v_scan(v, ci=ci, pe_cs=pe_cs):
                    v.wait_ge(s_pe.h, pe_cs)
                    v.wait_ge(s_ve.h, e512_done[0])
                    v.tensor_tensor(_hq(S[:], P0), _hq(S[:], P0),
                                    _b(e512[:, :, ci:ci + 1], [DS, H0, P0]), ALU.mult)
                    s_ve.inc(v.tensor_tensor(S[:], S[:], ps_cs[:], ALU.add))
                block.vector(v_scan)
                scan_done[0] = s_ve.v
            xbt_free[par] = scan_done[0]

        emit_tile(0)
        emit_group(-1, 128, stgh[:], 0)
        for g in range(8):
            for sub in range(4):
                emit_tile(4 * g + sub + 1)
            emit_group(g, 512, stg[:, g % 2], 128 + g * 512)

            def a_wf(a, g=g):
                a.wait_ge(s_ve.h, cum_done[0])
                a.wait_ge(s_pe.h, wfc_used_pe[0])
                for c in range(8):
                    cl = dtc[:, c * 64 + 63:c * 64 + 64]
                    ins = a.activation(wfc[:, c * 64:(c + 1) * 64], dtc[:, c * 64:(c + 1) * 64],
                                       AF.Exp, scale=-1.0, bias=cl)
                s_ac.inc(ins)
            block.scalar(a_wf)
            wf_ac = s_ac.v

            def v_wf(v, wf_ac=wf_ac):
                v.wait_ge(s_ac.h, wf_ac)
                v.wait_ge(s_pe.h, wfc_used_pe[0])
                s_ve.inc(v.tensor_copy(wfc_bf[:], wfc[:]))
            block.vector(v_wf)
            wf_ready[g] = s_ve.v

            for sub in range(4):
                emit_tm(g, sub)

        def gg_sloc(gg):
            gg.wait_ge(s_ve.h, scan_done[0])
            s_st.inc(gg.dma_start(out=t["S_loc"], in_=S[:]), 16)
        block.gpsimd(gg_sloc)

        def v_dl(v):
            v.wait_ge(s_ve.h, scan_done[0])
            s_ve.inc(v.reduce_sum(dta[:, 0:1], dcl_sb[:], axis=X))
        block.vector(v_dl)
        dl_ve = s_ve.v

        def gg_fin(gg):
            gg.wait_ge(s_ve.h, dl_ve)
            s_st.inc(gg.dma_start(out=t["D_loc"], in_=dta[:, 0:1]), 16)
            gg.wait_ge(s_st.h, s_st.v)
            gg.wait_ge(s_tb.h, s_tb.v)
            gg.wait_ge(s_uld.h, s_uld.v)
            gg.wait_ge(s_ecl.h, s_ecl.v)
        block.gpsimd(gg_fin)

    return nc



def _tr(a):
    return np.ascontiguousarray(np.asarray(a).T)


def prep_l0a_inputs(inputs):
    u = inputs["L0_attn_u"]
    W = inputs["L0_attn_in_proj_w"]
    wtxb = _tr(W[DI:DI + CONVD])
    wtdt = _tr(W[4352:4384])
    cw = np.ascontiguousarray(inputs["L0_attn_conv_w"][:CONVD, 0, :])
    cb = np.ascontiguousarray(inputs["L0_attn_conv_b"][:CONVD].reshape(17, 128).T)
    dtb = np.ascontiguousarray(inputs["L0_attn_dt_bias"][:, None])
    alog = np.ascontiguousarray(inputs["L0_attn_A_log"][:, None])
    maps = []
    for c in range(NCORES):
        b, q = divmod(c, 4)
        t0 = q * T
        useg = np.ascontiguousarray(u[b, t0:t0 + T])
        if q == 0:
            uh = np.zeros((128, D), np.float32)
        else:
            uh = np.ascontiguousarray(u[b, t0 - 128:t0])
        maps.append(dict(u=useg, uh=uh, wtxb=wtxb, wtdt=wtdt, cw=cw, cb=cb,
                         dtb=dtb, alog=alog))
    return maps



def build_moe():
    nc = bass.Bass("TRN2", target_bir_lowering=False, detect_race_conditions=False)
    tn = {}
    def inp(name, shape, dt=F32):
        tn[name] = nc.dram_tensor(name, shape, dt, kind="ExternalInput").ap()
    inp("xn_fm", [D, 128 + T], BF16)
    inp("router", [D, 8])
    inp("w1", [8, D, 2816]); inp("v1", [8, D, 2816]); inp("w2", [8, 2816, D])
    inp("bias_rep", [128, D])
    tn["g_out"] = nc.dram_tensor("g_out", [T, D], BF16, kind="ExternalOutput").ap()
    tn["gacc"] = nc.dram_tensor("gacc", [T, D], F32, kind="Internal").ap()
    emit_moe(nc, tn)
    return nc


def emit_moe(nc, t):
    FC = 22
    s_ld = Sem(nc, "m_ld")
    s_wl = Sem(nc, "m_wl")
    s_pe = Sem(nc, "m_pe")
    s_ve = Sem(nc, "m_ve")
    s_ac = Sem(nc, "m_ac")
    s_ga = Sem(nc, "m_ga")

    import contextlib
    with contextlib.ExitStack() as _stk:
        _pfx = f"s{_NAME_SEQ[0]}_"
        _NAME_SEQ[0] += 1
        def _sb(nm, *a):
            return _stk.enter_context(nc.sbuf_tensor(_pfx + nm, *a))
        def _ps(nm, *a):
            return _stk.enter_context(nc.psum_tensor(_pfx + nm, *a))
        wr_sb = _sb("wr_sb", [128, 8, 8], BF16)
        w1e = _sb("w1e", [128, 8, 2816], BF16)
        v1e = _sb("v1e", [128, 8, 2816], BF16)
        w2e = _sb("w2e", [128, FC, D], BF16)
        xnt = _sb("xnt", [128, 2, 8, 512], BF16)
        h_sb = _sb("h_sb", [128, FC, 512], BF16)
        gtmp = _sb("gtmp", [128, 2, 512], F32)
        ysb = _sb("ysb", [128, 2, D], F32)
        scw = _sb("scw", [128, 256], F32)
        sccol = _sb("sccol", [128, 16], F32)
        scex = _sb("scex", [128, 8], F32)
        bias_sb = _sb("bias_sb", [128, D], F32)
        fin = _sb("fin", [128, D], F32)
        gfin = _sb("gfin", [128, D], BF16)
        ps_h1 = _ps("ps_h1", [128, 2, 512], F32)
        ps_h2 = _ps("ps_h2", [128, 2, 512], F32)
        ps_y = _ps("ps_y", [128, 2, D], F32)
        ps_r = ps_h1[:, 0, 0:8]
        block = _stk.enter_context(nc.Block())

        xn_r = t["xn_fm"].rearrange("(c p) tt -> p c tt", p=128)

        def g_init(g):
            g.dma_start(out=wr_sb[:], in_=t["router"].rearrange("(c p) e -> p c e", p=128)).then_inc(s_wl.h, 16)
            g.dma_start(out=bias_sb[:], in_=t["bias_rep"]).then_inc(s_wl.h, 16)
        block.gpsimd(g_init)
        s_wl.v += 32
        winit = s_wl.v

        sc_ve = [0]
        sc_pe = [0]
        for tj in range(32):
            par = tj % 2

            def sy_x(sy, tj=tj, par=par):
                if tj >= 2:
                    sy.wait_ge(s_pe.h, sc_pe[0])
                sy.dma_start(out=xnt[:, par, :, 0:128],
                             in_=xn_r[:, :, 128 + tj * 128:128 + tj * 128 + 128]
                             ).then_inc(s_ld.h, 16)
            block.sync(sy_x)
            s_ld.v += 16
            ld_now = s_ld.v

            def t_r(tt, par=par, ld_now=ld_now):
                tt.wait_ge(s_ld.h, ld_now)
                tt.wait_ge(s_wl.h, winit)
                if sc_ve[0]:
                    tt.wait_ge(s_ve.h, sc_ve[0])
                for dc in range(8):
                    ins = tt.matmul(ps_r, xnt[:, par, dc, 0:128], wr_sb[:, dc, :],
                                    start=(dc == 0), stop=(dc == 7))
                s_pe.inc(ins)
            block.tensor(t_r)
            sc_pe[0] = s_pe.v
            pe_now = s_pe.v

            def v_m1(v, pe_now=pe_now):
                v.wait_ge(s_pe.h, pe_now)
                s_ve.inc(v.reduce_max(sccol[:, 0:1], ps_r, axis=X, negate=True))
            block.vector(v_m1)
            m1_ve = s_ve.v

            def a_ex(a, m1_ve=m1_ve):
                a.wait_ge(s_ve.h, m1_ve)
                s_ac.inc(a.activation(scex[:], ps_r, AF.Exp, bias=sccol[:, 0:1]))
            block.scalar(a_ex)
            ex_ac = s_ac.v

            def v_sm(v, tj=tj, ex_ac=ex_ac):
                v.wait_ge(s_ac.h, ex_ac)
                sv = s_ve.inc(v.reduce_sum(sccol[:, 1:2], scex[:], axis=X))
                v.wait_ge(s_ve.h, sv)
                sv = s_ve.inc(v.reciprocal(sccol[:, 2:3], sccol[:, 1:2]))
                v.wait_ge(s_ve.h, sv)
                sv = s_ve.inc(v.tensor_scalar(sccol[:, 4:12], scex[:], sccol[:, 2:3], None, ALU.mult))
                v.wait_ge(s_ve.h, sv)
                sv = s_ve.inc(v.reduce_max(sccol[:, 3:4], sccol[:, 4:12], axis=X))
                v.wait_ge(s_ve.h, sv)
                sv = s_ve.inc(v.tensor_scalar(scex[:], sccol[:, 4:12], sccol[:, 3:4], None, ALU.is_ge))
                v.wait_ge(s_ve.h, sv)
                sv = s_ve.inc(v.scalar_tensor_tensor(scex[:], scex[:], -1e30, sccol[:, 4:12],
                                                     ALU.mult, ALU.add))
                v.wait_ge(s_ve.h, sv)
                sv = s_ve.inc(v.reduce_max(sccol[:, 13:14], scex[:], axis=X))
                v.wait_ge(s_ve.h, sv)
                sv = s_ve.inc(v.tensor_scalar(scex[:], sccol[:, 4:12], sccol[:, 13:14], None, ALU.is_ge))
                v.wait_ge(s_ve.h, sv)
                sv = s_ve.inc(v.tensor_tensor(scw[:, tj * 8:(tj + 1) * 8], scex[:],
                                              sccol[:, 4:12], ALU.mult))
                v.wait_ge(s_ve.h, sv)
            block.vector(v_sm)
            sc_ve[0] = s_ve.v

        scores_done = sc_ve[0]
        h_pe = [0]
        y_pe = [0]
        hev_ve = [0, 0]
        hev_ac = [0, 0]
        yev_ve = [0, 0]
        ysb_free = [0, 0]
        ga_tile = [0] * 32
        wl_w1v1 = [0]
        wl_w2 = [0]
        xn_ld = [s_pe.v, s_pe.v]

        def load_w1v1(e):
            def g_w(g, e=e, hp=h_pe[0]):
                g.wait_ge(s_pe.h, hp)
                g.dma_start(out=w1e[:], in_=t["w1"][e].rearrange("(c p) f -> p c f", p=128)).then_inc(s_wl.h, 16)
                g.dma_start(out=v1e[:], in_=t["v1"][e].rearrange("(c p) f -> p c f", p=128)).then_inc(s_wl.h, 16)
            block.gpsimd(g_w)
            s_wl.v += 32
            wl_w1v1[0] = s_wl.v

        def load_w2(e):
            def g_w(g, e=e, yp=y_pe[0]):
                g.wait_ge(s_pe.h, yp)
                g.dma_start(out=w2e[:], in_=t["w2"][e].rearrange("(c p) d -> p c d", p=128)).then_inc(s_wl.h, 16)
            block.gpsimd(g_w)
            s_wl.v += 16
            wl_w2[0] = s_wl.v

        load_w1v1(0)
        load_w2(0)
        for e in range(8):
            we_h = wl_w1v1[0]
            we_y = wl_w2[0]
            for ti in range(8):
                par = ti % 2

                def sy_x2(sy, ti=ti, par=par, xl=xn_ld[par]):
                    sy.wait_ge(s_pe.h, xl)
                    sy.dma_start(out=xnt[:, par],
                                 in_=xn_r[:, :, 128 + ti * 512:128 + (ti + 1) * 512]
                                 ).then_inc(s_ld.h, 16)
                block.sync(sy_x2)
                s_ld.v += 16
                ld_now = s_ld.v

                for fc in range(FC):
                    fcp = fc % 2

                    def t_h(tt, fc=fc, fcp=fcp, par=par, ld_now=ld_now, we_h=we_h,
                            ha=hev_ac[fcp], hv=hev_ve[fcp],
                            sd=(scores_done if (e == 0 and ti == 0 and fc == 0) else 0)):
                        if fc == 0:
                            tt.wait_ge(s_ld.h, ld_now)
                            tt.wait_ge(s_wl.h, we_h)
                        if sd:
                            tt.wait_ge(s_ve.h, sd)
                        if ha:
                            tt.wait_ge(s_ac.h, ha)
                        for dc in range(8):
                            ins = tt.matmul(ps_h1[:, fcp], w1e[:, dc, fc * 128:(fc + 1) * 128],
                                            xnt[:, par, dc, :], start=(dc == 0), stop=(dc == 7))
                        s_pe.inc(ins)
                        if hv:
                            tt.wait_ge(s_ve.h, hv)
                        for dc in range(8):
                            ins = tt.matmul(ps_h2[:, fcp], v1e[:, dc, fc * 128:(fc + 1) * 128],
                                            xnt[:, par, dc, :], start=(dc == 0), stop=(dc == 7))
                        s_pe.inc(ins)
                    block.tensor(t_h)
                    hm_pe = s_pe.v

                    def a_g(a, fcp=fcp, hm_pe=hm_pe):
                        a.wait_ge(s_pe.h, hm_pe - 1)
                        s_ac.inc(a.activation(gtmp[:, fcp], ps_h1[:, fcp], AF.Gelu))
                    block.scalar(a_g)
                    hev_ac[fcp] = s_ac.v
                    gac = s_ac.v

                    def v_h(v, fc=fc, fcp=fcp, gac=gac, hm_pe=hm_pe):
                        v.wait_ge(s_ac.h, gac)
                        v.wait_ge(s_pe.h, hm_pe)
                        s_ve.inc(v.tensor_tensor(h_sb[:, fc, :], gtmp[:, fcp],
                                                 ps_h2[:, fcp], ALU.mult))
                    block.vector(v_h)
                    hev_ve[fcp] = s_ve.v
                h_pe[0] = s_pe.v
                h_ve = s_ve.v
                xn_ld[par] = s_pe.v

                if ti == 7 and e < 7:
                    load_w1v1(e + 1)

                for tq in range(4):
                    ypar = tq % 2

                    def t_y(tt, tq=tq, ypar=ypar, h_ve=h_ve, we_y=we_y, yv=yev_ve[ypar]):
                        if tq == 0:
                            tt.wait_ge(s_ve.h, h_ve)
                            tt.wait_ge(s_wl.h, we_y)
                        tt.wait_ge(s_ve.h, yv)
                        for fc in range(FC):
                            for dh in range(2):
                                ins = tt.matmul(ps_y[:, ypar, dh * 512:(dh + 1) * 512],
                                                h_sb[:, fc, tq * 128:(tq + 1) * 128],
                                                w2e[:, fc, dh * 512:(dh + 1) * 512],
                                                start=(fc == 0), stop=(fc == FC - 1))
                        s_pe.inc(ins)
                    block.tensor(t_y)
                    y_pe[0] = s_pe.v
                    ym_pe = s_pe.v

                    def v_y(v, ti=ti, tq=tq, ypar=ypar, e=e, ym_pe=ym_pe, yf=ysb_free[ypar]):
                        v.wait_ge(s_pe.h, ym_pe)
                        v.wait_ge(s_ga.h, yf)
                        col = (ti * 4 + tq) * 8 + e
                        s_ve.inc(v.tensor_scalar(ysb[:, ypar], ps_y[:, ypar], scw[:, col:col + 1],
                                                 None, ALU.mult))
                    block.vector(v_y)
                    yev_ve[ypar] = s_ve.v
                    yv2 = s_ve.v

                    def g_acc(g, ti=ti, tq=tq, ypar=ypar, e=e, yv2=yv2, gt=ga_tile[ti * 4 + tq]):
                        g.wait_ge(s_ve.h, yv2)
                        g.wait_ge(s_ga.h, gt)
                        tile = ti * 4 + tq
                        ins = g.dma_start(out=t["gacc"][tile * 128:(tile + 1) * 128, :],
                                          in_=ysb[:, ypar],
                                          accum_op=(ALU.bypass if e == 0 else ALU.add))
                        s_ga.inc(ins, 16)
                    block.gpsimd(g_acc)
                    ga_tile[ti * 4 + tq] = s_ga.v
                    ysb_free[ypar] = s_ga.v
            load_w2(e + 1) if e < 7 else None

        fin_free = [0]
        gfin_free = [0]
        for tk in range(32):
            def sy_f(sy, tk=tk, ff=fin_free[0]):
                sy.wait_ge(s_ga.h, ga_tile[tk])
                if ff:
                    sy.wait_ge(s_ac.h, ff)
                sy.dma_start(out=fin[:],
                             in_=t["gacc"][tk * 128:(tk + 1) * 128, :]).then_inc(s_ld.h, 16)
            block.sync(sy_f)
            s_ld.v += 16
            fl = s_ld.v

            def v_f(v, fl=fl):
                v.wait_ge(s_ld.h, fl)
                s_ve.inc(v.tensor_tensor(fin[:], fin[:], bias_sb[:], ALU.add))
            block.vector(v_f)
            add_ve = s_ve.v

            def a_f(a, add_ve=add_ve, gf=gfin_free[0]):
                a.wait_ge(s_ve.h, add_ve)
                if gf:
                    a.wait_ge(s_ga.h, gf)
                s_ac.inc(a.activation(gfin[:], fin[:], AF.Silu))
            block.scalar(a_f)
            fin_free[0] = s_ac.v
            si_ac = s_ac.v

            def g_f(g, tk=tk, si_ac=si_ac):
                g.wait_ge(s_ac.h, si_ac)
                ins = g.dma_start(out=t["g_out"][tk * 128:(tk + 1) * 128, :], in_=gfin[:])
                s_ga.inc(ins, 16)
            block.gpsimd(g_f)
            gfin_free[0] = s_ga.v

        def g_end(g):
            g.wait_ge(s_ga.h, s_ga.v)
            g.wait_ge(s_ld.h, s_ld.v)
            g.wait_ge(s_wl.h, s_wl.v)
        block.gpsimd(g_end)

    return nc


def prep_moe_inputs(inputs, xn_fms):
    router = np.ascontiguousarray(inputs["L0_ffn_router_w"])
    w1 = inputs["L0_ffn_w1"]; v1 = inputs["L0_ffn_v1"]; w2 = inputs["L0_ffn_w2"]
    bias_rep = np.ascontiguousarray(np.broadcast_to(inputs["L0_ffn_bias"][None, :], (128, D)))
    return [dict(xn_fm=xn_fms[c], router=router, w1=w1, v1=v1, w2=w2,
                 bias_rep=bias_rep) for c in range(NCORES)]



def build_l0b():
    nc = bass.Bass("TRN2", target_bir_lowering=False, detect_race_conditions=False)
    tn = {}
    def inp(name, shape, dt=F32):
        tn[name] = nc.dram_tensor(name, shape, dt, kind="ExternalInput").ap()
    inp("xn_fm", [D, 128 + T], BF16)
    inp("xcb_fm", [CONVD, T], BF16)
    inp("xdt_tm", [T, DI], BF16)
    inp("xw_tm", [T, DI], BF16)
    inp("cumA_fm", [H0, T])
    inp("cumA_tm", [T, H0])
    inp("S_all", [8, DS, DI])
    inp("D_tall", [8, H0])
    inp("Mt", [8, 8])
    inp("pm_rep", [H0, 8])
    inp("wtz", [D, DI]); inp("wtc", [D, DS])
    inp("cw_c", [DS, 4]); inp("cb_c", [DS, 1])
    inp("d_rep", [128, H0])
    inp("nw_rep", [128, DI])
    inp("wto", [DI, D])
    tn["s_tm"] = nc.dram_tensor("s_tm", [T, D], F32, kind="ExternalOutput").ap()
    tn["tg_d"] = nc.dram_tensor("tg_d", [T, DI], BF16, kind="Internal").ap()
    tn["ecl_d"] = nc.dram_tensor("ecl_d2", [H0, NCHUNK], F32, kind="Internal").ap()
    tn["f_d"] = nc.dram_tensor("f_d", [H0, 8], F32, kind="Internal").ap()
    tn["id128"] = nc.inline_tensor(np.eye(128, dtype=BF), name="id128b").ap()
    tn["ones1"] = nc.inline_tensor(np.ones((1, 128), dtype=BF), name="ones1b").ap()
    tn["ones1f"] = nc.inline_tensor(np.ones((1, 64), np.float32), name="ones1f_c").ap()
    tn["mask_su"] = nc.inline_tensor(np.triu(np.ones((64, 64), np.float32)), name="mask_su_c").ap()
    tn["cvals"] = nc.inline_tensor(np.tile(np.array([EPS, 1.0 / DI], np.float32), (128, 1)),
                                   name="cvalsb").ap()
    emit_l0b(nc, tn)
    return nc


def emit_l0b(nc, t):
    s_ld = Sem(nc, "b_ld")
    s_tb = Sem(nc, "b_tb")
    s_wl = Sem(nc, "b_wl")
    s_pe = Sem(nc, "b_pe")
    s_ve = Sem(nc, "b_ve")
    s_ac = Sem(nc, "b_ac")
    s_po = Sem(nc, "b_po")
    s_st = Sem(nc, "b_st")

    import contextlib
    with contextlib.ExitStack() as _stk:
        _pfx = f"s{_NAME_SEQ[0]}_"
        _NAME_SEQ[0] += 1
        def _sb(nm, *a):
            return _stk.enter_context(nc.sbuf_tensor(_pfx + nm, *a))
        def _ps(nm, *a):
            return _stk.enter_context(nc.psum_tensor(_pfx + nm, *a))
        wz_sb = _sb("wz_sb", [128, 8, DI], BF16)
        wc_sb = _sb("wc_sb", [128, 8, DS], BF16)
        ones1 = _sb("ones1", [1, 128], BF16)
        ones1f = _sb("ones1f", [1, 64], F32)
        msu = _sb("msu", [64, 64], F32)
        cst = _sb("cst", [128, 2], F32)
        cwc_sb = _sb("cwc_sb", [128, 4], F32)
        cbc_sb = _sb("cbc_sb", [128, 1], F32)
        drep = _sb("drep", [128, H0], F32)
        nwrep = _sb("nwrep", [128, DI], F32)
        pmrep = _sb("pmrep", [H0, 8], F32)
        dtall = _sb("dtall", [8, H0], F32)
        mt_sb = _sb("mt_sb", [8, 8], F32)
        fh_sb = _sb("fh_sb", [H0, 8], F32)
        fbc = _sb("fbc", [128, H0, 8], F32)
        S = _sb("S", [DS, DI], F32)
        S_bf = _sb("S_bf", [DS, DI], BF16)
        xnt = _sb("xnt", [128, 2, 8, 512], BF16)
        ca_sb = _sb("ca_sb", [H0, 512], F32)
        ecl = _sb("ecl", [H0, 8], F32)
        eclrow = _sb("eclrow", [1, 256], F32)
        eclrow_bf = _sb("eclrow_bf", [1, 256], BF16)
        e512 = _sb("e512", [128, H0, 8], F32)
        car = _sb("car", [1, DI], F32)
        erow_bf = _sb("erow_bf", [1, DI], BF16)
        c_sb = _sb("c_sb", [128, 512], BF16)
        b_sb = _sb("b_sb", [128, 512], BF16)
        pcc = _sb("pcc", [128, 515], F32)
        carc = _sb("carc", [128, 3], F32)
        cvoc = _sb("cvoc", [128, 512], F32)
        z_sb = _sb("z_sb", [128, DI], F32)
        xbt = _sb("xbt", [128, 2176], BF16)
        xdtc = _sb("xdtc", [64, DI], BF16)
        xw_t = _sb("xw_t", [128, DI], BF16)
        cat_tm = _sb("cat_tm", [128, H0], F32)
        b2_sb = _sb("b2_sb", [64, DI], F32)
        d_sb = _sb("d_sb", [64, DI], F32)
        gm_sb = _sb("gm_sb", [64, 64], F32)
        rhs_bf = _sb("rhs_bf", [64, DI], BF16)
        cs_bf = _sb("cs_bf", [128, H0, 64], BF16)
        y_t = _sb("y_t", [128, DI], F32)
        t1_t = _sb("t1_t", [128, DI], F32)
        sq_t = _sb("sq_t", [128, DI], F32)
        tg_sb = _sb("tg_sb", [128, DI], BF16)
        cols = _sb("cols", [128, 4], F32)
        tgf = _sb("tgf", [128, 16, 512], BF16)
        s_outb = _sb("s_outb", [128, D], F32)
        ps_a = _ps("ps_a", [128, DI], F32)
        ps_b = _ps("ps_b", [128, DI], F32)
        block = _stk.enter_context(nc.Block())

        nc.const_aps.register(F32, EPS, cst[:, 0:1])
        nc.const_aps.register(F32, 1.0 / DI, cst[:, 1:2])

        xn_r = t["xn_fm"].rearrange("(c p) tt -> p c tt", p=128)
        ca_r = t["cumA_fm"]

        def g_init(g):
            for o_, i_ in [(wz_sb[:], t["wtz"].rearrange("(c p) e -> p c e", p=128)),
                           (wc_sb[:], t["wtc"].rearrange("(c p) e -> p c e", p=128)),
                           (ones1[:], t["ones1"]),
                           (ones1f[:], t["ones1f"]), (msu[:], t["mask_su"]),
                           (cst[:], t["cvals"]), (cwc_sb[:], t["cw_c"]),
                           (cbc_sb[:], t["cb_c"]), (drep[:], t["d_rep"]),
                           (nwrep[:], t["nw_rep"]), (pmrep[:], t["pm_rep"]),
                           (dtall[:], t["D_tall"]), (mt_sb[:], t["Mt"])]:
                g.dma_start(out=o_, in_=i_).then_inc(s_wl.h, 16)
        block.gpsimd(g_init)
        s_wl.v += 16 * 13
        winit = s_wl.v

        def t_f(tt):
            tt.wait_ge(s_wl.h, winit)
            s_pe.inc(tt.matmul(ps_b[0:H0, 0:8], dtall[:], mt_sb[:], start=True, stop=True))
        block.tensor(t_f)
        f_pe = s_pe.v

        def a_f(a):
            a.wait_ge(s_pe.h, f_pe)
            s_ac.inc(a.activation(fh_sb[:], ps_b[0:H0, 0:8], AF.Exp))
        block.scalar(a_f)
        f_ac = s_ac.v

        def v_f(v):
            v.wait_ge(s_ac.h, f_ac)
            sv = s_ve.inc(v.tensor_tensor(fh_sb[:], fh_sb[:], pmrep[:], ALU.mult))
            v.wait_ge(s_ve.h, sv)
        block.vector(v_f)
        f_ve = s_ve.v

        def g_fd(g):
            g.wait_ge(s_ve.h, f_ve)
            base = s_wl.v
            g.dma_start(out=t["f_d"], in_=fh_sb[:]).then_inc(s_wl.h, 16)
            g.wait_ge(s_wl.h, base + 16)
            g.dma_start(out=eclrow[0:1, 0:256],
                        in_=t["f_d"].rearrange("h j -> (h j)").unsqueeze(0)).then_inc(s_wl.h, 16)
        block.gpsimd(g_fd)
        s_wl.v += 32
        frow_ld = s_wl.v

        def v_frow(v):
            v.wait_ge(s_wl.h, frow_ld)
            sv = s_ve.inc(v.tensor_copy(eclrow_bf[0:1, 0:256], eclrow[0:1, 0:256]))
            v.wait_ge(s_ve.h, sv)
        block.vector(v_frow)
        frow_ve = s_ve.v

        def t_fb(tt):
            tt.wait_ge(s_ve.h, frow_ve)
            s_pe.inc(tt.matmul(ps_a[:, 0:256], ones1[:], eclrow_bf[0:1, 0:256],
                               start=True, stop=True))
        block.tensor(t_fb)
        fb_pe = s_pe.v

        def v_fbc(v):
            v.wait_ge(s_pe.h, fb_pe)
            v.tensor_copy(fbc[:].rearrange("p h j -> p (h j)"), ps_a[:, 0:256])
            s_ve.inc(v.memset(S[:], 0.0))
        block.vector(v_fbc)
        sin_ve = [s_ve.v]

        for j in range(8):
            def sy_sj(sy, j=j, prev=sin_ve[0]):
                sy.wait_ge(s_ve.h, prev)
                sy.dma_start(out=y_t[:], in_=t["S_all"][j]).then_inc(s_ld.h, 16)
            block.sync(sy_sj)
            s_ld.v += 16
            sj_ld = s_ld.v

            def v_sj(v, j=j, sj_ld=sj_ld):
                v.wait_ge(s_ld.h, sj_ld)
                v.tensor_tensor(_hq(y_t[0:DS, :], P0), _hq(y_t[0:DS, :], P0),
                                _b(fbc[:, :, j:j + 1], [DS, H0, P0]), ALU.mult)
                s_ve.inc(v.tensor_tensor(S[:], S[:], y_t[0:DS, :], ALU.add))
            block.vector(v_sj)
            sin_ve[0] = s_ve.v

        def v_sbf(v, p=sin_ve[0]):
            v.wait_ge(s_ve.h, p)
            s_ve.inc(v.tensor_copy(S_bf[:], S[:]))
        block.vector(v_sbf)
        scan_ve = [s_ve.v]

        xn_free = [s_pe.v, s_pe.v]
        ca_used = [0]
        ecl_free = [0]
        eclbf_ve = [0]
        e512_pe = [0]
        e512_ve = [0]
        psa_free = [s_ve.v]
        psb_yfree = [0]
        d_ve = [0]
        carc_ve = [0]
        sic_ac = [0]
        csb_pe = [0]
        csb_ve = [0]
        car_pe = [0]
        car_ac = [0]
        gm_ve = [0]
        rhs_ve = [0]
        cs_ve = [0]
        xbt_users = [0, 0]
        xdt_users = [0]
        z_ac = [0]
        z_users = [0]
        yt_done = [0]
        gat_ve = [0]
        tg_st = [0]
        cat_users = [0]
        erow_pe = [0]
        xdtc_pe = [0]

        def sy_h(sy):
            sy.dma_start(out=xnt[:, 0], in_=xn_r[:, :, 0:512]).then_inc(s_ld.h, 16)
        block.sync(sy_h)
        s_ld.v += 16
        hl = s_ld.v

        def t_ch(tt):
            tt.wait_ge(s_ld.h, hl)
            tt.wait_ge(s_wl.h, winit)
            tt.wait_ge(s_ve.h, psa_free[0])
            for dc in range(8):
                ins = tt.matmul(ps_a[0:DS, 0:128], wc_sb[:, dc, :], xnt[:, 0, dc, 0:128],
                                start=(dc == 0), stop=(dc == 7))
            s_pe.inc(ins)
        block.tensor(t_ch)
        pe_ch = s_pe.v
        xn_free[0] = s_pe.v

        def v_ch(v):
            v.wait_ge(s_pe.h, pe_ch)
            sv = s_ve.inc(v.tensor_copy(pcc[:, 3:131], ps_a[0:DS, 0:128]))
            v.wait_ge(s_ve.h, sv)
            sv = s_ve.inc(v.tensor_copy(carc[:], pcc[:, 128:131]))
            v.wait_ge(s_ve.h, sv)
        block.vector(v_ch)
        carc_ve[0] = s_ve.v
        psa_free[0] = s_ve.v

        for g in range(8):
            par = g % 2

            def sy_xn(sy, g=g, par=par, xf=xn_free[par]):
                sy.wait_ge(s_pe.h, xf)
                sy.dma_start(out=xnt[:, par],
                             in_=xn_r[:, :, 128 + g * 512:128 + (g + 1) * 512]).then_inc(s_ld.h, 16)
            block.sync(sy_xn)
            s_ld.v += 16
            xl = s_ld.v

            def sy_ca(sy, g=g, cu=ca_used[0]):
                if cu:
                    sy.wait_ge(s_ve.h, cu)
                sy.dma_start(out=ca_sb[:], in_=ca_r[:, g * 512:(g + 1) * 512]).then_inc(s_ld.h, 16)
            block.sync(sy_ca)
            s_ld.v += 16
            cal = s_ld.v

            def v_ecl(v, cal=cal, ef=ecl_free[0]):
                v.wait_ge(s_ld.h, cal)
                if ef:
                    v.wait_ge(s_wl.h, ef)
                sv = s_ve.inc(v.tensor_copy(ecl[:, 0:8],
                                            ca_sb[:].rearrange("h (c l) -> h c l", l=CH)[:, :, CH - 1]))
                v.wait_ge(s_ve.h, sv)
            block.vector(v_ecl)
            ca_used[0] = s_ve.v
            ecl_ve = s_ve.v

            def a_ecl(a, ecl_ve=ecl_ve):
                a.wait_ge(s_ve.h, ecl_ve)
                s_ac.inc(a.activation(ecl[:, 0:8], ecl[:, 0:8], AF.Exp))
            block.scalar(a_ecl)
            ecl_ac = s_ac.v

            def g_ecl(g_, g2=g, ecl_ac=ecl_ac, ebv=eclbf_ve[0]):
                g_.wait_ge(s_ac.h, ecl_ac)
                if ebv:
                    g_.wait_ge(s_ve.h, ebv)
                base = s_wl.v
                g_.dma_start(out=t["ecl_d"][:, g2 * 8:(g2 + 1) * 8], in_=ecl[:, 0:8]).then_inc(s_wl.h, 16)
                g_.wait_ge(s_wl.h, base + 16)
                g_.dma_start(out=eclrow[:].rearrange("one (h c) -> one h c", c=8),
                             in_=t["ecl_d"][:, g2 * 8:(g2 + 1) * 8].unsqueeze(0)).then_inc(s_wl.h, 16)
            block.gpsimd(g_ecl)
            s_wl.v += 32
            ecl_free[0] = s_wl.v
            erow_wl = s_wl.v

            def v_ebf(v, wv=erow_wl):
                v.wait_ge(s_wl.h, wv)
                sv = s_ve.inc(v.tensor_copy(eclrow_bf[:], eclrow[:]))
                v.wait_ge(s_ve.h, sv)
            block.vector(v_ebf)
            eclbf_ve[0] = s_ve.v

            def t_e5(tt, ev=eclbf_ve[0], pf=psa_free[0]):
                tt.wait_ge(s_ve.h, ev)
                tt.wait_ge(s_ve.h, pf)
                s_pe.inc(tt.matmul(ps_a[:, 0:256], ones1[:], eclrow_bf[:], start=True, stop=True))
            block.tensor(t_e5)
            e512_pe[0] = s_pe.v

            def v_e5(v, pv=e512_pe[0], sc=scan_ve[0]):
                v.wait_ge(s_pe.h, pv)
                v.wait_ge(s_ve.h, sc)
                s_ve.inc(v.tensor_copy(e512[:].rearrange("p h c -> p (h c)"), ps_a[:, 0:256]))
            block.vector(v_e5)
            e512_ve[0] = s_ve.v
            psa_free[0] = s_ve.v

            def t_c(tt, par=par, xl=xl, pf=psa_free[0]):
                tt.wait_ge(s_ld.h, xl)
                tt.wait_ge(s_ve.h, pf)
                for dc in range(8):
                    ins = tt.matmul(ps_a[0:DS, 0:512], wc_sb[:, dc, :], xnt[:, par, dc, :],
                                    start=(dc == 0), stop=(dc == 7))
                s_pe.inc(ins)
            block.tensor(t_c)
            pe_c = s_pe.v

            def v_c(v, pe_c=pe_c, sic=sic_ac[0], cv=carc_ve[0]):
                v.wait_ge(s_pe.h, pe_c)
                if sic:
                    v.wait_ge(s_ac.h, sic)
                v.wait_ge(s_ve.h, cv)
                sv = s_ve.inc(v.tensor_copy(pcc[:, 0:3], carc[:]))
                sv = s_ve.inc(v.tensor_copy(pcc[:, 3:515], ps_a[0:DS, 0:512]))
                v.wait_ge(s_ve.h, sv)
                sv = s_ve.inc(v.tensor_copy(carc[:], pcc[:, 512:515]))
                v.wait_ge(s_ve.h, sv)
                v.tensor_scalar(cvoc[:], pcc[:, 0:512], cwc_sb[:, 0:1], None, ALU.mult)
                for k in (1, 2, 3):
                    ins = v.scalar_tensor_tensor(cvoc[:], pcc[:, k:k + 512], cwc_sb[:, k:k + 1],
                                                 cvoc[:], ALU.mult, ALU.add)
                s_ve.inc(ins)
            block.vector(v_c)
            carc_ve[0] = s_ve.v
            cv_ve = s_ve.v
            psa_free[0] = s_ve.v

            def a_sic(a, cv_ve=cv_ve, cp=csb_pe[0], cvv=csb_ve[0]):
                a.wait_ge(s_ve.h, cv_ve)
                if cp:
                    a.wait_ge(s_pe.h, cp)
                if cvv:
                    a.wait_ge(s_ve.h, cvv)
                s_ac.inc(a.activation(c_sb[:], cvoc[:], AF.Silu, bias=cbc_sb[:, 0:1]))
            block.scalar(a_sic)
            sic_ac[0] = s_ac.v
            csb_ac = s_ac.v

            def sy_b(sy, g=g, bp=csb_pe[0]):
                if bp:
                    sy.wait_ge(s_pe.h, bp)
                sy.dma_start(out=b_sb[:], in_=t["xcb_fm"][DI:DI + DS, g * 512:(g + 1) * 512]
                             ).then_inc(s_ld.h, 16)
            block.sync(sy_b)
            s_ld.v += 16
            bl = s_ld.v

            for sub in range(4):
                tcol = g * 512 + sub * 128

                def sy_t(sy, tcol=tcol, xu=xbt_users[:], du=xdt_users[0], cu=cat_users[0]):
                    sy.wait_ge(s_pe.h, xu[0])
                    sy.wait_ge(s_ve.h, xu[1])
                    sy.dma_start_transpose(xbt[:], t["xcb_fm"][:, tcol:tcol + 128]).then_inc(s_tb.h, 16)
                    sy.wait_ge(s_pe.h, du)
                    sy.dma_start(out=xw_t[:], in_=t["xw_tm"][tcol:tcol + 128, :]).then_inc(s_tb.h, 16)
                    sy.wait_ge(s_ve.h, cu)
                    sy.dma_start(out=cat_tm[:], in_=t["cumA_tm"][tcol:tcol + 128, :]).then_inc(s_tb.h, 16)
                block.sync(sy_t)
                s_tb.v += 48
                tl = s_tb.v

                def t_z(tt, par=par, sub=sub, pf=psa_free[0], xl=xl):
                    tt.wait_ge(s_ld.h, xl)
                    tt.wait_ge(s_ve.h, pf)
                    for nn in range(4):
                        for dc in range(8):
                            ins = tt.matmul(ps_a[:, nn * 512:(nn + 1) * 512],
                                            xnt[:, par, dc, sub * 128:(sub + 1) * 128],
                                            wz_sb[:, dc, nn * 512:(nn + 1) * 512],
                                            start=(dc == 0), stop=(dc == 7))
                    s_pe.inc(ins)
                block.tensor(t_z)
                z_pe = s_pe.v
                xn_free[par] = s_pe.v

                def a_z(a, z_pe=z_pe, zu=z_users[0]):
                    a.wait_ge(s_pe.h, z_pe)
                    if zu:
                        a.wait_ge(s_ve.h, zu)
                    s_ac.inc(a.activation(z_sb[:], ps_a[:], AF.Silu))
                block.scalar(a_z)
                z_ac[0] = s_ac.v
                psa_free_ac = s_ac.v

                for cc in range(2):
                    coff = cc * 64
                    ci = sub * 2 + cc

                    def sy_car(sy, g=g, ci=ci, tcol=tcol, coff=coff, cp=car_pe[0],
                               ca2=car_ac[0], xp=xdtc_pe[0]):
                        if cp:
                            sy.wait_ge(s_pe.h, cp)
                        if ca2:
                            sy.wait_ge(s_ac.h, ca2)
                        sy.dma_start(out=car[0:1].rearrange("one (h l) -> one h l", l=CH),
                                     in_=ca_r[:, g * 512 + ci * 64:g * 512 + (ci + 1) * 64].unsqueeze(0)
                                     ).then_inc(s_tb.h, 16)
                        if xp:
                            sy.wait_ge(s_pe.h, xp)
                        sy.dma_start(out=xdtc[:],
                                     in_=t["xdt_tm"][tcol + coff:tcol + coff + 64, :]).then_inc(s_tb.h, 16)
                    block.sync(sy_car)
                    s_tb.v += 32
                    carl = s_tb.v

                    def t_gt(tt, ci=ci, bl=bl, ca=csb_ac, zac=psa_free_ac, tl=tl,
                             pf=psa_free[0]):
                        tt.wait_ge(s_ld.h, bl)
                        tt.wait_ge(s_ac.h, max(ca, zac))
                        tt.wait_ge(s_ve.h, pf)
                        tt.wait_ge(s_tb.h, tl)
                        s_pe.inc(tt.matmul(ps_a[0:64, 0:64], b_sb[:, ci * 64:(ci + 1) * 64],
                                           c_sb[:, ci * 64:(ci + 1) * 64], start=True, stop=True))
                    block.tensor(t_gt)
                    gt_pe = s_pe.v
                    csb_pe[0] = s_pe.v

                    def v_gm(v, gt_pe=gt_pe, gu=rhs_ve[0]):
                        v.wait_ge(s_pe.h, gt_pe)
                        if gu:
                            v.wait_ge(s_ve.h, gu)
                        sv = s_ve.inc(v.tensor_tensor(gm_sb[:], ps_a[0:64, 0:64], msu[:], ALU.mult))
                        v.wait_ge(s_ve.h, sv)
                    block.vector(v_gm)
                    gm_done = s_ve.v

                    def t_b1(tt, carl=carl, ya=psb_yfree[0], dv=d_ve[0]):
                        tt.wait_ge(s_tb.h, carl)
                        if ya:
                            tt.wait_ge(s_ac.h, ya)
                        if dv:
                            tt.wait_ge(s_ve.h, dv)
                        for nn in range(4):
                            ins = tt.matmul(ps_b[0:64, nn * 512:(nn + 1) * 512], ones1f[:],
                                            car[0:1, nn * 512:(nn + 1) * 512], start=True, stop=True)
                        s_pe.inc(ins)
                    block.tensor(t_b1)
                    b1_pe = s_pe.v
                    car_pe[0] = s_pe.v

                    def p_b2(p, coff=coff, tl=tl, lu=d_ve[0]):
                        p.wait_ge(s_tb.h, tl)
                        if lu:
                            p.wait_ge(s_ve.h, lu)
                        ins = p.tensor_copy(b2_sb[:].rearrange("s (h l) -> s h l", l=P0),
                                            _b(cat_tm[coff:coff + 64, :].unsqueeze(2), [64, H0, P0]))
                        s_po.inc(ins)
                    block.gpsimd(p_b2)
                    b2_po = s_po.v

                    def v_d(v, b1_pe=b1_pe, b2_po=b2_po, le=rhs_ve[0]):
                        v.wait_ge(s_pe.h, b1_pe)
                        v.wait_ge(s_po.h, b2_po)
                        if le:
                            v.wait_ge(s_ve.h, le)
                        sv = s_ve.inc(v.tensor_tensor(d_sb[:], ps_b[0:64, :], b2_sb[:], ALU.subtract))
                        v.wait_ge(s_ve.h, sv)
                        sv = s_ve.inc(v.tensor_scalar(d_sb[:], d_sb[:], 0.0, None, ALU.min))
                        v.wait_ge(s_ve.h, sv)
                    block.vector(v_d)
                    d_ve[0] = s_ve.v
                    dmin_ve = s_ve.v

                    def a_lx(a, dmin_ve=dmin_ve, ru=rhs_ve[0]):
                        a.wait_ge(s_ve.h, dmin_ve)
                        s_ac.inc(a.activation(d_sb[:], d_sb[:], AF.Exp))
                    block.scalar(a_lx)
                    lx_ac = s_ac.v

                    def v_rhs(v, lx_ac=lx_ac, gm_done=gm_done, yp=yt_done[0]):
                        v.wait_ge(s_ac.h, lx_ac)
                        s_ve.inc(v.tensor_tensor(rhs_bf[:].rearrange("s (h l) -> s h l", l=P0),
                                                 d_sb[:].rearrange("s (h l) -> s h l", l=P0),
                                                 _b(gm_sb[:].unsqueeze(1), [64, H0, P0]), ALU.mult))
                    block.vector(v_rhs)
                    rhs_ve[0] = s_ve.v
                    rhs_done = s_ve.v

                    def a_er(a, carl=carl, ep=erow_pe[0]):
                        a.wait_ge(s_tb.h, carl)
                        if ep:
                            a.wait_ge(s_pe.h, ep)
                        s_ac.inc(a.activation(erow_bf[:], car[:], AF.Exp))
                    block.scalar(a_er)
                    car_ac[0] = s_ac.v
                    er_ac = s_ac.v

                    def t_eb(tt, er_ac=er_ac, zac=psa_free_ac, gmv=gm_done, csv=cs_ve[0],
                             pf=psa_free[0]):
                        tt.wait_ge(s_ac.h, max(er_ac, zac))
                        tt.wait_ge(s_ve.h, max(gmv, pf))
                        if csv:
                            tt.wait_ge(s_ve.h, csv)
                        for nn in range(4):
                            ins = tt.matmul(ps_a[:, nn * 512:(nn + 1) * 512], ones1[:],
                                            erow_bf[0:1, nn * 512:(nn + 1) * 512], start=True, stop=True)
                        s_pe.inc(ins)
                    block.tensor(t_eb)
                    eb_pe = s_pe.v
                    erow_pe[0] = s_pe.v

                    def v_cs(v, ci=ci, eb_pe=eb_pe, ca=csb_ac):
                        v.wait_ge(s_pe.h, eb_pe)
                        v.wait_ge(s_ac.h, ca)
                        s_ve.inc(v.tensor_tensor(cs_bf[:],
                                                 _b(c_sb[:, ci * 64:(ci + 1) * 64].unsqueeze(1), [128, H0, P0]),
                                                 ps_a[:].rearrange("p (h l) -> p h l", l=P0), ALU.mult))
                    block.vector(v_cs)
                    cs_ve[0] = s_ve.v
                    csb_ve[0] = s_ve.v
                    cs_done = s_ve.v

                    def t_y(tt, coff=coff, cc=cc, rhs_done=rhs_done, cs_done=cs_done,
                            sc=scan_ve[0], tl=tl, ya=psb_yfree[0]):
                        tt.wait_ge(s_ve.h, max(rhs_done, cs_done, sc))
                        tt.wait_ge(s_tb.h, tl)
                        if ya:
                            tt.wait_ge(s_ac.h, ya)
                        base = cc * 64
                        for h in range(H0):
                            tt.matmul(ps_b[base:base + 64, h * P0:(h + 1) * P0],
                                      rhs_bf[:, h * P0:(h + 1) * P0],
                                      xdtc[:, h * P0:(h + 1) * P0],
                                      start=True, stop=False)
                            ins = tt.matmul(ps_b[base:base + 64, h * P0:(h + 1) * P0],
                                            cs_bf[:, h, :], S_bf[:, h * P0:(h + 1) * P0],
                                            start=False, stop=True)
                        s_pe.inc(ins)
                    block.tensor(t_y)
                    y_pe = s_pe.v
                    xdtc_pe[0] = s_pe.v

                    def a_y(a, cc=cc, y_pe=y_pe, gv=gat_ve[0]):
                        a.wait_ge(s_pe.h, y_pe)
                        if gv:
                            a.wait_ge(s_ve.h, gv)
                        base = cc * 64
                        s_ac.inc(a.activation(y_t[base:base + 64, :], ps_b[base:base + 64, :], AF.Copy))
                    block.scalar(a_y)
                    psb_yfree[0] = s_ac.v
                    yt_done[0] = s_ac.v

                    def t_cs2(tt, coff=coff, cs_done=cs_done, tl=tl, sc=scan_ve[0]):
                        tt.wait_ge(s_ve.h, max(cs_done, sc))
                        tt.wait_ge(s_tb.h, tl)
                        for nn in range(4):
                            ins = tt.matmul(ps_a[:, nn * 512:(nn + 1) * 512],
                                            xbt[coff:coff + 64, DI:DI + DS],
                                            xw_t[coff:coff + 64, nn * 512:(nn + 1) * 512],
                                            start=True, stop=True)
                        s_pe.inc(ins)
                    block.tensor(t_cs2)
                    cs2_pe = s_pe.v
                    xdt_users[0] = s_pe.v
                    xbt_users[0] = s_pe.v

                    def v_scan(v, g=g, ci=ci, cs2_pe=cs2_pe, ev=e512_ve[0]):
                        v.wait_ge(s_pe.h, cs2_pe)
                        v.wait_ge(s_ve.h, ev)
                        v.tensor_tensor(_hq(S[:], P0), _hq(S[:], P0),
                                        _b(e512[:, :, ci:ci + 1], [DS, H0, P0]), ALU.mult)
                        v.tensor_tensor(S[:], S[:], ps_a[:], ALU.add)
                        s_ve.inc(v.tensor_copy(S_bf[:], S[:]))
                    block.vector(v_scan)
                    scan_ve[0] = s_ve.v
                    psa_free[0] = s_ve.v

                def v_gat(v, yd=yt_done[0], za=z_ac[0], tl=tl, ts_=tg_st[0]):
                    v.wait_ge(s_ac.h, max(yd, za))
                    v.wait_ge(s_tb.h, tl)
                    v.tensor_tensor(_hq(sq_t[:], P0), _hq(xbt[:, 0:DI], P0),
                                    _b(drep[:].unsqueeze(2), [128, H0, P0]), ALU.mult)
                    v.tensor_tensor(y_t[:], y_t[:], sq_t[:], ALU.add)
                    sv = s_ve.inc(v.tensor_tensor(t1_t[:], y_t[:], z_sb[:], ALU.mult))
                    v.wait_ge(s_ve.h, sv)
                block.vector(v_gat)
                gat1 = s_ve.v
                z_users[0] = s_ve.v
                xbt_users[1] = s_ve.v

                def a_sq2(a, gat1=gat1):
                    a.wait_ge(s_ve.h, gat1)
                    s_ac.inc(a.activation(sq_t[:], t1_t[:], AF.Square))
                block.scalar(a_sq2)
                sq_ac = s_ac.v

                def v_rs(v, sq_ac=sq_ac):
                    v.wait_ge(s_ac.h, sq_ac)
                    sv = s_ve.inc(v.reduce_sum(cols[:, 0:1], sq_t[:], axis=X))
                    v.wait_ge(s_ve.h, sv)
                block.vector(v_rs)
                rs_ve = s_ve.v

                def a_rstd(a, rs_ve=rs_ve):
                    a.wait_ge(s_ve.h, rs_ve)
                    s_ac.inc(a.activation(cols[:, 1:2], cols[:, 0:1], AF.Sqrt,
                                          scale=1.0 / DI, bias=EPS))
                block.scalar(a_rstd)
                rst_ac = s_ac.v

                def v_tg(v, tcol=tcol, rst_ac=rst_ac, ts_=tg_st[0]):
                    v.wait_ge(s_ac.h, rst_ac)
                    sv = s_ve.inc(v.reciprocal(cols[:, 2:3], cols[:, 1:2]))
                    v.wait_ge(s_ve.h, sv)
                    v.tensor_scalar(t1_t[:], t1_t[:], cols[:, 2:3], None, ALU.mult)
                    if ts_:
                        v.wait_ge(s_st.h, ts_)
                    s_ve.inc(v.tensor_tensor(tg_sb[:], t1_t[:], nwrep[:], ALU.mult))
                block.vector(v_tg)
                gat_ve[0] = s_ve.v
                tg_ve = s_ve.v

                def g_tg(g_, tcol=tcol, tg_ve=tg_ve):
                    g_.wait_ge(s_ve.h, tg_ve)
                    ins = g_.dma_start(out=t["tg_d"][tcol:tcol + 128, :], in_=tg_sb[:])
                    s_st.inc(ins, 16)
                block.gpsimd(g_tg)
                tg_st[0] = s_st.v

        wo_v = wz_sb[:].rearrange("p c e -> p (c e)")

        def g_wo(g_, pv=s_pe.v):
            g_.wait_ge(s_pe.h, pv)
            g_.dma_start(out=wo_v.rearrange("p (c e) -> p c e", e=D),
                         in_=t["wto"].rearrange("(c p) e -> p c e", p=128)).then_inc(s_wl.h, 16)
        block.gpsimd(g_wo)
        s_wl.v += 16
        wo_ld = s_wl.v

        tgf_pe = [0]
        so_st = [0]
        for g in range(8):
            def sy_tg(sy, g=g, pf=tgf_pe[0]):
                sy.wait_ge(s_st.h, tg_st[0])
                if pf:
                    sy.wait_ge(s_pe.h, pf)
                for zc in range(16):
                    sy.dma_start_transpose(tgf[:, zc, :],
                                           t["tg_d"][g * 512:(g + 1) * 512, zc * 128:(zc + 1) * 128]
                                           ).then_inc(s_tb.h, 16)
            block.sync(sy_tg)
            s_tb.v += 16 * 16
            tfl = s_tb.v

            for sub in range(4):
                def t_o(tt, sub=sub, tfl=tfl, pf=psa_free[0], wl=wo_ld):
                    tt.wait_ge(s_tb.h, tfl)
                    tt.wait_ge(s_ve.h, pf)
                    tt.wait_ge(s_wl.h, wl)
                    for nn in range(2):
                        for zc in range(16):
                            ins = tt.matmul(ps_a[:, nn * 512:(nn + 1) * 512],
                                            tgf[:, zc, sub * 128:(sub + 1) * 128],
                                            wo_v.rearrange("p (c e) -> p c e", e=D)[:, zc, nn * 512:(nn + 1) * 512],
                                            start=(zc == 0), stop=(zc == 15))
                    s_pe.inc(ins)
                block.tensor(t_o)
                tgf_pe[0] = s_pe.v
                o_pe = s_pe.v

                def v_o(v, o_pe=o_pe, sos=so_st[0]):
                    v.wait_ge(s_pe.h, o_pe)
                    if sos:
                        v.wait_ge(s_st.h, sos)
                    s_ve.inc(v.tensor_copy(s_outb[:], ps_a[:, 0:D]))
                block.vector(v_o)
                psa_free[0] = s_ve.v
                ov = s_ve.v

                def g_o(g_, g2=g, sub=sub, ov=ov):
                    g_.wait_ge(s_ve.h, ov)
                    ins = g_.dma_start(out=t["s_tm"][g2 * 512 + sub * 128:g2 * 512 + sub * 128 + 128, :],
                                       in_=s_outb[:])
                    s_st.inc(ins, 16)
                block.gpsimd(g_o)
                so_st[0] = s_st.v

        def g_end(g_):
            g_.wait_ge(s_st.h, s_st.v)
            g_.wait_ge(s_tb.h, s_tb.v)
            g_.wait_ge(s_ld.h, s_ld.v)
            g_.wait_ge(s_wl.h, s_wl.v)
            g_.wait_ge(s_po.h, s_po.v)
        block.gpsimd(g_end)

    return nc


def prep_l0b_inputs(inputs, A_outs):
    W = inputs["L0_attn_in_proj_w"]
    wtz = _tr(W[0:DI])
    wtc = _tr(W[DI + CONVD - DS + 0:0] if False else W[4224:4352])
    cw_c = np.ascontiguousarray(inputs["L0_attn_conv_w"][CONVD:CONVD + DS, 0, :])
    cb_c = np.ascontiguousarray(inputs["L0_attn_conv_b"][CONVD:CONVD + DS, None])
    d_rep = np.ascontiguousarray(np.broadcast_to(inputs["L0_attn_D"][None, :], (128, H0)))
    nw_rep = np.ascontiguousarray(np.broadcast_to(inputs["L0_attn_norm_w"][None, :], (128, DI)))
    wto = _tr(inputs["L0_attn_out_proj_w"])
    S_all_b = [np.stack([A_outs[b * 4 + j]["S_loc"] for j in range(4)] +
                        [np.zeros((DS, DI), np.float32)] * 4) for b in range(2)]
    D_tall_b = [np.stack([np.asarray(A_outs[b * 4 + j]["D_loc"])[:, 0] for j in range(4)] +
                         [np.zeros(H0, np.float32)] * 4) for b in range(2)]
    maps = []
    for c in range(NCORES):
        b, q = divmod(c, 4)
        Mt = np.zeros((8, 8), np.float32)
        pm = np.zeros(8, np.float32)
        for j in range(q):
            pm[j] = 1.0
            for m in range(j + 1, q):
                Mt[m, j] = 1.0
        pm_rep = np.ascontiguousarray(np.broadcast_to(pm[None, :], (H0, 8)))
        A = A_outs[c]
        maps.append(dict(
            xn_fm=np.asarray(A["xn_fm"]), xcb_fm=np.asarray(A["xcb_fm"]),
            xdt_tm=np.asarray(A["xdt_tm"]), xw_tm=np.asarray(A["xw_tm"]),
            cumA_fm=np.asarray(A["cumA_fm"]), cumA_tm=np.asarray(A["cumA_tm"]),
            S_all=S_all_b[b], D_tall=D_tall_b[b], Mt=Mt, pm_rep=pm_rep,
            wtz=wtz, wtc=wtc, cw_c=cw_c, cb_c=cb_c, d_rep=d_rep,
            nw_rep=nw_rep, wto=wto))
    return maps



def build_c1():
    nc = bass.Bass("TRN2", target_bir_lowering=False, detect_race_conditions=False)
    tn = {}
    def inp(name, shape, dt=F32):
        tn[name] = nc.dram_tensor(name, shape, dt, kind="ExternalInput").ap()
    inp("u", [T, D])
    inp("s_tm", [T, D])
    inp("g_out", [T, D], BF16)
    tn["x1_tm"] = nc.dram_tensor("x1_tm", [T, D], F32, kind="ExternalOutput").ap()
    tn["xn1_ext"] = nc.dram_tensor("xn1_ext", [D, 1 + T], BF16, kind="ExternalOutput").ap()
    tn["id128"] = nc.inline_tensor(np.eye(128, dtype=BF), name="id128c").ap()
    tn["cvals"] = nc.inline_tensor(np.tile(np.array([EPS, 1.0 / D], np.float32), (128, 1)),
                                   name="cvalsc").ap()
    emit_c1(nc, tn)
    return nc


def emit_c1(nc, t):
    s_ld = Sem(nc, "c_ld")
    s_wl = Sem(nc, "c_wl")
    s_pe = Sem(nc, "c_pe")
    s_ve = Sem(nc, "c_ve")
    s_ac = Sem(nc, "c_ac")
    s_st = Sem(nc, "c_st")
    import contextlib
    with contextlib.ExitStack() as _stk:
        _pfx = f"s{_NAME_SEQ[0]}_"
        _NAME_SEQ[0] += 1
        def _sb(nm, *a):
            return _stk.enter_context(nc.sbuf_tensor(_pfx + nm, *a))
        def _ps(nm, *a):
            return _stk.enter_context(nc.psum_tensor(_pfx + nm, *a))
        idn = _sb("idn", [128, 128], BF16)
        cst = _sb("cst", [128, 2], F32)
        ut = _sb("ut", [128, 2, D], F32)
        st_ = _sb("st_", [128, 2, D], F32)
        gt_ = _sb("gt_", [128, 2, D], BF16)
        x1 = _sb("x1", [128, D], F32)
        sq = _sb("sq", [128, D], F32)
        xnb = _sb("xnb", [128, D], BF16)
        stg = _sb("stg", [128, 2, 8, 128], BF16)
        cols = _sb("cols", [128, 8], F32)
        ps_tp = _ps("ps_tp", [128, 128], BF16)
        block = _stk.enter_context(nc.Block())

        nc.const_aps.register(F32, EPS, cst[:, 0:1])
        nc.const_aps.register(F32, 1.0 / D, cst[:, 1:2])

        def g_init(g):
            g.dma_start(out=idn[:], in_=t["id128"]).then_inc(s_wl.h, 16)
            g.dma_start(out=cst[:], in_=t["cvals"]).then_inc(s_wl.h, 16)
        block.gpsimd(g_init)
        s_wl.v += 32
        winit = s_wl.v

        xo_r = t["xn1_ext"].rearrange("(c p) tt -> p c tt", p=128)

        x1_done = [0] * 33
        stg_st = [0, 0]
        tp_done = [0] * 33
        evdc = [0]
        x1st = [0]

        for j in range(32):
            par = j % 2

            def sy_l(sy, j=j, par=par, fa=x1_done[j - 2] if j >= 2 else 0):
                if fa:
                    sy.wait_ge(s_ve.h, fa)
                sy.dma_start(out=ut[:, par], in_=t["u"][j * 128:(j + 1) * 128, :]).then_inc(s_ld.h, 16)
                sy.dma_start(out=st_[:, par], in_=t["s_tm"][j * 128:(j + 1) * 128, :]).then_inc(s_ld.h, 16)
                sy.dma_start(out=gt_[:, par], in_=t["g_out"][j * 128:(j + 1) * 128, :]).then_inc(s_ld.h, 16)
            block.sync(sy_l)
            s_ld.v += 48
            ldn = s_ld.v

            def v_x1(v, j=j, par=par, ldn=ldn, xs=x1st[0], av=s_ac.v):
                v.wait_ge(s_ld.h, ldn)
                if xs:
                    v.wait_ge(s_st.h, xs)
                if av:
                    v.wait_ge(s_ac.h, av)
                v.tensor_tensor(x1[:], st_[:, par], gt_[:, par], ALU.mult)
                sv = s_ve.inc(v.tensor_tensor(x1[:], x1[:], ut[:, par], ALU.add))
                v.wait_ge(s_ve.h, sv)
                s_ve.inc(v.reduce_sum(cols[:, 0:1], x1[:], axis=X))
            block.vector(v_x1)
            x1v = s_ve.v

            def g_x1(g, j=j, x1v=x1v):
                g.wait_ge(s_ve.h, x1v)
                ins = g.dma_start(out=t["x1_tm"][j * 128:(j + 1) * 128, :], in_=x1[:])
                s_st.inc(ins, 16)
            block.gpsimd(g_x1)
            x1st[0] = s_st.v

            def a_sq(a, x1v=x1v):
                a.wait_ge(s_ve.h, x1v)
                s_ac.inc(a.activation(sq[:], x1[:], AF.Square))
            block.scalar(a_sq)
            sq_ac = s_ac.v

            def v_l2(v, sq_ac=sq_ac):
                v.wait_ge(s_ac.h, sq_ac)
                sv = s_ve.inc(v.reduce_sum(cols[:, 1:2], sq[:], axis=X))
                v.wait_ge(s_ve.h, sv)
            block.vector(v_l2)
            r2 = s_ve.v

            def a_mm(a, r2=r2):
                a.wait_ge(s_ve.h, r2)
                a.activation(cols[:, 2:3], cols[:, 0:1], AF.Copy, scale=1.0 / D)
                s_ac.inc(a.activation(cols[:, 3:4], cols[:, 1:2], AF.Copy, scale=1.0 / D))
            block.scalar(a_mm)
            mm_ac = s_ac.v

            def v_var(v, mm_ac=mm_ac):
                v.wait_ge(s_ac.h, mm_ac)
                sv = s_ve.inc(v.tensor_tensor(cols[:, 4:5], cols[:, 2:3], cols[:, 2:3], ALU.mult))
                v.wait_ge(s_ve.h, sv)
                sv = s_ve.inc(v.tensor_tensor(cols[:, 5:6], cols[:, 3:4], cols[:, 4:5], ALU.subtract))
                v.wait_ge(s_ve.h, sv)
            block.vector(v_var)
            var_ve = s_ve.v

            def a_rs(a, var_ve=var_ve):
                a.wait_ge(s_ve.h, var_ve)
                s_ac.inc(a.activation(cols[:, 6:7], cols[:, 5:6], AF.Sqrt, bias=EPS))
            block.scalar(a_rs)
            rs_ac = s_ac.v

            def v_xn(v, j=j, rs_ac=rs_ac, tp=tp_done[j - 1] if j >= 1 else 0):
                v.wait_ge(s_ac.h, rs_ac)
                sv = s_ve.inc(v.reciprocal(cols[:, 7:8], cols[:, 6:7]))
                v.wait_ge(s_ve.h, sv)
                if tp:
                    v.wait_ge(s_pe.h, tp)
                s_ve.inc(v.tensor_scalar(xnb[:], x1[:], cols[:, 2:3], cols[:, 7:8],
                                         ALU.subtract, ALU.mult))
            block.vector(v_xn)
            x1_done[j] = s_ve.v
            ln_ve = s_ve.v

            for dc in range(8):
                def t_tp(tt, dc=dc, ln_ve=ln_ve, ev=evdc[0]):
                    if dc == 0:
                        tt.wait_ge(s_ve.h, ln_ve)
                        tt.wait_ge(s_wl.h, winit)
                    tt.wait_ge(s_ve.h, ev)
                    s_pe.inc(tt.transpose(ps_tp[:], xnb[:, dc * 128:(dc + 1) * 128], idn[:]))
                block.tensor(t_tp)
                pe_now = s_pe.v

                def v_ev(v, j=j, par=par, dc=dc, pe_now=pe_now, ss=stg_st[par]):
                    v.wait_ge(s_pe.h, pe_now)
                    if dc == 0 and ss:
                        v.wait_ge(s_st.h, ss)
                    s_ve.inc(v.tensor_copy(stg[:, par, dc, :], ps_tp[:]))
                block.vector(v_ev)
                evdc[0] = s_ve.v
            tp_done[j] = s_pe.v
            sd = s_ve.v

            def g_st(g, j=j, par=par, sd=sd):
                g.wait_ge(s_ve.h, sd)
                ins = g.dma_start(out=xo_r[:, :, 1 + j * 128:1 + (j + 1) * 128], in_=stg[:, par])
                s_st.inc(ins, 16)
            block.gpsimd(g_st)
            stg_st[par] = s_st.v

        def v_z(v):
            s_ve.inc(v.memset(cols[:], 0.0))
        block.vector(v_z)
        zv = s_ve.v

        def g_z(g):
            g.wait_ge(s_ve.h, zv)
            ins = g.dma_start(out=xo_r[:, :, 0:1], in_=cols[:, 0:8].unsqueeze(2),
                              allow_slow_non_contiguous=True)
            s_st.inc(ins, 16)
        block.gpsimd(g_z)

        def g_end(g):
            g.wait_ge(s_st.h, s_st.v)
            g.wait_ge(s_ld.h, s_ld.v)
        block.gpsimd(g_end)
    return nc



def build_l1c():
    nc = bass.Bass("TRN2", target_bir_lowering=False, detect_race_conditions=False)
    tn = {}
    def inp(name, shape, dt=F32):
        tn[name] = nc.dram_tensor(name, shape, dt, kind="ExternalInput").ap()
    inp("xn1_ext", [D, 4224], BF16)
    inp("wtw", [D, D]); inp("wtg", [D, H1]); inp("wtb", [D, H1]); inp("wta", [D, H1])
    inp("alog1", [H1, 1]); inp("dtb1", [H1, 1])
    tn["vb_tm"] = nc.dram_tensor("vb_tm", [T, D], BF16, kind="ExternalOutput").ap()
    tn["dec_fm"] = nc.dram_tensor("dec_fm", [H1, T], F32, kind="ExternalOutput").ap()
    tn["dec_tm"] = nc.dram_tensor("dec_tm", [T, H1], F32, kind="ExternalOutput").ap()
    tn["dwf_tm"] = nc.dram_tensor("dwf_tm", [T, H1], BF16, kind="ExternalOutput").ap()
    tn["gate_tm"] = nc.dram_tensor("gate_tm", [T, H1], F32, kind="ExternalOutput").ap()
    tn["S1_loc"] = nc.dram_tensor("S1_loc", [64, 1024], F32, kind="ExternalOutput").ap()
    tn["D1_loc"] = nc.dram_tensor("D1_loc", [H1, 1], F32, kind="ExternalOutput").ap()
    tn["id128"] = nc.inline_tensor(np.eye(128, dtype=BF), name="id128d").ap()
    d0 = np.ones((H1, 512), np.float32)
    d0[:, 0::CH] = 0.0
    tn["d0pat1"] = nc.inline_tensor(d0, name="d0pat1").ap()
    tn["ones1"] = nc.inline_tensor(np.ones((1, 128), dtype=BF), name="ones1d").ap()
    tn["ecl1_d"] = nc.dram_tensor("ecl1_d", [H1, NCHUNK], F32, kind="Internal").ap()
    emit_l1c(nc, tn)
    return nc


def emit_l1c(nc, t):
    s_ld = Sem(nc, "d_ld")
    s_tb = Sem(nc, "d_tb")
    s_wl = Sem(nc, "d_wl")
    s_pe = Sem(nc, "d_pe")
    s_ve = Sem(nc, "d_ve")
    s_ac = Sem(nc, "d_ac")
    s_st = Sem(nc, "d_st")
    import contextlib
    with contextlib.ExitStack() as _stk:
        _pfx = f"s{_NAME_SEQ[0]}_"
        _NAME_SEQ[0] += 1
        def _sb(nm, *a):
            return _stk.enter_context(nc.sbuf_tensor(_pfx + nm, *a))
        def _ps(nm, *a):
            return _stk.enter_context(nc.psum_tensor(_pfx + nm, *a))
        ww_sb = _sb("ww_sb", [128, 8, D], BF16)
        wg_sb = _sb("wg_sb", [128, 8, H1], BF16)
        wb_sb = _sb("wb_sb", [128, 8, H1], BF16)
        wa_sb = _sb("wa_sb", [128, 8, H1], BF16)
        idn = _sb("idn", [128, 128], BF16)
        ones1 = _sb("ones1", [1, 128], BF16)
        d0p = _sb("d0p", [H1, 512], F32)
        alog = _sb("alog", [H1, 1], F32)
        dtb = _sb("dtb", [H1, 1], F32)
        nega = _sb("nega", [H1, 1], F32)
        xnt = _sb("xnt", [128, 2, 8, 513], BF16)
        spa = _sb("spa", [H1, 512], F32)
        spb = _sb("spb", [H1, 512], F32)
        decf = _sb("decf", [H1, 512], F32)
        dh_hi = _sb("dh_hi", [H1, 512], BF16)
        dh_lo = _sb("dh_lo", [H1, 512], BF16)
        dwff = _sb("dwff", [H1, 512], F32)
        dwfb = _sb("dwfb", [H1, 512], BF16)
        dcl1 = _sb("dcl1", [H1, NCHUNK], F32)
        ecl1 = _sb("ecl1", [H1, 8], F32)
        e1rows = _sb("e1rows", [1, 128], F32)
        e1rows_bf = _sb("e1rows_bf", [1, 128], BF16)
        e1full = _sb("e1full", [64, H1, 8], F32)
        his = _sb("his", [128, H1], F32)
        dtm = _sb("dtm", [128, H1], F32)
        gtm = _sb("gtm", [128, H1], F32)
        btm = _sb("btm", [128, H1], F32)
        dwtm = _sb("dwtm", [128, H1], BF16)
        vbt = _sb("vbt", [128, D], BF16)
        wkt = _sb("wkt", [128, D], BF16)
        vbc = _sb("vbc", [64, D], BF16)
        dwfc = _sb("dwfc", [64, H1], BF16)
        wkp = _sb("wkp", [64, D], BF16)
        S1 = _sb("S1", [64, 1024], F32)
        cols = _sb("cols", [H1, 4], F32)
        ps_a = _ps("ps_a", [128, D], F32)
        ps_b = _ps("ps_b", [128, 1024], F32)
        ps_t = _ps("ps_t", [128, 128], BF16)
        block = _stk.enter_context(nc.Block())

        xn_r = t["xn1_ext"].rearrange("(c p) tt -> p c tt", p=128)

        def g_init(g):
            for o_, i_ in [(ww_sb[:], t["wtw"].rearrange("(c p) e -> p c e", p=128)),
                           (wg_sb[:], t["wtg"].rearrange("(c p) e -> p c e", p=128)),
                           (wb_sb[:], t["wtb"].rearrange("(c p) e -> p c e", p=128)),
                           (wa_sb[:], t["wta"].rearrange("(c p) e -> p c e", p=128)),
                           (idn[:], t["id128"]), (ones1[:], t["ones1"]),
                           (d0p[:], t["d0pat1"]), (alog[:], t["alog1"]),
                           (dtb[:], t["dtb1"])]:
                g.dma_start(out=o_, in_=i_).then_inc(s_wl.h, 16)
        block.gpsimd(g_init)
        s_wl.v += 16 * 9
        winit = s_wl.v

        def a_neg(a):
            a.wait_ge(s_wl.h, winit)
            s_ac.inc(a.activation(nega[:], alog[:], AF.Exp))
        block.scalar(a_neg)
        neg_ac = s_ac.v

        def v_neg(v):
            v.wait_ge(s_ac.h, neg_ac)
            sv = s_ve.inc(v.tensor_scalar(nega[:], nega[:], -1.0, None, ALU.mult))
            v.wait_ge(s_ve.h, sv)
            s_ve.inc(v.memset(S1[:], 0.0))
        block.vector(v_neg)
        neg_ve = s_ve.v

        alpha_ac = [0]
        e1_pe = [0]
        e1_ve = [0]
        ecl_wl = [0]
        eclbf_ve = [0]
        scan_ve = [neg_ve]
        vb_st = [0]
        sp_used = [0]
        dec_used = [0]
        dec_st = [0]
        dtm_st = [0]
        gtm_st = [0]
        xn_pe = [0, 0]
        psb_ve = [0]
        wkp_pe = [0]
        wkt_ve = [0]
        hi_ready = [0]
        dwf_ready = [0]
        tp_pe_all = [0]
        tp_last_ve = [0]
        vb_ac = [0]
        gb_ac_t = [0]

        for g in range(8):
            par = g % 2

            def sy_x(sy, g=g, par=par, xp=xn_pe[par]):
                if xp:
                    sy.wait_ge(s_pe.h, xp)
                sy.dma_start(out=xnt[:, par],
                             in_=xn_r[:, :, g * 512:g * 512 + 513]).then_inc(s_ld.h, 16)
            block.sync(sy_x)
            s_ld.v += 16
            xl = s_ld.v

            def t_al(tt, par=par, xl=xl, au=alpha_ac[0], vv=vb_ac[0]):
                tt.wait_ge(s_ld.h, xl)
                tt.wait_ge(s_wl.h, winit)
                if au:
                    tt.wait_ge(s_ac.h, au)
                if vv:
                    tt.wait_ge(s_ve.h, vv)
                for dc in range(8):
                    ins = tt.matmul(ps_a[0:H1, 0:512], wa_sb[:, dc, :],
                                    xnt[:, par, dc, 1:513], start=(dc == 0), stop=(dc == 7))
                s_pe.inc(ins)
            block.tensor(t_al)
            al_pe = s_pe.v

            def a_sp(a, al_pe=al_pe, su=sp_used[0]):
                a.wait_ge(s_pe.h, al_pe)
                a.wait_ge(s_ve.h, neg_ve)
                if su:
                    a.wait_ge(s_ve.h, su)
                a.activation(spa[:], ps_a[0:H1, 0:512], AF.Abs, bias=dtb[:, 0:1])
                a.activation(spa[:], spa[:], AF.Exp, scale=-1.0)
                a.activation(spa[:], spa[:], AF.Ln, bias=1.0)
                s_ac.inc(a.activation(spb[:], ps_a[0:H1, 0:512], AF.Relu, bias=dtb[:, 0:1]))
            block.scalar(a_sp)
            alpha_ac[0] = s_ac.v
            sp_ac = s_ac.v

            def v_dec(v, g=g, sp_ac=sp_ac, du=dec_used[0], ds=dec_st[0], tp=tp_pe_all[0]):
                v.wait_ge(s_ac.h, sp_ac)
                if du:
                    v.wait_ge(s_ac.h, du)
                if ds:
                    v.wait_ge(s_st.h, ds)
                if tp:
                    v.wait_ge(s_pe.h, tp)
                v.tensor_tensor(spb[:], spb[:], spa[:], ALU.add)
                v.tensor_scalar(spa[:], spb[:], nega[:, 0:1], None, ALU.mult)
                sv = s_ve.inc(v.tensor_tensor_scan(decf[:], d0p[:], spa[:], 0.0,
                                                   ALU.mult, ALU.add))
                v.wait_ge(s_ve.h, sv)
                v.tensor_copy(dh_hi[:], decf[:])
                sv = s_ve.inc(v.tensor_tensor(dh_lo[:], decf[:], dh_hi[:], ALU.subtract))
                v.wait_ge(s_ve.h, sv)
                s_ve.inc(v.tensor_copy(dcl1[:, g * 8:(g + 1) * 8],
                                       decf[:].rearrange("h (c l) -> h c l", l=CH)[:, :, CH - 1]))
            block.vector(v_dec)
            dec_ve = s_ve.v
            hi_ready[0] = s_ve.v

            def g_dec(g_, g2=g, dec_ve=dec_ve):
                g_.wait_ge(s_ve.h, dec_ve)
                ins = g_.dma_start(out=t["dec_fm"][:, g2 * 512:(g2 + 1) * 512], in_=decf[:])
                s_st.inc(ins, 16)
            block.gpsimd(g_dec)
            dec_st[0] = s_st.v

            def a_dwf(a, dec_ve=dec_ve, dw=dwf_ready[0]):
                a.wait_ge(s_ve.h, dec_ve)
                if dw:
                    a.wait_ge(s_ve.h, dw)
                for c in range(8):
                    cl = decf[:, c * 64 + 63:c * 64 + 64]
                    ins = a.activation(dwff[:, c * 64:(c + 1) * 64], decf[:, c * 64:(c + 1) * 64],
                                       AF.Exp, scale=-1.0, bias=cl)
                s_ac.inc(ins)
            block.scalar(a_dwf)
            dwf_ac = s_ac.v
            dec_used[0] = s_ac.v

            def v_dwf(v, dwf_ac=dwf_ac, tp=tp_pe_all[0]):
                v.wait_ge(s_ac.h, dwf_ac)
                if tp:
                    v.wait_ge(s_pe.h, tp)
                s_ve.inc(v.tensor_copy(dwfb[:], dwff[:]))
            block.vector(v_dwf)
            dwf_ready[0] = s_ve.v
            sp_used[0] = s_ve.v

            def v_e1(v, g=g, dec_ve=dec_ve, ew=ecl_wl[0]):
                v.wait_ge(s_ve.h, dec_ve)
                if ew:
                    v.wait_ge(s_wl.h, ew)
                sv = s_ve.inc(v.tensor_copy(ecl1[:, 0:8],
                                            decf[:].rearrange("h (c l) -> h c l", l=CH)[:, :, CH - 1]))
                v.wait_ge(s_ve.h, sv)
            block.vector(v_e1)
            e1v = s_ve.v

            def a_e1(a, e1v=e1v):
                a.wait_ge(s_ve.h, e1v)
                s_ac.inc(a.activation(ecl1[:, 0:8], ecl1[:, 0:8], AF.Exp))
            block.scalar(a_e1)
            e1a = s_ac.v

            def g_e1(g_, g2=g, e1a=e1a, ebv=eclbf_ve[0]):
                g_.wait_ge(s_ac.h, e1a)
                if ebv:
                    g_.wait_ge(s_ve.h, ebv)
                base = s_wl.v
                g_.dma_start(out=t["ecl1_d"][:, g2 * 8:(g2 + 1) * 8], in_=ecl1[:, 0:8]).then_inc(s_wl.h, 16)
                g_.wait_ge(s_wl.h, base + 16)
                g_.dma_start(out=e1rows[0:1, 0:128].rearrange("one (a c) -> one a c", c=8),
                             in_=t["ecl1_d"][:, g2 * 8:(g2 + 1) * 8].unsqueeze(0)).then_inc(s_wl.h, 16)
            block.gpsimd(g_e1)
            s_wl.v += 32
            ecl_wl[0] = s_wl.v

            def v_eb(v, wv=s_wl.v):
                v.wait_ge(s_wl.h, wv)
                sv = s_ve.inc(v.tensor_copy(e1rows_bf[:], e1rows[:]))
                v.wait_ge(s_ve.h, sv)
            block.vector(v_eb)
            eclbf_ve[0] = s_ve.v

            def t_e1(tt, ev=eclbf_ve[0], pv=psb_ve[0]):
                tt.wait_ge(s_ve.h, ev)
                if pv:
                    tt.wait_ge(s_ve.h, pv)
                s_pe.inc(tt.matmul(ps_b[0:64, 0:128], ones1[0:1, 0:64], e1rows_bf[0:1, 0:128],
                                   start=True, stop=True))
            block.tensor(t_e1)
            e1_pe[0] = s_pe.v

            def v_e1c(v, pv=e1_pe[0], sc=scan_ve[0]):
                v.wait_ge(s_pe.h, pv)
                v.wait_ge(s_ve.h, sc)
                s_ve.inc(v.tensor_copy(e1full[:].rearrange("p a c -> p (a c)"), ps_b[0:64, 0:128]))
            block.vector(v_e1c)
            e1_ve[0] = s_ve.v
            psb_ve[0] = s_ve.v

            for sub in range(4):
                tcol = g * 512 + sub * 128

                def t_gb(tt, par=par, sub=sub, xl=xl, vv=vb_ac[0], ga=gb_ac_t[0]):
                    tt.wait_ge(s_ld.h, xl)
                    if vv:
                        tt.wait_ge(s_ve.h, vv)
                    if ga:
                        tt.wait_ge(s_ac.h, ga)
                    for dc in range(8):
                        tt.matmul(ps_a[:, 0:H1], xnt[:, par, dc, 1 + sub * 128:1 + (sub + 1) * 128],
                                  wg_sb[:, dc, :], start=(dc == 0), stop=(dc == 7))
                    for dc in range(8):
                        ins = tt.matmul(ps_a[:, 64:64 + H1],
                                        xnt[:, par, dc, 1 + sub * 128:1 + (sub + 1) * 128],
                                        wb_sb[:, dc, :], start=(dc == 0), stop=(dc == 7))
                    s_pe.inc(ins)
                block.tensor(t_gb)
                gb_pe = s_pe.v

                def a_gb(a, gb_pe=gb_pe, gs=gtm_st[0]):
                    a.wait_ge(s_pe.h, gb_pe)
                    if gs:
                        a.wait_ge(s_st.h, gs)
                    a.activation(gtm[:], ps_a[:, 0:H1], AF.Sigmoid)
                    s_ac.inc(a.activation(btm[:], ps_a[:, 64:64 + H1], AF.Sigmoid))
                block.scalar(a_gb)
                gb_ac = s_ac.v
                gb_ac_t[0] = s_ac.v

                def g_gt(g_, tcol=tcol, gb_ac=gb_ac):
                    g_.wait_ge(s_ac.h, gb_ac)
                    ins = g_.dma_start(out=t["gate_tm"][tcol:tcol + 128, :], in_=gtm[:])
                    s_st.inc(ins, 16)
                block.gpsimd(g_gt)
                gtm_st[0] = s_st.v

                def t_tp1(tt, sub=sub, hr=hi_ready[0], dw=dwf_ready[0], tv=tp_last_ve[0]):
                    tt.wait_ge(s_ve.h, max(hr, dw))
                    if tv:
                        tt.wait_ge(s_ve.h, tv)
                    s_pe.inc(tt.transpose(ps_t[:, 0:H1], dh_hi[:, sub * 128:(sub + 1) * 128],
                                          idn[0:H1, 0:H1]))
                block.tensor(t_tp1)
                p1 = s_pe.v

                def v_hi(v, p1=p1):
                    v.wait_ge(s_pe.h, p1)
                    s_ve.inc(v.tensor_copy(his[:, 0:H1], ps_t[:, 0:H1]))
                block.vector(v_hi)
                hv = s_ve.v

                def t_tp2(tt, sub=sub, hv=hv):
                    tt.wait_ge(s_ve.h, hv)
                    s_pe.inc(tt.transpose(ps_t[:, 0:H1], dh_lo[:, sub * 128:(sub + 1) * 128],
                                          idn[0:H1, 0:H1]))
                block.tensor(t_tp2)
                p2 = s_pe.v

                def v_dtm(v, p2=p2, ds=dtm_st[0]):
                    v.wait_ge(s_pe.h, p2)
                    if ds:
                        v.wait_ge(s_st.h, ds)
                    s_ve.inc(v.tensor_tensor(dtm[:, 0:H1], his[:, 0:H1], ps_t[:, 0:H1], ALU.add))
                block.vector(v_dtm)
                dv = s_ve.v

                def t_tp3(tt, sub=sub, dv=dv):
                    tt.wait_ge(s_ve.h, dv)
                    s_pe.inc(tt.transpose(ps_t[:, 0:H1], dwfb[:, sub * 128:(sub + 1) * 128],
                                          idn[0:H1, 0:H1]))
                block.tensor(t_tp3)
                p3 = s_pe.v
                tp_pe_all[0] = s_pe.v

                def v_dwtm(v, p3=p3):
                    v.wait_ge(s_pe.h, p3)
                    s_ve.inc(v.tensor_copy(dwtm[:, 0:H1], ps_t[:, 0:H1]))
                block.vector(v_dwtm)
                dwv = s_ve.v
                tp_last_ve[0] = s_ve.v

                def g_dtm(g_, tcol=tcol, dwv=dwv):
                    g_.wait_ge(s_ve.h, dwv)
                    g_.dma_start(out=t["dec_tm"][tcol:tcol + 128, :], in_=dtm[:, 0:H1]).then_inc(s_st.h, 16)
                    g_.dma_start(out=t["dwf_tm"][tcol:tcol + 128, :], in_=dwtm[:, 0:H1]).then_inc(s_st.h, 16)
                block.gpsimd(g_dtm)
                s_st.v += 32
                dtm_st[0] = s_st.v

                def t_v(tt, par=par, sub=sub, xl=xl, va=vb_ac[0], ga=gb_ac):
                    tt.wait_ge(s_ac.h, ga)
                    if va:
                        tt.wait_ge(s_ve.h, va)
                    for nn in range(2):
                        for dc in range(8):
                            ins = tt.matmul(ps_a[:, nn * 512:(nn + 1) * 512],
                                            xnt[:, par, dc, 1 + sub * 128:1 + (sub + 1) * 128],
                                            ww_sb[:, dc, nn * 512:(nn + 1) * 512],
                                            start=(dc == 0), stop=(dc == 7))
                    s_pe.inc(ins)
                block.tensor(t_v)
                v_pe = s_pe.v
                xn_pe[par] = s_pe.v

                def v_vb(v, v_pe=v_pe, gb_ac=gb_ac, vs=vb_st[0]):
                    v.wait_ge(s_pe.h, v_pe)
                    v.wait_ge(s_ac.h, gb_ac)
                    if vs:
                        v.wait_ge(s_st.h, vs)
                    s_ve.inc(v.tensor_tensor(vbt[:].rearrange("p (h q) -> p h q", q=DD),
                                             ps_a[:].rearrange("p (h q) -> p h q", q=DD),
                                             _b(btm[:, 0:H1].unsqueeze(2), [128, H1, DD]), ALU.mult))
                block.vector(v_vb)
                vb_ac[0] = s_ve.v
                vbv = s_ve.v

                def g_vb(g_, tcol=tcol, vbv=vbv):
                    g_.wait_ge(s_ve.h, vbv)
                    ins = g_.dma_start(out=t["vb_tm"][tcol:tcol + 128, :], in_=vbt[:])
                    s_st.inc(ins, 16)
                block.gpsimd(g_vb)
                vb_st[0] = s_st.v
                vbst_now = s_st.v

                for cc in range(2):
                    coff = cc * 64
                    ci = sub * 2 + cc
                    tch = tcol + coff

                    def sy_wk(sy, tch=tch, wp=wkp_pe[0], wv=wkt_ve[0], vb_=vbst_now,
                              dws=dtm_st[0]):
                        if wv:
                            sy.wait_ge(s_ve.h, wv)
                        sy.dma_start_transpose(wkt[:], t["xn1_ext"][:, tch:tch + 128]).then_inc(s_tb.h, 16)
                        if wp:
                            sy.wait_ge(s_pe.h, wp)
                        sy.wait_ge(s_st.h, max(vb_, dws))
                        sy.dma_start(out=vbc[:], in_=t["vb_tm"][tch:tch + 64, :]).then_inc(s_tb.h, 16)
                        sy.dma_start(out=dwfc[:], in_=t["dwf_tm"][tch:tch + 64, :]).then_inc(s_tb.h, 16)
                    block.sync(sy_wk)
                    s_tb.v += 48
                    wkl = s_tb.v

                    def v_wkp(v, wkl=wkl, wp=wkp_pe[0]):
                        v.wait_ge(s_tb.h, wkl)
                        if wp:
                            v.wait_ge(s_pe.h, wp)
                        s_ve.inc(v.tensor_tensor(wkp[:].rearrange("p (h q) -> p h q", q=DD),
                                                 wkt[0:64, :].rearrange("p (h q) -> p h q", q=DD),
                                                 _b(dwfc[:].unsqueeze(2), [64, H1, DD]), ALU.mult))
                    block.vector(v_wkp)
                    wkp_ve = s_ve.v
                    wkt_ve[0] = s_ve.v

                    def t_ds(tt, wkp_ve=wkp_ve, pv=psb_ve[0]):
                        tt.wait_ge(s_ve.h, wkp_ve)
                        tt.wait_ge(s_ve.h, pv)
                        for h in range(H1):
                            ins = tt.matmul(ps_b[0:64, h * DD:(h + 1) * DD],
                                            wkp[:, h * DD:(h + 1) * DD],
                                            vbc[:, h * DD:(h + 1) * DD],
                                            start=True, stop=True)
                        s_pe.inc(ins)
                    block.tensor(t_ds)
                    ds_pe = s_pe.v
                    wkp_pe[0] = s_pe.v

                    def v_sc(v, ci=ci, ds_pe=ds_pe, ev=e1_ve[0]):
                        v.wait_ge(s_pe.h, ds_pe)
                        v.wait_ge(s_ve.h, ev)
                        v.tensor_tensor(S1[:].rearrange("p (a q) -> p a q", q=DD),
                                        S1[:].rearrange("p (a q) -> p a q", q=DD),
                                        _b(e1full[:, :, ci:ci + 1], [64, H1, DD]), ALU.mult)
                        s_ve.inc(v.tensor_tensor(S1[:], S1[:], ps_b[0:64, :], ALU.add))
                    block.vector(v_sc)
                    scan_ve[0] = s_ve.v
                    psb_ve[0] = s_ve.v

        def v_d1(v, sc=scan_ve[0]):
            v.wait_ge(s_ve.h, sc)
            sv = s_ve.inc(v.reduce_sum(cols[:, 0:1], dcl1[:], axis=X))
            v.wait_ge(s_ve.h, sv)
        block.vector(v_d1)
        d1v = s_ve.v

        def g_f(g_, d1v=d1v):
            g_.wait_ge(s_ve.h, d1v)
            g_.dma_start(out=t["D1_loc"], in_=cols[:, 0:1]).then_inc(s_st.h, 16)
            g_.dma_start(out=t["S1_loc"], in_=S1[:]).then_inc(s_st.h, 16)
        block.gpsimd(g_f)
        s_st.v += 32

        def g_end(g_):
            g_.wait_ge(s_st.h, s_st.v)
            g_.wait_ge(s_tb.h, s_tb.v)
            g_.wait_ge(s_ld.h, s_ld.v)
            g_.wait_ge(s_wl.h, s_wl.v)
        block.gpsimd(g_end)
    return nc



def build_l1f():
    nc = bass.Bass("TRN2", target_bir_lowering=False, detect_race_conditions=False)
    tn = {}
    def inp(name, shape, dt=F32):
        tn[name] = nc.dram_tensor(name, shape, dt, kind="ExternalInput").ap()
    inp("xn1_ext", [D, 4224], BF16)
    inp("wfcT", [D, 4096]); inp("bfc_fm", [128, 32])
    inp("afcT", [D, H1]); inp("bfcT", [H1, 4096])
    inp("wprT", [4096, D]); inp("bpr_rep", [128, D])
    inp("aprT", [4096, H1]); inp("bprT", [H1, D])
    tn["g1_out"] = nc.dram_tensor("g1_out", [T, D], BF16, kind="ExternalOutput").ap()
    tn["wqf_d"] = nc.dram_tensor("wqf_d", [D, 4096], BF16, kind="Internal").ap()
    tn["colb_d"] = nc.dram_tensor("colb_d", [128, 1], F32, kind="Internal").ap()
    tn["ones1"] = nc.inline_tensor(np.ones((1, 128), dtype=BF), name="ones1f2").ap()
    tn["ones1f"] = nc.inline_tensor(np.ones((1, 128), np.float32), name="ones1f3").ap()
    emit_l1f(nc, tn)
    return nc


def emit_l1f(nc, t):
    FC = 32
    s_ld = Sem(nc, "f_ld")
    s_wl = Sem(nc, "f_wl")
    s_pe = Sem(nc, "f_pe")
    s_ve = Sem(nc, "f_ve")
    s_ac = Sem(nc, "f_ac")
    s_st = Sem(nc, "f_st")
    import contextlib
    with contextlib.ExitStack() as _stk:
        _pfx = f"s{_NAME_SEQ[0]}_"
        _NAME_SEQ[0] += 1
        def _sb(nm, *a):
            return _stk.enter_context(nc.sbuf_tensor(_pfx + nm, *a))
        def _ps(nm, *a):
            return _stk.enter_context(nc.psum_tensor(_pfx + nm, *a))
        ones1 = _sb("ones1", [1, 128], BF16)
        ones1f = _sb("ones1f", [1, 128], F32)
        wtile = _sb("wtile", [128, 2, 4096], F32)
        wqtile = _sb("wqtile", [128, 2, 4096], BF16)
        wqp = _sb("wqp", [128, 32, 1024], BF16)
        wfc_c = _sb("wfc_c", [128, 2, 8, 128], BF16)
        afc = _sb("afc", [128, 8, H1], BF16)
        bfcT_sb = _sb("bfcT_sb", [H1, 4096], BF16)
        aprT_sb = _sb("aprT_sb", [128, 32, H1], BF16)
        bprT_sb = _sb("bprT_sb", [H1, D], BF16)
        bfc_sb = _sb("bfc_sb", [128, 32], F32)
        bpr_sb = _sb("bpr_sb", [128, D], F32)
        xnt = _sb("xnt", [128, 2, 8, 512], BF16)
        h_sb = _sb("h_sb", [128, FC, 512], BF16)
        lor1 = _sb("lor1", [H1, 512], BF16)
        lor2 = _sb("lor2", [H1, 512], BF16)
        o2f = _sb("o2f", [128, D], F32)
        g1t = _sb("g1t", [128, D], BF16)
        mcol = _sb("mcol", [128, 4], F32)
        mrow = _sb("mrow", [1, 128], F32)
        ps_h = _ps("ps_h", [128, 512], F32)
        ps_hb = _ps("ps_hb", [128, 512], F32)
        ps_l = _ps("ps_l", [H1, 512], F32)
        ps_o = _ps("ps_o", [128, 2, D], F32)
        block = _stk.enter_context(nc.Block())

        xn_r = t["xn1_ext"].rearrange("(c p) tt -> p c tt", p=128)

        def g_init(g):
            for o_, i_ in [(ones1[:], t["ones1"]), (ones1f[:], t["ones1f"]),
                           (afc[:], t["afcT"].rearrange("(c p) e -> p c e", p=128)),
                           (bfcT_sb[:], t["bfcT"]),
                           (aprT_sb[:], t["aprT"].rearrange("(c p) e -> p c e", p=128)),
                           (bprT_sb[:], t["bprT"]), (bfc_sb[:], t["bfc_fm"]),
                           (bpr_sb[:], t["bpr_rep"])]:
                g.dma_start(out=o_, in_=i_).then_inc(s_wl.h, 16)
        block.gpsimd(g_init)
        s_wl.v += 16 * 8
        winit = s_wl.v

        def emit_quant(wsrc, n_tiles, wq_dst, resident_dst):
            F = wsrc.shape[1]
            def v_m0(v, mv=s_ve.v):
                v.wait_ge(s_wl.h, winit)
                if mv:
                    v.wait_ge(s_ve.h, mv)
                s_ve.inc(v.memset(mcol[:], 0.0))
            block.vector(v_m0)
            vm_mark = [s_ve.v, s_ve.v]
            for i in range(n_tiles):
                p = i % 2

                def sy_w(sy, i=i, p=p, wv=vm_mark[p]):
                    sy.wait_ge(s_ve.h, wv)
                    sy.dma_start(out=wtile[:, p, 0:F], in_=wsrc[i * 128:(i + 1) * 128, :]).then_inc(s_ld.h, 16)
                block.sync(sy_w)
                s_ld.v += 16
                wl = s_ld.v

                def v_m(v, p=p, wl=wl):
                    v.wait_ge(s_ld.h, wl)
                    sv = s_ve.inc(v.reduce_max(mcol[:, 1:2], wtile[:, p, 0:F], axis=X,
                                               apply_absolute_value=True))
                    v.wait_ge(s_ve.h, sv)
                    sv = s_ve.inc(v.tensor_tensor(mcol[:, 0:1], mcol[:, 0:1], mcol[:, 1:2], ALU.max))
                    v.wait_ge(s_ve.h, sv)
                block.vector(v_m)
                vm_mark[p] = s_ve.v
            def g_mb(g_, mv=s_ve.v):
                g_.wait_ge(s_ve.h, mv)
                base = s_wl.v
                g_.dma_start(out=t["colb_d"], in_=mcol[:, 0:1]).then_inc(s_wl.h, 16)
                g_.wait_ge(s_wl.h, base + 16)
                g_.dma_start(out=mrow[:], in_=t["colb_d"].rearrange("p one -> (p one)").unsqueeze(0)
                             ).then_inc(s_wl.h, 16)
            block.gpsimd(g_mb)
            s_wl.v += 32
            mb_wl = s_wl.v

            def v_gm(v, mb_wl=mb_wl):
                v.wait_ge(s_wl.h, mb_wl)
                sv = s_ve.inc(v.reduce_max(mrow[0:1, 0:1], mrow[:], axis=X))
                v.wait_ge(s_ve.h, sv)
                sv = s_ve.inc(v.tensor_copy(mrow[0:1, 1:3], mrow[0:1, 0:2]))
                v.wait_ge(s_ve.h, sv)
            block.vector(v_gm)
            gm_ve = s_ve.v

            def t_mb(tt, gm_ve=gm_ve, hp=s_pe.v):
                tt.wait_ge(s_ve.h, gm_ve)
                s_pe.inc(tt.matmul(ps_l[0:128, 0:1] if False else ps_h[:, 0:1],
                                   ones1f[:], mrow[0:1, 0:1], start=True, stop=True))
            block.tensor(t_mb)
            mb_pe = s_pe.v

            def v_sc2(v, mb_pe=mb_pe):
                v.wait_ge(s_pe.h, mb_pe)
                sv = s_ve.inc(v.tensor_copy(mcol[:, 0:1], ps_h[:, 0:1]))
                v.wait_ge(s_ve.h, sv)
                sv = s_ve.inc(v.reciprocal(mcol[:, 1:2], mcol[:, 0:1]))
                v.wait_ge(s_ve.h, sv)
            block.vector(v_sc2)
            sc_ve = s_ve.v

            def a_sc(a, sc_ve=sc_ve):
                a.wait_ge(s_ve.h, sc_ve)
                a.activation(mcol[:, 2:3], mcol[:, 0:1], AF.Copy, scale=1.0 / 127.0)
                s_ac.inc(a.activation(mcol[:, 3:4], mcol[:, 1:2], AF.Copy, scale=127.0))
            block.scalar(a_sc)
            sc_ac = s_ac.v

            vq_mark = [s_ve.v, s_ve.v]
            st_mark = [s_st.v, s_st.v]
            for i in range(n_tiles):
                p = i % 2

                def sy_w2(sy, i=i, p=p, qv=vq_mark[p]):
                    sy.wait_ge(s_ve.h, qv)
                    sy.dma_start(out=wtile[:, p, 0:F], in_=wsrc[i * 128:(i + 1) * 128, :]).then_inc(s_ld.h, 16)
                block.sync(sy_w2)
                s_ld.v += 16
                wl = s_ld.v

                def v_q(v, p=p, wl=wl, sc_ac=sc_ac, qs=st_mark[p]):
                    v.wait_ge(s_ld.h, wl)
                    v.wait_ge(s_ac.h, sc_ac)
                    if qs:
                        v.wait_ge(s_st.h, qs)
                    v.tensor_scalar(wtile[:, p, 0:F], wtile[:, p, 0:F], mcol[:, 3:4], MAGIC,
                                    ALU.mult, ALU.add)
                    s_ve.inc(v.tensor_scalar(wqtile[:, p, 0:F], wtile[:, p, 0:F], MAGIC, mcol[:, 2:3],
                                             ALU.subtract, ALU.mult))
                block.vector(v_q)
                vq_mark[p] = s_ve.v
                qv = s_ve.v

                if wq_dst is not None:
                    def g_q(g_, i=i, p=p, qv=qv):
                        g_.wait_ge(s_ve.h, qv)
                        ins = g_.dma_start(out=wq_dst[i * 128:(i + 1) * 128, :], in_=wqtile[:, p, 0:F])
                        s_st.inc(ins, 16)
                    block.gpsimd(g_q)
                    st_mark[p] = s_st.v
                else:
                    def v_qc(v, i=i, p=p, qv=qv):
                        s_ve.inc(v.tensor_copy(resident_dst[:, i, :], wqtile[:, p, 0:F]))
                    block.vector(v_qc)
                    vq_mark[p] = s_ve.v

        emit_quant(t["wfcT"], 8, t["wqf_d"], None)
        emit_quant(t["wprT"], 32, None, wqp)
        q_ve = s_ve.v
        q_st = s_st.v

        xn_pe = [0, 0]
        hev_ac = [0, 0]
        l1_ve = [0]
        l2_ve = [0]
        wfc_pe = [0, 0]
        yo_ve = [0, 0]
        g1_st = [0]
        hread_pe = [0]

        for ti in range(8):
            par = ti % 2

            def sy_x(sy, ti=ti, par=par, xp=xn_pe[par]):
                if xp:
                    sy.wait_ge(s_pe.h, xp)
                sy.dma_start(out=xnt[:, par],
                             in_=xn_r[:, :, 1 + ti * 512:1 + (ti + 1) * 512]).then_inc(s_ld.h, 16)
            block.sync(sy_x)
            s_ld.v += 16
            xl = s_ld.v

            def t_l1(tt, par=par, xl=xl, lv=l1_ve[0], qv=q_ve, qs=q_st):
                tt.wait_ge(s_ld.h, xl)
                tt.wait_ge(s_ve.h, max(qv, lv))
                tt.wait_ge(s_st.h, qs)
                for dc in range(8):
                    ins = tt.matmul(ps_l[:, 0:512], afc[:, dc, :], xnt[:, par, dc, :],
                                    start=(dc == 0), stop=(dc == 7))
                s_pe.inc(ins)
            block.tensor(t_l1)
            l1_pe = s_pe.v

            def a_l1(a, l1_pe=l1_pe):
                a.wait_ge(s_pe.h, l1_pe)
                s_ac.inc(a.activation(lor1[:], ps_l[:, 0:512], AF.Copy, scale=2.0))
            block.scalar(a_l1)
            l1_ac = s_ac.v
            l1_ve[0] = 0

            for fc in range(FC):
                fpar = fc % 2

                def sy_wf(sy, ti=ti, fc=fc, fpar=fpar, wp=wfc_pe[fpar], qs=q_st):
                    sy.wait_ge(s_st.h, qs)
                    if wp:
                        sy.wait_ge(s_pe.h, wp)
                    sy.dma_start(out=wfc_c[:, fpar],
                                 in_=t["wqf_d"].rearrange("(c p) f -> p c f", p=128)[:, :, fc * 128:(fc + 1) * 128]
                                 ).then_inc(s_ld.h, 16)
                block.sync(sy_wf)
                s_ld.v += 16
                wfl = s_ld.v

                php = ps_h

                def t_h(tt, fc=fc, par=par, fpar=fpar, php=php, wfl=wfl,
                        ha=hev_ac[0], l1_ac=l1_ac):
                    tt.wait_ge(s_ld.h, wfl)
                    if ha:
                        tt.wait_ge(s_ac.h, ha)
                    for dc in range(8):
                        tt.matmul(php[:], wfc_c[:, fpar, dc, :], xnt[:, par, dc, :],
                                  start=(dc == 0), stop=False)
                    tt.wait_ge(s_ac.h, l1_ac)
                    ins = tt.matmul(php[:], bfcT_sb[:, fc * 128:(fc + 1) * 128], lor1[:],
                                    start=False, stop=True)
                    s_pe.inc(ins)
                block.tensor(t_h)
                h_pe = s_pe.v
                wfc_pe[fpar] = s_pe.v

                def a_h(a, fc=fc, php=php, h_pe=h_pe, hr=hread_pe[0]):
                    a.wait_ge(s_pe.h, h_pe)
                    if hr:
                        a.wait_ge(s_pe.h, hr)
                    s_ac.inc(a.activation(h_sb[:, fc, :], php[:], AF.Gelu,
                                          bias=bfc_sb[:, fc:fc + 1]))
                block.scalar(a_h)
                hev_ac[0] = s_ac.v
            h_ac = s_ac.v
            xn_pe[par] = s_pe.v

            def t_l2(tt, h_ac=h_ac, lv=l2_ve[0]):
                tt.wait_ge(s_ac.h, h_ac)
                if lv:
                    tt.wait_ge(s_ve.h, lv)
                for fc in range(FC):
                    ins = tt.matmul(ps_l[:, 0:512], aprT_sb[:, fc, :], h_sb[:, fc, :],
                                    start=(fc == 0), stop=(fc == FC - 1))
                s_pe.inc(ins)
            block.tensor(t_l2)
            l2_pe = s_pe.v

            def a_l2(a, l2_pe=l2_pe):
                a.wait_ge(s_pe.h, l2_pe)
                s_ac.inc(a.activation(lor2[:], ps_l[:, 0:512], AF.Copy, scale=2.0))
            block.scalar(a_l2)
            l2_ac = s_ac.v
            l2_ve[0] = 0

            for tq in range(4):
                ypar = tq % 2

                def t_o(tt, tq=tq, ypar=ypar, l2_ac=l2_ac, yv=yo_ve[ypar]):
                    tt.wait_ge(s_ac.h, l2_ac)
                    if yv:
                        tt.wait_ge(s_ve.h, yv)
                    for fc in range(FC):
                        for nn in range(2):
                            tt.matmul(ps_o[:, ypar, nn * 512:(nn + 1) * 512],
                                      h_sb[:, fc, tq * 128:(tq + 1) * 128],
                                      wqp[:, fc, nn * 512:(nn + 1) * 512],
                                      start=(fc == 0), stop=False)
                    for nn in range(2):
                        ins = tt.matmul(ps_o[:, ypar, nn * 512:(nn + 1) * 512],
                                        lor2[:, tq * 128:(tq + 1) * 128],
                                        bprT_sb[:, nn * 512:(nn + 1) * 512],
                                        start=False, stop=True)
                    s_pe.inc(ins)
                block.tensor(t_o)
                o_pe = s_pe.v
                hread_pe[0] = s_pe.v

                def v_o(v, ypar=ypar, o_pe=o_pe, gs=g1_st[0]):
                    v.wait_ge(s_pe.h, o_pe)
                    if gs:
                        v.wait_ge(s_st.h, gs)
                    s_ve.inc(v.tensor_tensor(o2f[:], ps_o[:, ypar], bpr_sb[:], ALU.add))
                block.vector(v_o)
                yo_ve[ypar] = s_ve.v
                ov = s_ve.v

                def a_g1(a, ov=ov):
                    a.wait_ge(s_ve.h, ov)
                    s_ac.inc(a.activation(g1t[:], o2f[:], AF.Silu))
                block.scalar(a_g1)
                g1a = s_ac.v

                def g_g1(g_, ti=ti, tq=tq, g1a=g1a):
                    g_.wait_ge(s_ac.h, g1a)
                    ins = g_.dma_start(out=t["g1_out"][ti * 512 + tq * 128:ti * 512 + (tq + 1) * 128, :],
                                       in_=g1t[:])
                    s_st.inc(ins, 16)
                block.gpsimd(g_g1)
                g1_st[0] = s_st.v

        def g_end(g_):
            g_.wait_ge(s_st.h, s_st.v)
            g_.wait_ge(s_ld.h, s_ld.v)
            g_.wait_ge(s_wl.h, s_wl.v)
        block.gpsimd(g_end)
    return nc



def build_l1d():
    nc = bass.Bass("TRN2", target_bir_lowering=False, detect_race_conditions=False)
    tn = {}
    def inp(name, shape, dt=F32):
        tn[name] = nc.dram_tensor(name, shape, dt, kind="ExternalInput").ap()
    inp("xn1_ext", [D, 4224], BF16)
    inp("vb_tm", [T, D], BF16)
    inp("dec_fm", [H1, T]); inp("dec_tm", [T, H1])
    inp("dwf_tm", [T, H1], BF16)
    inp("gate_tm", [T, H1])
    inp("S1_all", [8, 64, 1024]); inp("D1_tall", [8, H1])
    inp("Mt", [8, 8]); inp("pm_rep1", [H1, 8])
    inp("wtout", [D, D])
    inp("x1_tm", [T, D]); inp("g1_out", [T, D], BF16)
    tn["x2"] = nc.dram_tensor("x2", [T, D], F32, kind="ExternalOutput").ap()
    tn["o_d"] = nc.dram_tensor("o_d", [T, D], BF16, kind="Internal").ap()
    tn["f1_d"] = nc.dram_tensor("f1_d", [H1, 8], F32, kind="Internal").ap()
    tn["ones1"] = nc.inline_tensor(np.ones((1, 128), dtype=BF), name="ones1e").ap()
    tn["ones1f"] = nc.inline_tensor(np.ones((1, 128), np.float32), name="ones1f4").ap()
    tn["mask_st"] = nc.inline_tensor(np.triu(np.ones((64, 64), np.float32), 1), name="mask_st_c").ap()
    emit_l1d(nc, tn)
    return nc


def emit_l1d(nc, t):
    s_ld = Sem(nc, "e_ld")
    s_tb = Sem(nc, "e_tb")
    s_wl = Sem(nc, "e_wl")
    s_pe = Sem(nc, "e_pe")
    s_ve = Sem(nc, "e_ve")
    s_ac = Sem(nc, "e_ac")
    s_st = Sem(nc, "e_st")
    import contextlib
    with contextlib.ExitStack() as _stk:
        _pfx = f"s{_NAME_SEQ[0]}_"
        _NAME_SEQ[0] += 1
        def _sb(nm, *a):
            return _stk.enter_context(nc.sbuf_tensor(_pfx + nm, *a))
        def _ps(nm, *a):
            return _stk.enter_context(nc.psum_tensor(_pfx + nm, *a))
        wo_sb = _sb("wo_sb", [128, 8, D], BF16)
        ones1 = _sb("ones1", [1, 128], BF16)
        ones1f = _sb("ones1f", [1, 128], F32)
        mstr = _sb("mstr", [64, 64], F32)
        pmrep = _sb("pmrep", [H1, 8], F32)
        dtall = _sb("dtall", [8, H1], F32)
        mt_sb = _sb("mt_sb", [8, 8], F32)
        fh_sb = _sb("fh_sb", [H1, 8], F32)
        f1bc = _sb("f1bc", [64, H1, 8], F32)
        S1 = _sb("S1", [64, 1024], F32)
        S1_bf = _sb("S1_bf", [64, 1024], BF16)
        sjt = _sb("sjt", [64, 1024], F32)
        xnf = _sb("xnf", [64, 2, H1, 129], BF16)
        xnf2 = _sb("xnf2", [64, 2, H1, 129], BF16)
        vbc = _sb("vbc", [64, D], BF16)
        wkt = _sb("wkt", [128, D], BF16)
        wkp = _sb("wkp", [64, D], BF16)
        dwfc = _sb("dwfc", [64, H1], BF16)
        decc = _sb("decc", [64, H1], F32)
        edecc = _sb("edecc", [64, H1], F32)
        gatec = _sb("gatec", [64, H1], F32)
        edecf = _sb("edecf", [64, H1], F32)
        edrow = _sb("edrow", [1, 16], F32)
        b2_sb = _sb("b2_sb", [64, 1024], F32)
        d1row = _sb("d1row", [1, 1024], F32)
        intra = _sb("intra", [64, 1024], BF16)
        o_half = _sb("o_half", [64, 1024], F32)
        o_bf = _sb("o_bf", [64, 1024], BF16)
        ofm = _sb("ofm", [128, 8, 512], BF16)
        xn1t = _sb("xn1t", [128, D], BF16)
        x1t = _sb("x1t", [128, D], F32)
        g1t = _sb("g1t", [128, D], BF16)
        x2t = _sb("x2t", [128, D], F32)
        ps_w = _ps("ps_w", [128, 1024], F32)
        ps_b1 = _ps("ps_b1", [128, 1024], F32)
        ps_o1 = _ps("ps_o1", [64, 1024], F32)
        ps_o2 = _ps("ps_o2", [64, 1024], F32)
        block = _stk.enter_context(nc.Block())

        xn_r2 = t["xn1_ext"].rearrange("(h p) tt -> p h tt", p=64)

        def g_init(g):
            for o_, i_ in [(wo_sb[:], t["wtout"].rearrange("(c p) e -> p c e", p=128)),
                           (ones1[:], t["ones1"]), (ones1f[:], t["ones1f"]),
                           (mstr[:], t["mask_st"]), (pmrep[:], t["pm_rep1"]),
                           (dtall[:], t["D1_tall"]), (mt_sb[:], t["Mt"])]:
                g.dma_start(out=o_, in_=i_).then_inc(s_wl.h, 16)
        block.gpsimd(g_init)
        s_wl.v += 16 * 7
        winit = s_wl.v

        def t_f(tt):
            tt.wait_ge(s_wl.h, winit)
            s_pe.inc(tt.matmul(ps_w[0:H1, 0:8], dtall[:], mt_sb[:], start=True, stop=True))
        block.tensor(t_f)
        f_pe = s_pe.v

        def a_f(a):
            a.wait_ge(s_pe.h, f_pe)
            s_ac.inc(a.activation(fh_sb[:], ps_w[0:H1, 0:8], AF.Exp))
        block.scalar(a_f)
        f_ac = s_ac.v

        def v_f(v):
            v.wait_ge(s_ac.h, f_ac)
            sv = s_ve.inc(v.tensor_tensor(fh_sb[:], fh_sb[:], pmrep[:], ALU.mult))
            v.wait_ge(s_ve.h, sv)
        block.vector(v_f)
        f_ve = s_ve.v

        def g_fd(g):
            g.wait_ge(s_ve.h, f_ve)
            base = s_wl.v
            g.dma_start(out=t["f1_d"], in_=fh_sb[:]).then_inc(s_wl.h, 16)
            g.wait_ge(s_wl.h, base + 16)
            g.dma_start(out=d1row[0:1, 0:128].rearrange("one (a j) -> one a j", j=8),
                        in_=t["f1_d"][:, :].unsqueeze(0)).then_inc(s_wl.h, 16)
        block.gpsimd(g_fd)
        s_wl.v += 32
        frow_wl = s_wl.v

        def v_frb(v):
            v.wait_ge(s_wl.h, frow_wl)
            sv = s_ve.inc(v.tensor_copy(o_half[0:1, 0:128], d1row[0:1, 0:128]))
            v.wait_ge(s_ve.h, sv)
        block.vector(v_frb)
        frb_ve = s_ve.v

        def t_fb(tt, frb_ve=frb_ve):
            tt.wait_ge(s_ve.h, frb_ve)
            s_pe.inc(tt.matmul(ps_b1[0:64, 0:128], ones1f[0:1, 0:64], o_half[0:1, 0:128],
                               start=True, stop=True))
        block.tensor(t_fb)
        fb_pe = s_pe.v

        def v_fbc(v, fb_pe=fb_pe):
            v.wait_ge(s_pe.h, fb_pe)
            sv = s_ve.inc(v.tensor_copy(f1bc[:].rearrange("p a j -> p (a j)"),
                                        ps_b1[0:64, 0:128]))
            v.wait_ge(s_ve.h, sv)
            s_ve.inc(v.memset(S1[:], 0.0))
        block.vector(v_fbc)
        sin_ve = [s_ve.v]

        for j in range(8):
            def sy_sj(sy, j=j, prev=sin_ve[0]):
                sy.wait_ge(s_ve.h, prev)
                sy.dma_start(out=sjt[:], in_=t["S1_all"][j]).then_inc(s_ld.h, 16)
            block.sync(sy_sj)
            s_ld.v += 16
            sl = s_ld.v

            def v_sj(v, j=j, sl=sl):
                v.wait_ge(s_ld.h, sl)
                v.tensor_tensor(sjt[:].rearrange("p (a q) -> p a q", q=DD),
                                sjt[:].rearrange("p (a q) -> p a q", q=DD),
                                _b(f1bc[:, :, j:j + 1], [64, H1, DD]), ALU.mult)
                s_ve.inc(v.tensor_tensor(S1[:], S1[:], sjt[:], ALU.add))
            block.vector(v_sj)
            sin_ve[0] = s_ve.v

        def v_sbf(v, p=sin_ve[0]):
            v.wait_ge(s_ve.h, p)
            s_ve.inc(v.tensor_copy(S1_bf[:], S1[:]))
        block.vector(v_sbf)
        scan_ve = [s_ve.v]

        xn_pe = [0, 0]
        w1_ve = [0]
        b1_ve = [scan_ve[0]]
        chunk_ld = [0]
        ointra_ve = [0]
        o1_ac = [0]
        o2_ve = [0]
        psw_ve = [scan_ve[0]]
        ost = [0]
        wkd_pe = [0]
        wkt_ve = [0]
        ecc_ac = [0]
        ecc_users = [0]
        edec_row_users = [0]

        for g in range(8):
            par = g % 2
            for sub in range(4):
                tcol = g * 512 + sub * 128
                jj = 4 * g + sub

                def sy_x(sy, tcol=tcol, par=par, xp=xn_pe[par]):
                    if xp:
                        sy.wait_ge(s_pe.h, xp)
                    sy.dma_start(out=xnf[:, par],
                                 in_=xn_r2[:, :, tcol:tcol + 129]).then_inc(s_ld.h, 16)
                    sy.dma_start(out=xnf2[:, par],
                                 in_=xn_r2[:, :, tcol:tcol + 129]).then_inc(s_ld.h, 16)
                block.sync(sy_x)
                s_ld.v += 32
                xl = s_ld.v

                for cc in range(2):
                    coff = cc * 64
                    ci = tcol // 64 + cc
                    tch = tcol + coff

                    def sy_c(sy, tch=tch, ci=ci, wv=wkt_ve[0], wp=wkd_pe[0],
                             ea=ecc_users[0]):
                        if wv:
                            sy.wait_ge(s_ve.h, wv)
                        sy.dma_start_transpose(wkt[:], t["xn1_ext"][:, tch:tch + 128]).then_inc(s_tb.h, 16)
                        if wp:
                            sy.wait_ge(s_pe.h, wp)
                        sy.dma_start(out=vbc[:], in_=t["vb_tm"][tch:tch + 64, :]).then_inc(s_tb.h, 16)
                        if ea:
                            sy.wait_ge(s_ve.h, ea)
                        sy.dma_start(out=dwfc[:], in_=t["dwf_tm"][tch:tch + 64, :]).then_inc(s_tb.h, 16)
                        sy.dma_start(out=decc[:], in_=t["dec_tm"][tch:tch + 64, :]).then_inc(s_tb.h, 16)
                        sy.dma_start(out=gatec[:], in_=t["gate_tm"][tch:tch + 64, :]).then_inc(s_tb.h, 16)
                        sy.dma_start(out=d1row[0:1].rearrange("one (h l) -> one h l", l=CH),
                                     in_=t["dec_fm"][:, ci * 64:(ci + 1) * 64].unsqueeze(0)
                                     ).then_inc(s_tb.h, 16)
                    block.sync(sy_c)
                    s_tb.v += 96
                    cl = s_tb.v

                    def t_w1(tt, par=par, sub=sub, coff=coff, xl=xl, wv=w1_ve[0],
                             pv=psw_ve[0]):
                        tt.wait_ge(s_ld.h, xl)
                        if wv:
                            tt.wait_ge(s_ve.h, wv)
                        tt.wait_ge(s_ve.h, pv)
                        for h in range(H1):
                            ins = tt.matmul(ps_w[0:64, h * 64:(h + 1) * 64],
                                            xnf2[:, par, h, coff:coff + 64],
                                            xnf[:, par, h, coff + 1:coff + 65],
                                            start=True, stop=True)
                        s_pe.inc(ins)
                    block.tensor(t_w1)
                    w1_pe = s_pe.v

                    def v_b2(v, cl=cl, iv=ointra_ve[0]):
                        v.wait_ge(s_tb.h, cl)
                        if iv:
                            v.wait_ge(s_ve.h, iv)
                        sv = s_ve.inc(v.tensor_copy(b2_sb[:].rearrange("s (h l) -> s h l", l=DD),
                                                    _b(decc[:].unsqueeze(2), [64, H1, DD])))
                        v.wait_ge(s_ve.h, sv)
                    block.vector(v_b2)
                    b2v = s_ve.v

                    def t_b1(tt, cl=cl, bv=b1_ve[0]):
                        tt.wait_ge(s_tb.h, cl)
                        if bv:
                            tt.wait_ge(s_ve.h, bv)
                        for nn in range(2):
                            ins = tt.matmul(ps_b1[0:64, nn * 512:(nn + 1) * 512], ones1f[0:1, 0:64],
                                            d1row[0:1, nn * 512:(nn + 1) * 512], start=True, stop=True)
                        s_pe.inc(ins)
                    block.tensor(t_b1)
                    b1_pe = s_pe.v

                    def v_dL(v, b1_pe=b1_pe, b2v=b2v):
                        v.wait_ge(s_pe.h, b1_pe)
                        sv = s_ve.inc(v.tensor_tensor(b2_sb[:], ps_b1[0:64, :], b2_sb[:], ALU.subtract))
                        v.wait_ge(s_ve.h, sv)
                        sv = s_ve.inc(v.tensor_scalar(b2_sb[:], b2_sb[:], 0.0, None, ALU.min))
                        v.wait_ge(s_ve.h, sv)
                    block.vector(v_dL)
                    b1_ve[0] = s_ve.v
                    dL_ve = s_ve.v

                    def a_lx(a, dL_ve=dL_ve):
                        a.wait_ge(s_ve.h, dL_ve)
                        s_ac.inc(a.activation(b2_sb[:], b2_sb[:], AF.Exp))
                    block.scalar(a_lx)
                    lx_ac = s_ac.v

                    def v_intra(v, lx_ac=lx_ac, w1_pe=w1_pe, iv=ointra_ve[0]):
                        v.wait_ge(s_ac.h, lx_ac)
                        v.wait_ge(s_pe.h, w1_pe)
                        v.tensor_tensor(b2_sb[:].rearrange("s (h l) -> s h l", l=DD),
                                        b2_sb[:].rearrange("s (h l) -> s h l", l=DD),
                                        _b(mstr[:].unsqueeze(1), [64, H1, DD]), ALU.mult)
                        s_ve.inc(v.tensor_tensor(intra[:], b2_sb[:], ps_w[0:64, :], ALU.mult))
                    block.vector(v_intra)
                    w1_ve[0] = s_ve.v
                    ointra_ve[0] = s_ve.v
                    intra_ve = s_ve.v

                    def a_ed(a, cl=cl, eu=ecc_users[0]):
                        a.wait_ge(s_tb.h, cl)
                        if eu:
                            a.wait_ge(s_ve.h, eu)
                        s_ac.inc(a.activation(edecc[:], decc[:], AF.Exp))
                    block.scalar(a_ed)
                    ed_ac = s_ac.v
                    ecc_ac[0] = s_ac.v

                    def t_o12(tt, par=par, sub=sub, coff=coff, intra_ve=intra_ve,
                              sc=scan_ve[0], o1a=o1_ac[0], o2v=o2_ve[0], cl=cl):
                        tt.wait_ge(s_ve.h, max(intra_ve, sc))
                        tt.wait_ge(s_tb.h, cl)
                        if o1a:
                            tt.wait_ge(s_ac.h, o1a)
                        if o2v:
                            tt.wait_ge(s_ve.h, o2v)
                        for h in range(H1):
                            tt.matmul(ps_o1[0:64, h * 64:(h + 1) * 64],
                                      xnf[:, par, h, coff + 1:coff + 65],
                                      S1_bf[:, h * 64:(h + 1) * 64],
                                      start=True, stop=True)
                            ins = tt.matmul(ps_o2[0:64, h * 64:(h + 1) * 64],
                                            intra[:, h * 64:(h + 1) * 64],
                                            vbc[:, h * 64:(h + 1) * 64],
                                            start=True, stop=True)
                        s_pe.inc(ins)
                    block.tensor(t_o12)
                    o12_pe = s_pe.v
                    xn_pe[par] = s_pe.v

                    def a_o1(a, o12_pe=o12_pe, ed_ac=ed_ac, ov=o2_ve[0]):
                        a.wait_ge(s_pe.h, o12_pe)
                        a.wait_ge(s_ac.h, ed_ac)
                        s_ac.inc(a.activation(o_half[:], ps_o1[0:64, :], AF.Copy))
                    block.scalar(a_o1)
                    o1_ac[0] = s_ac.v
                    o1a = s_ac.v

                    def v_o(v, o1a=o1a, o12_pe=o12_pe, ci=ci):
                        v.wait_ge(s_ac.h, o1a)
                        v.tensor_tensor(o_half[:].rearrange("s (h l) -> s h l", l=DD),
                                        o_half[:].rearrange("s (h l) -> s h l", l=DD),
                                        _b(edecc[:].unsqueeze(2), [64, H1, DD]), ALU.mult)
                        v.tensor_tensor(o_half[:], o_half[:], ps_o2[0:64, :], ALU.add)
                        sv = s_ve.inc(v.tensor_tensor(o_bf[:].rearrange("s (h l) -> s h l", l=DD),
                                                      o_half[:].rearrange("s (h l) -> s h l", l=DD),
                                                      _b(gatec[:].unsqueeze(2), [64, H1, DD]), ALU.mult))
                        v.wait_ge(s_ve.h, sv)
                    block.vector(v_o)
                    o2_ve[0] = s_ve.v
                    ecc_users[0] = s_ve.v
                    ov = s_ve.v

                    def g_o(g_, tch=tch, ov=ov):
                        g_.wait_ge(s_ve.h, ov)
                        ins = g_.dma_start(out=t["o_d"][tch:tch + 64, :], in_=o_bf[:])
                        s_st.inc(ins, 16)
                    block.gpsimd(g_o)
                    ost[0] = s_st.v

                    def v_wkp(v, cl=cl, wp=wkd_pe[0]):
                        v.wait_ge(s_tb.h, cl)
                        if wp:
                            v.wait_ge(s_pe.h, wp)
                        s_ve.inc(v.tensor_tensor(wkp[:].rearrange("p (h q) -> p h q", q=DD),
                                                 wkt[0:64, :].rearrange("p (h q) -> p h q", q=DD),
                                                 _b(dwfc[:].unsqueeze(2), [64, H1, DD]), ALU.mult))
                    block.vector(v_wkp)
                    wkt_ve[0] = s_ve.v
                    wkp_ve = s_ve.v

                    def t_ds(tt, wkp_ve=wkp_ve, pv=psw_ve[0], w1v=w1_ve[0]):
                        tt.wait_ge(s_ve.h, max(wkp_ve, w1v))
                        for h in range(H1):
                            ins = tt.matmul(ps_w[0:64, h * DD:(h + 1) * DD],
                                            wkp[:, h * DD:(h + 1) * DD],
                                            vbc[:, h * DD:(h + 1) * DD],
                                            start=True, stop=True)
                        s_pe.inc(ins)
                    block.tensor(t_ds)
                    ds_pe = s_pe.v
                    wkd_pe[0] = s_pe.v

                    def v_dc(v, cl=cl, ee=edec_row_users[0]):
                        v.wait_ge(s_tb.h, cl)
                        if ee:
                            v.wait_ge(s_pe.h, ee)
                        sv = s_ve.inc(v.tensor_copy(edrow[0:1, 0:16], d1row[0:1, 63:1024:64]))
                        v.wait_ge(s_ve.h, sv)
                    block.vector(v_dc)
                    edrow_ve = s_ve.v

                    def t_el(tt, ds_pe=ds_pe, bv=b1_ve[0], ev=edrow_ve):
                        if bv:
                            tt.wait_ge(s_ve.h, bv)
                        tt.wait_ge(s_ve.h, ev)
                        s_pe.inc(tt.matmul(ps_b1[0:64, 0:16], ones1f[0:1, 0:64],
                                           edrow[0:1, 0:16], start=True, stop=True))
                    block.tensor(t_el)
                    el_pe = s_pe.v
                    edec_row_users[0] = s_pe.v

                    def v_el(v, el_pe=el_pe, sc=scan_ve[0]):
                        v.wait_ge(s_pe.h, el_pe)
                        v.wait_ge(s_ve.h, sc)
                        sv = s_ve.inc(v.tensor_copy(edecf[:], ps_b1[0:64, 0:16]))
                        v.wait_ge(s_ve.h, sv)
                    block.vector(v_el)
                    el_ve = s_ve.v
                    b1_ve[0] = s_ve.v

                    def a_el(a, el_ve=el_ve):
                        a.wait_ge(s_ve.h, el_ve)
                        s_ac.inc(a.activation(edecf[:], edecf[:], AF.Exp))
                    block.scalar(a_el)
                    el_ac = s_ac.v

                    def v_sc2(v, el_ac=el_ac, ds_pe=ds_pe):
                        v.wait_ge(s_ac.h, el_ac)
                        v.wait_ge(s_pe.h, ds_pe)
                        v.tensor_tensor(S1[:].rearrange("p (a q) -> p a q", q=DD),
                                        S1[:].rearrange("p (a q) -> p a q", q=DD),
                                        _b(edecf[:].unsqueeze(2), [64, H1, DD]), ALU.mult)
                        v.tensor_tensor(S1[:], S1[:], ps_w[0:64, :], ALU.add)
                        s_ve.inc(v.tensor_copy(S1_bf[:], S1[:]))
                    block.vector(v_sc2)
                    scan_ve[0] = s_ve.v
                    psw_ve[0] = s_ve.v
                    ecc_users[0] = s_ve.v



        of_pe = [0]
        x2_st = [0]
        for g in range(8):
            for sub in range(4):
                tcol = g * 512 + sub * 128

                if sub == 0:
                    def sy_og(sy, g=g, op=of_pe[0], os_=ost[0]):
                        sy.wait_ge(s_st.h, os_)
                        if op:
                            sy.wait_ge(s_pe.h, op)
                        for dc in range(8):
                            sy.dma_start_transpose(ofm[:, dc, :],
                                                   t["o_d"][g * 512:(g + 1) * 512,
                                                            dc * 128:(dc + 1) * 128]
                                                   ).then_inc(s_tb.h, 16)
                    block.sync(sy_og)
                    s_tb.v += 16 * 8

                def sy_of(sy, tcol=tcol, pv=psw_ve[0]):
                    if pv:
                        sy.wait_ge(s_ve.h, pv)
                    sy.dma_start_transpose(xn1t[:],
                                           t["xn1_ext"][:, 1 + tcol:1 + tcol + 128]).then_inc(s_tb.h, 16)
                block.sync(sy_of)
                s_tb.v += 16
                ofl = s_tb.v

                def sy_x2(sy, tcol=tcol, xs=x2_st[0]):
                    if xs:
                        sy.wait_ge(s_st.h, xs)
                    sy.dma_start(out=x1t[:], in_=t["x1_tm"][tcol:tcol + 128, :]).then_inc(s_ld.h, 16)
                    sy.dma_start(out=g1t[:], in_=t["g1_out"][tcol:tcol + 128, :]).then_inc(s_ld.h, 16)
                block.sync(sy_x2)
                s_ld.v += 32
                xxl = s_ld.v

                def t_wo(tt, sub=sub, ofl=ofl, pv=psw_ve[0]):
                    tt.wait_ge(s_tb.h, ofl)
                    tt.wait_ge(s_ve.h, pv)
                    for nn in range(2):
                        for dc in range(8):
                            ins = tt.matmul(ps_w[:, nn * 512:(nn + 1) * 512],
                                            ofm[:, dc, sub * 128:(sub + 1) * 128],
                                            wo_sb[:, dc, nn * 512:(nn + 1) * 512],
                                            start=(dc == 0), stop=(dc == 7))
                    s_pe.inc(ins)
                block.tensor(t_wo)
                of_pe[0] = s_pe.v
                wo_pe = s_pe.v

                def v_x2(v, tcol=tcol, wo_pe=wo_pe, xxl=xxl, ofl=ofl):
                    v.wait_ge(s_pe.h, wo_pe)
                    v.wait_ge(s_ld.h, xxl)
                    v.wait_ge(s_tb.h, ofl)
                    v.tensor_tensor(x2t[:], ps_w[:, 0:D], xn1t[:], ALU.add)
                    v.tensor_tensor(x2t[:], x2t[:], g1t[:], ALU.mult)
                    s_ve.inc(v.tensor_tensor(x2t[:], x2t[:], x1t[:], ALU.add))
                block.vector(v_x2)
                psw_ve[0] = s_ve.v
                xv = s_ve.v

                def g_x2(g_, tcol=tcol, xv=xv):
                    g_.wait_ge(s_ve.h, xv)
                    ins = g_.dma_start(out=t["x2"][tcol:tcol + 128, :], in_=x2t[:])
                    s_st.inc(ins, 16)
                block.gpsimd(g_x2)
                x2_st[0] = s_st.v

        def g_end(g_):
            g_.wait_ge(s_st.h, s_st.v)
            g_.wait_ge(s_tb.h, s_tb.v)
            g_.wait_ge(s_ld.h, s_ld.v)
            g_.wait_ge(s_wl.h, s_wl.v)
        block.gpsimd(g_end)
    return nc



def prep_c1_inputs(inputs, resB, resM):
    u = inputs["L0_attn_u"]
    maps = []
    for c in range(NCORES):
        b, q = divmod(c, 4)
        maps.append(dict(u=np.ascontiguousarray(u[b, q * T:(q + 1) * T]),
                         s_tm=np.asarray(resB[c]["s_tm"]),
                         g_out=np.asarray(resM[c]["g_out"])))
    return maps


def route_xn1(resC):
    import ml_dtypes as _md
    exts = []
    for c in range(NCORES):
        e = np.zeros((D, 4224), dtype=_md.bfloat16)
        e[:, 0:1 + T] = np.asarray(resC[c]["xn1_ext"])
        exts.append(e)
    for c in range(NCORES):
        b, q = divmod(c, 4)
        if q > 0:
            exts[c][:, 0] = exts[c - 1][:, T]
    return exts


def prep_l1c_inputs(inputs, exts):
    wtw = _tr(inputs["L1_attn_W_write"])
    wtg = _tr(inputs["L1_attn_W_gate"])
    wtb = _tr(inputs["L1_attn_W_beta"])
    wta = _tr(inputs["L1_attn_W_alpha"])
    alog1 = np.ascontiguousarray(inputs["L1_attn_A_log"][:, None])
    dtb1 = np.ascontiguousarray(inputs["L1_attn_dt_bias"][:, None])
    return [dict(xn1_ext=exts[c], wtw=wtw, wtg=wtg, wtb=wtb, wta=wta,
                 alog1=alog1, dtb1=dtb1) for c in range(NCORES)]


def prep_l1f_inputs(inputs, exts):
    wfcT = _tr(inputs["L1_ffn_W_fc"])
    bfc_fm = np.ascontiguousarray(inputs["L1_ffn_b_fc"].reshape(32, 128).T)
    afcT = _tr(inputs["L1_ffn_A_fc"])
    bfcT = _tr(inputs["L1_ffn_B_fc"])
    wprT = _tr(inputs["L1_ffn_W_proj"])
    bpr_rep = np.ascontiguousarray(np.broadcast_to(inputs["L1_ffn_b_proj"][None, :], (128, D)))
    aprT = _tr(inputs["L1_ffn_A_proj"])
    bprT = _tr(inputs["L1_ffn_B_proj"])
    return [dict(xn1_ext=exts[c], wfcT=wfcT, bfc_fm=bfc_fm, afcT=afcT, bfcT=bfcT,
                 wprT=wprT, bpr_rep=bpr_rep, aprT=aprT, bprT=bprT) for c in range(NCORES)]


def prep_l1d_inputs(inputs, exts, resL1C, resL1F, resC):
    wtout = _tr(inputs["L1_attn_W_out"])
    S1_all_b = [np.stack([np.asarray(resL1C[b * 4 + j]["S1_loc"]) for j in range(4)] +
                         [np.zeros((64, 1024), np.float32)] * 4) for b in range(2)]
    D1_tall_b = [np.stack([np.asarray(resL1C[b * 4 + j]["D1_loc"])[:, 0] for j in range(4)] +
                          [np.zeros(H1, np.float32)] * 4) for b in range(2)]
    maps = []
    for c in range(NCORES):
        b, q = divmod(c, 4)
        Mt = np.zeros((8, 8), np.float32)
        pm = np.zeros(8, np.float32)
        for j in range(q):
            pm[j] = 1.0
            for m in range(j + 1, q):
                Mt[m, j] = 1.0
        pm_rep1 = np.ascontiguousarray(np.broadcast_to(pm[None, :], (H1, 8)))
        r = resL1C[c]
        maps.append(dict(
            xn1_ext=exts[c], vb_tm=np.asarray(r["vb_tm"]),
            dec_fm=np.asarray(r["dec_fm"]), dec_tm=np.asarray(r["dec_tm"]),
            dwf_tm=np.asarray(r["dwf_tm"]), gate_tm=np.asarray(r["gate_tm"]),
            S1_all=S1_all_b[b], D1_tall=D1_tall_b[b], Mt=Mt, pm_rep1=pm_rep1,
            wtout=wtout, x1_tm=np.asarray(resC[c]["x1_tm"]),
            g1_out=np.asarray(resL1F[c]["g1_out"])))
    return maps


_BUILD_CACHE = {}


def _get(name, fn):
    if name not in _BUILD_CACHE:
        _BUILD_CACHE[name] = fn()
    return _BUILD_CACHE[name]


def kernel_multi(**inputs):
    import sys
    def _p(m):
        print(f"[kernel] {m}", file=sys.stderr, flush=True)
    cores = list(range(NCORES))
    _p("l0a")
    resA = run_bass_kernel_spmd(_get("l0a", build_l0a), prep_l0a_inputs(inputs), core_ids=cores).results
    xn_fms = [np.asarray(r["xn_fm"]) for r in resA]
    _p("moe")
    resM = run_bass_kernel_spmd(_get("moe", build_moe), prep_moe_inputs(inputs, xn_fms), core_ids=cores).results
    _p("l0b")
    resB = run_bass_kernel_spmd(_get("l0b", build_l0b), prep_l0b_inputs(inputs, resA), core_ids=cores).results
    _p("c1")
    resC = run_bass_kernel_spmd(_get("c1", build_c1), prep_c1_inputs(inputs, resB, resM), core_ids=cores).results
    exts = route_xn1(resC)
    _p("l1c")
    resL1C = run_bass_kernel_spmd(_get("l1c", build_l1c), prep_l1c_inputs(inputs, exts), core_ids=cores).results
    _p("l1f")
    resL1F = run_bass_kernel_spmd(_get("l1f", build_l1f), prep_l1f_inputs(inputs, exts), core_ids=cores).results
    _p("l1d")
    resL1D = run_bass_kernel_spmd(_get("l1d", build_l1d),
                                  prep_l1d_inputs(inputs, exts, resL1C, resL1F, resC),
                                  core_ids=cores).results
    out = np.zeros((2, 4 * T, D), np.float32)
    for c in range(NCORES):
        b, q = divmod(c, 4)
        out[b, q * T:(q + 1) * T] = np.asarray(resL1D[c]["x2"])
    return out



def _barrier(nc, exclude=()):
    snap = [(s.h, s.v) for s in Sem._reg if s.v > 0 and s not in exclude]
    with nc.Block() as bb:
        for en in ("sync", "gpsimd", "tensor", "vector", "scalar"):
            def f(e, snap=snap):
                for h, v in snap:
                    e.wait_ge(h, v)
            getattr(bb, en)(f)


def build_fused():
    nc = bass.Bass("TRN2", target_bir_lowering=False, detect_race_conditions=False)
    Sem._reg = []
    RG = [list(range(NCORES))]

    def ext(name, shape, dt=F32):
        return nc.dram_tensor(name, shape, dt, kind="ExternalInput").ap()

    def itn(name, shape, dt=F32):
        return nc.dram_tensor(name, shape, dt, kind="Internal").ap()

    def il(arr, name):
        return nc.inline_tensor(arr, name=name).ap()

    eye = np.eye(128, dtype=BF)
    onesb = np.ones((1, 128), dtype=BF)
    onesf = np.ones((1, 128), np.float32)
    d0a = np.ones((H0, 512), np.float32); d0a[:, 0::CH] = 0.0
    d0c = np.ones((H1, 512), np.float32); d0c[:, 0::CH] = 0.0
    cvD = np.tile(np.array([EPS, 1.0 / D], np.float32), (128, 1))
    cvDI = np.tile(np.array([EPS, 1.0 / DI], np.float32), (128, 1))

    u = ext("u", [T, D])
    Mt = ext("Mt", [8, 8])
    xn_fm = itn("xn_fm", [D, 128 + T], BF16)
    xcb_fm = itn("xcb_fm", [CONVD, T], BF16)
    xdt_tm = itn("xdt_tm", [T, DI], BF16)
    xw_tm = itn("xw_tm", [T, DI], BF16)
    cumA_fm = itn("cumA_fm", [H0, T])
    cumA_tm = itn("cumA_tm", [T, H0])
    S_loc = itn("S_loc", [DS, DI])
    D_loc = itn("D_loc", [H0, 1])
    S_all = itn("S_all", [8, DS, DI])
    D_tall = itn("D_tall", [8, H0])
    g_out = itn("g_out", [T, D], BF16)
    s_tm = itn("s_tm", [T, D])
    x1_tm = itn("x1_tm", [T, D])
    xn1_ext = itn("xn1_ext", [D, 4224], BF16)
    bcol_src = itn("bcol_src", [D, 1], BF16)
    bcol_all = itn("bcol_all", [8, D], BF16)
    vb_tm = itn("vb_tm", [T, D], BF16)
    dec_fm = itn("dec_fm", [H1, T])
    dec_tm = itn("dec_tm", [T, H1])
    dwf_tm = itn("dwf_tm", [T, H1], BF16)
    gate_tm = itn("gate_tm", [T, H1])
    S1_loc = itn("S1_loc", [64, 1024])
    D1_loc = itn("D1_loc", [H1, 1])
    S1_all = itn("S1_all", [8, 64, 1024])
    D1_tall = itn("D1_tall", [8, H1])
    g1_out = itn("g1_out", [T, D], BF16)
    x2 = nc.dram_tensor("x2", [T, D], F32, kind="ExternalOutput").ap()

    tA = dict(u=u, uh=ext("uh", [128, D]),
              wtxb=ext("wtxb", [D, CONVD]), wtdt=ext("wtdt", [D, H0]),
              cw=ext("cw", [CONVD, 4]), cb=ext("cb", [128, 17]),
              dtb=ext("dtb", [H0, 1]), alog=ext("alog", [H0, 1]),
              xn_fm=xn_fm, xcb_fm=xcb_fm, xdt_tm=xdt_tm, xw_tm=xw_tm,
              cumA_fm=cumA_fm, cumA_tm=cumA_tm, S_loc=S_loc, D_loc=D_loc,
              id128=il(eye, "id128"), d0pat=il(d0a, "d0pat"),
              ones1=il(onesb, "ones1c"), cvals=il(cvD, "cvals"),
              ecl_d=itn("ecl_d", [H0, NCHUNK]))
    emit_l0a(nc, tA)

    cc1 = Sem(nc, "cc1")
    with nc.Block() as bb:
        def g_c1(g):
            g.collective_compute("AllGather", ALU.bypass, replica_groups=RG,
                                 ins=[S_loc.opt()], outs=[S_all.opt()]).then_inc(cc1.h, 1)
            g.collective_compute("AllGather", ALU.bypass, replica_groups=RG,
                                 ins=[D_loc.opt()], outs=[D_tall.opt()]).then_inc(cc1.h, 1)
        bb.gpsimd(g_c1)
    cc1.v += 2
    _barrier(nc, exclude=(cc1,))

    tM = dict(xn_fm=xn_fm, router=ext("router", [D, 8]),
              w1=ext("w1", [8, D, 2816]), v1=ext("v1", [8, D, 2816]),
              w2=ext("w2", [8, 2816, D]), bias_rep=ext("bias_rep", [128, D]),
              g_out=g_out, gacc=itn("gacc", [T, D]))
    emit_moe(nc, tM)
    _barrier(nc)

    tB = dict(xn_fm=xn_fm, xcb_fm=xcb_fm, xdt_tm=xdt_tm, xw_tm=xw_tm,
              cumA_fm=cumA_fm, cumA_tm=cumA_tm, S_all=S_all, D_tall=D_tall,
              Mt=Mt, pm_rep=ext("pm_rep", [H0, 8]),
              wtz=ext("wtz", [D, DI]), wtc=ext("wtc", [D, DS]),
              cw_c=ext("cw_c", [DS, 4]), cb_c=ext("cb_c", [DS, 1]),
              d_rep=ext("d_rep", [128, H0]), nw_rep=ext("nw_rep", [128, DI]),
              wto=ext("wto", [DI, D]), s_tm=s_tm,
              tg_d=itn("tg_d", [T, DI], BF16), ecl_d=itn("ecl_d2", [H0, NCHUNK]),
              f_d=itn("f_d", [H0, 8]),
              id128=il(eye, "id128b"), ones1=il(onesb, "ones1b"),
              ones1f=il(np.ones((1, 64), np.float32), "ones1f_c"),
              mask_su=il(np.triu(np.ones((64, 64), np.float32)), "mask_su_c"),
              cvals=il(cvDI, "cvalsb"))
    emit_l0b(nc, tB)
    _barrier(nc)

    tC = dict(u=u, s_tm=s_tm, g_out=g_out, x1_tm=x1_tm,
              xn1_ext=xn1_ext[:, 0:1 + T],
              id128=il(eye, "id128c"), cvals=il(cvD, "cvalsc"))
    emit_c1(nc, tC)

    cc2 = Sem(nc, "cc2")
    cbs = Sem(nc, "cbs")
    with nc.Block() as bb:
        def g_c2(g):
            g.dma_start(out=bcol_src, in_=xn1_ext[:, T:T + 1],
                        allow_slow_non_contiguous=True).then_inc(cbs.h, 16)
            g.wait_ge(cbs.h, 16)
            g.collective_compute("AllGather", ALU.bypass, replica_groups=RG,
                                 ins=[bcol_src.opt()], outs=[bcol_all.opt()]).then_inc(cc2.h, 1)
        bb.gpsimd(g_c2)
    cbs.v += 16
    cc2.v += 1
    _barrier(nc, exclude=(cc2,))

    tF = dict(xn1_ext=xn1_ext, wfcT=ext("wfcT", [D, 4096]),
              bfc_fm=ext("bfc_fm", [128, 32]), afcT=ext("afcT", [D, H1]),
              bfcT=ext("bfcT", [H1, 4096]), wprT=ext("wprT", [4096, D]),
              bpr_rep=ext("bpr_rep", [128, D]), aprT=ext("aprT", [4096, H1]),
              bprT=ext("bprT", [H1, D]), g1_out=g1_out,
              wqf_d=itn("wqf_d", [D, 4096], BF16), colb_d=itn("colb_d", [128, 1]),
              ones1=il(onesb, "ones1f2"), ones1f=il(onesf, "ones1f3"))
    emit_l1f(nc, tF)
    _barrier(nc)

    selb = ext("selb", [8, 1], BF16)
    s_sl = Sem(nc, "sl_ld")
    s_sp = Sem(nc, "sl_pe")
    s_sv = Sem(nc, "sl_ve")
    s_ss = Sem(nc, "sl_st")
    import contextlib
    with contextlib.ExitStack() as _stk:
        gsb = _stk.enter_context(nc.sbuf_tensor("gsb", [8, D], BF16))
        selc = _stk.enter_context(nc.sbuf_tensor("selc", [8, 1], BF16))
        colb = _stk.enter_context(nc.sbuf_tensor("colb", [128, 8], BF16))
        ps_s = _stk.enter_context(nc.psum_tensor("ps_s", [128, 8], F32))
        with nc.Block() as bb:
            def sy_s(sy):
                sy.wait_ge(cc2.h, cc2.v)
                sy.dma_start(out=gsb[:], in_=bcol_all).then_inc(s_sl.h, 16)
                sy.dma_start(out=selc[:], in_=selb).then_inc(s_sl.h, 16)
            bb.sync(sy_s)
            s_sl.v += 32

            def t_s(tt):
                tt.wait_ge(s_sl.h, s_sl.v)
                for k in range(8):
                    ins = tt.matmul(ps_s[:, k:k + 1], gsb[:, k * 128:(k + 1) * 128],
                                    selc[:], start=True, stop=True)
                s_sp.inc(ins)
            bb.tensor(t_s)

            def v_s(v):
                v.wait_ge(s_sp.h, s_sp.v)
                s_sv.inc(v.tensor_copy(colb[:], ps_s[:]))
            bb.vector(v_s)

            def g_s(g):
                g.wait_ge(s_sv.h, s_sv.v)
                for k in range(8):
                    g.dma_start(out=xn1_ext[k * 128:(k + 1) * 128, 0:1],
                                in_=colb[:, k:k + 1],
                                allow_slow_non_contiguous=True).then_inc(s_ss.h, 16)
            bb.gpsimd(g_s)
            s_ss.v += 16 * 8
    _barrier(nc)

    tE = dict(xn1_ext=xn1_ext, wtw=ext("wtw", [D, D]), wtg=ext("wtg", [D, H1]),
              wtb=ext("wtb", [D, H1]), wta=ext("wta", [D, H1]),
              alog1=ext("alog1", [H1, 1]), dtb1=ext("dtb1", [H1, 1]),
              vb_tm=vb_tm, dec_fm=dec_fm, dec_tm=dec_tm, dwf_tm=dwf_tm,
              gate_tm=gate_tm, S1_loc=S1_loc, D1_loc=D1_loc,
              id128=il(eye, "id128d"), d0pat1=il(d0c, "d0pat1"),
              ones1=il(onesb, "ones1d"), ecl1_d=itn("ecl1_d", [H1, NCHUNK]))
    emit_l1c(nc, tE)

    cc3 = Sem(nc, "cc3")
    with nc.Block() as bb:
        def g_c3(g):
            g.collective_compute("AllGather", ALU.bypass, replica_groups=RG,
                                 ins=[S1_loc.opt()], outs=[S1_all.opt()]).then_inc(cc3.h, 1)
            g.collective_compute("AllGather", ALU.bypass, replica_groups=RG,
                                 ins=[D1_loc.opt()], outs=[D1_tall.opt()]).then_inc(cc3.h, 1)
        bb.gpsimd(g_c3)
    cc3.v += 2
    _barrier(nc)

    tG = dict(xn1_ext=xn1_ext, vb_tm=vb_tm, dec_fm=dec_fm, dec_tm=dec_tm,
              dwf_tm=dwf_tm, gate_tm=gate_tm, S1_all=S1_all, D1_tall=D1_tall,
              Mt=Mt, pm_rep1=ext("pm_rep1", [H1, 8]), wtout=ext("wtout", [D, D]),
              x1_tm=x1_tm, g1_out=g1_out, x2=x2,
              o_d=itn("o_d", [T, D], BF16), f1_d=itn("f1_d", [H1, 8]),
              ones1=il(onesb, "ones1e"), ones1f=il(onesf, "ones1f4"),
              mask_st=il(np.triu(np.ones((64, 64), np.float32), 1), "mask_st_c"))
    emit_l1d(nc, tG)
    return nc


def prep_fused_inputs(inputs):
    import ml_dtypes as _md
    W = inputs["L0_attn_in_proj_w"]
    shared = dict(
        wtxb=_tr(W[DI:DI + CONVD]), wtdt=_tr(W[4352:4384]),
        cw=np.ascontiguousarray(inputs["L0_attn_conv_w"][:CONVD, 0, :]),
        cb=np.ascontiguousarray(inputs["L0_attn_conv_b"][:CONVD].reshape(17, 128).T),
        dtb=np.ascontiguousarray(inputs["L0_attn_dt_bias"][:, None]),
        alog=np.ascontiguousarray(inputs["L0_attn_A_log"][:, None]),
        router=np.ascontiguousarray(inputs["L0_ffn_router_w"]),
        w1=inputs["L0_ffn_w1"], v1=inputs["L0_ffn_v1"], w2=inputs["L0_ffn_w2"],
        bias_rep=np.ascontiguousarray(np.broadcast_to(inputs["L0_ffn_bias"][None, :], (128, D))),
        wtz=_tr(W[0:DI]), wtc=_tr(W[4224:4352]),
        cw_c=np.ascontiguousarray(inputs["L0_attn_conv_w"][CONVD:CONVD + DS, 0, :]),
        cb_c=np.ascontiguousarray(inputs["L0_attn_conv_b"][CONVD:CONVD + DS, None]),
        d_rep=np.ascontiguousarray(np.broadcast_to(inputs["L0_attn_D"][None, :], (128, H0))),
        nw_rep=np.ascontiguousarray(np.broadcast_to(inputs["L0_attn_norm_w"][None, :], (128, DI))),
        wto=_tr(inputs["L0_attn_out_proj_w"]),
        wtw=_tr(inputs["L1_attn_W_write"]), wtg=_tr(inputs["L1_attn_W_gate"]),
        wtb=_tr(inputs["L1_attn_W_beta"]), wta=_tr(inputs["L1_attn_W_alpha"]),
        alog1=np.ascontiguousarray(inputs["L1_attn_A_log"][:, None]),
        dtb1=np.ascontiguousarray(inputs["L1_attn_dt_bias"][:, None]),
        wfcT=_tr(inputs["L1_ffn_W_fc"]),
        bfc_fm=np.ascontiguousarray(inputs["L1_ffn_b_fc"].reshape(32, 128).T),
        afcT=_tr(inputs["L1_ffn_A_fc"]), bfcT=_tr(inputs["L1_ffn_B_fc"]),
        wprT=_tr(inputs["L1_ffn_W_proj"]),
        bpr_rep=np.ascontiguousarray(np.broadcast_to(inputs["L1_ffn_b_proj"][None, :], (128, D))),
        aprT=_tr(inputs["L1_ffn_A_proj"]), bprT=_tr(inputs["L1_ffn_B_proj"]),
        wtout=_tr(inputs["L1_attn_W_out"]))
    u = inputs["L0_attn_u"]
    maps = []
    for c in range(NCORES):
        b, q = divmod(c, 4)
        t0 = q * T
        useg = np.ascontiguousarray(u[b, t0:t0 + T])
        uh = np.ascontiguousarray(u[b, t0 - 128:t0]) if q > 0 else np.zeros((128, D), np.float32)
        Mt8 = np.zeros((8, 8), np.float32)
        pm8 = np.zeros(8, np.float32)
        for j in range(q):
            pm8[b * 4 + j] = 1.0
            for m in range(j + 1, q):
                Mt8[b * 4 + m, b * 4 + j] = 1.0
        selb = np.zeros((8, 1), _md.bfloat16)
        if q > 0:
            selb[c - 1, 0] = 1.0
        maps.append(dict(u=useg, uh=uh, Mt=Mt8,
                         pm_rep=np.ascontiguousarray(np.broadcast_to(pm8[None, :], (H0, 8))),
                         pm_rep1=np.ascontiguousarray(np.broadcast_to(pm8[None, :], (H1, 8))),
                         selb=selb, **shared))
    return maps


def kernel(**inputs):
    cores = list(range(NCORES))
    res = run_bass_kernel_spmd(_get("fused", build_fused), prep_fused_inputs(inputs),
                               core_ids=cores).results
    out = np.zeros((2, 4 * T, D), np.float32)
    for c in range(NCORES):
        b, q = divmod(c, 4)
        out[b, q * T:(q + 1) * T] = np.asarray(res[c]["x2"])
    return out
